# version 04
# baseline (speedup 1.0000x reference)
import sys
if "/opt/trn_rl_repo" not in sys.path:
    sys.path.insert(0, "/opt/trn_rl_repo")
import math
import numpy as np
import ml_dtypes
import concourse.bass as bass
import concourse.mybir as mybir
from concourse.bass_utils import run_bass_kernel_spmd

F32 = mybir.dt.float32
BF16 = mybir.dt.bfloat16
I32 = mybir.dt.int32
AF = mybir.ActivationFunctionType
ALU = mybir.AluOpType
AX = mybir.AxisListType

D = 2048
T = 2048
NCHUNK = 16
SLAB = 512
NSLAB = T // SLAB
NEG = -1e30
INTER = 5504
ICHUNK = INTER // 128


def _bf(x):
    return np.asarray(x, dtype=np.float32).astype(ml_dtypes.bfloat16)

def build_consts():
    c = {}
    dh = 64
    pos = np.arange(T, dtype=np.float64)
    invf = 1.0 / (10000.0 ** (np.arange(0, dh, 2, dtype=np.float64) / dh))
    p = np.arange(128)
    d_in_head = p % 64
    freq_idx = np.where(d_in_head < 32, d_in_head, d_in_head - 32)
    ang = pos[None, :] * invf[freq_idx][:, None]
    c["cCOS"] = _bf(np.cos(ang))
    c["cSIN"] = _bf(np.sin(ang))
    R = np.zeros((64, 64), np.float64)
    for d_ in range(32):
        R[d_, d_ + 32] = -1.0
    for d_ in range(32, 64):
        R[d_, d_ - 32] = 1.0
    R128 = np.zeros((128, 128), np.float64)
    R128[:64, :64] = R
    R128[64:, 64:] = R
    c["cR128T"] = _bf(R128.T)
    c["cID128"] = _bf(np.eye(128))
    masks = np.zeros((128, 4, SLAB), np.float64)
    ktl = np.arange(128)
    qtl = np.arange(SLAB)
    for j in range(4):
        bad = (ktl[:, None] + j * 128) > qtl[None, :]
        masks[:, j, :] = np.where(bad, NEG, 0.0)
    c["cMASKT"] = _bf(masks)
    c["cONES128"] = _bf(np.ones((128, 1)))
    c["cMEANVEC"] = _bf(np.full((128, 1), 1.0 / D))
    c["cONESROW"] = np.ones((1, 128), np.float32)
    std_mask = np.zeros((1, 128), np.float32); std_mask[0, :120] = 1.0
    tail_mask = np.zeros((1, 128), np.float32); tail_mask[0, 120:] = 1.0
    c["cSTDMASK"] = std_mask
    c["cTAILMASK"] = tail_mask
    c["cONESROWB"] = _bf(np.ones((1, 128)))
    return c

CONSTS = build_consts()

def layout_weights(inp):
    w = {}
    Wq = np.asarray(inp["L0_attn_Wq"])
    cols = []
    for cchunk in range(16):
        for h in (cchunk, cchunk + 16):
            cols.append(Wq[:, h * 64:(h + 1) * 64])
    w["Wq"] = np.ascontiguousarray(np.concatenate(cols, axis=1))
    Wkv = np.asarray(inp["L0_attn_Wkv"])
    kc = Wkv[:, :512]; vc = Wkv[:, 512:]
    cols = []
    for j in range(4):
        cols.append(kc[:, j * 64:(j + 1) * 64])
        cols.append(kc[:, (j + 4) * 64:(j + 5) * 64])
    w["Wk"] = np.ascontiguousarray(np.concatenate(cols, axis=1))
    w["Wv"] = np.ascontiguousarray(vc)
    Wo = np.asarray(inp["L0_attn_Wo"])
    rows = []
    for cchunk in range(16):
        for h in (cchunk, cchunk + 16):
            rows.append(Wo[h * 64:(h + 1) * 64, :])
    w["Wo"] = np.ascontiguousarray(np.concatenate(rows, axis=0))
    Wa = np.asarray(inp["L0_ffn_Wa"])
    H, Ds, r = 16, 120, 8
    q_std_end = H * Ds; k_std_end = 2 * H * Ds; v_end = k_std_end + D
    cols = []
    for h in range(H):
        cols.append(Wa[:, h * Ds:(h + 1) * Ds])
        cols.append(Wa[:, v_end + h * r: v_end + (h + 1) * r])
        cols.append(Wa[:, q_std_end + h * Ds: q_std_end + (h + 1) * Ds])
        cols.append(Wa[:, v_end + H * r + h * r: v_end + H * r + (h + 1) * r])
    w["WaQK"] = np.ascontiguousarray(np.concatenate(cols, axis=1))
    w["WaV"] = np.ascontiguousarray(Wa[:, k_std_end:v_end])
    w["Wp"] = np.ascontiguousarray(np.asarray(inp["L0_ffn_Wp"]))
    w["CwT"] = np.ascontiguousarray(np.asarray(inp["L1_attn_C_w"]).T)
    w["DwT"] = np.ascontiguousarray(np.asarray(inp["L1_attn_D_w"]).T)
    w["Cb"] = np.ascontiguousarray(np.asarray(inp["L1_attn_C_b"]).reshape(1, D))
    w["Db"] = np.ascontiguousarray(np.asarray(inp["L1_attn_D_b"]).reshape(1, D))
    w["LD"] = np.ascontiguousarray(
        np.asarray(inp["L1_attn_log_delta"]).reshape(16, 128).T)
    w["Afm"] = np.ascontiguousarray(
        np.asarray(inp["L1_attn_A"]).reshape(16, 128, 64).transpose(1, 0, 2))
    w["Bfm"] = np.ascontiguousarray(
        np.asarray(inp["L1_attn_B"]).reshape(16, 128, 64).transpose(1, 0, 2))
    w["wstd"] = np.ascontiguousarray(
        np.asarray(inp["L0_ffn_w_std"], dtype=np.float32).reshape(1, 16))
    w["wrec"] = np.ascontiguousarray(
        np.asarray(inp["L0_ffn_w_rec"], dtype=np.float32).reshape(1, 16))
    w["gidxT"] = np.ascontiguousarray(np.asarray(inp["L1_ffn_gate_idx"]).T)
    w["uidxT"] = np.ascontiguousarray(np.asarray(inp["L1_ffn_up_idx"]).T)
    w["didxT"] = np.ascontiguousarray(np.asarray(inp["L1_ffn_down_idx"]).T)
    w["lut"] = np.ascontiguousarray(
        np.asarray(inp["L1_ffn_lut"], dtype=np.float32).reshape(1, 16))
    return w


ENGS = ("sync", "tensor", "vector", "scalar", "gpsimd")

class Sem:
    def __init__(self, nc, name):
        self.h = nc.alloc_semaphore(name)
        self.v = 0
    def inc(self, n=1):
        self.v += n
        return (self.h, n)
    def at(self):
        return (self.h, self.v)

_BLK_N = [0]

class Prog2:
    def __init__(self):
        self.ops = {e: [] for e in ENGS}
    def add(self, eng, fn, waits=(), incs=()):
        self.ops[eng].append((fn, tuple(waits), tuple(incs)))
    def emit(self, nc, name=None):
        if name is not None:
            _BLK_N[0] += 1
            name = f"{name}_{_BLK_N[0]}"
        with nc.Block(name) as block:
            for e in ENGS:
                oplist = self.ops[e]
                if not oplist:
                    continue
                def body(eng, oplist=oplist):
                    for fn, waits, incs in oplist:
                        for (sem, val) in waits:
                            if val > 0:
                                eng.wait_ge(sem, val)
                        ins = fn(eng) if fn is not None else None
                        for (sem, delta) in incs:
                            ins = ins.then_inc(sem, delta)
                getattr(block, e)(body)

class SbufPlan:
    def __init__(self, nc, base=18432, top=229248):
        self.nc = nc
        self.cur = base
        self.top = top
        self.n = 0
    def take(self, name, shape, dtype):
        szb = {F32: 4, BF16: 2, I32: 4}[dtype]
        per_part = int(np.prod(shape[1:])) * szb
        off = (self.cur + 31) & ~31
        assert off + per_part <= self.top, (
            f"SBUF overflow at {name}: {off}+{per_part} > {self.top}")
        self.cur = off + per_part
        self.n += 1
        return self.nc.alloc_sbuf_tensor_at(f"{name}_{self.n}", list(shape), dtype,
                                            offset=off)
    def mark(self):
        return self.cur
    def reset(self, mark):
        self.cur = mark

def dma(ap_out, ap_in, **kw):
    def fn(eng):
        return eng.dma_start(ap_out, ap_in, **kw)
    return fn

def mm(out, lhsT, rhs, start, stop, **kw):
    def fn(eng):
        return eng.matmul(out, lhsT, rhs, start=start, stop=stop,
                          skip_group_check=True, **kw)
    return fn

class KB:
    def __init__(self, nc):
        self.nc = nc
        self.plan = SbufPlan(nc)
        self.sems = {}
        self.pb = [nc.alloc_psum_tensor(f"pb{i}", [128, 512], F32) for i in range(8)]
        self.d = {}
        self.c = {}

    def sem(self, name):
        if name not in self.sems:
            self.sems[name] = Sem(self.nc, name)
        return self.sems[name]

    def dram_in(self, name, shape, dtype=F32):
        if name not in self.d:
            self.d[name] = self.nc.dram_tensor(name, list(shape), dtype,
                                               kind="ExternalInput").ap()
        return self.d[name]

    def dram_out(self, name, shape, dtype=F32):
        self.d[name] = self.nc.dram_tensor(name, list(shape), dtype,
                                           kind="ExternalOutput").ap()
        return self.d[name]

    def dram_tmp(self, name, shape, dtype=F32):
        if name not in self.d:
            self.d[name] = self.nc.dram_tensor(name, list(shape), dtype,
                                               kind="Internal").ap()
        return self.d[name]

    def stage_prep(self, want=("l0a", "l0f", "l1a", "l1f")):
        nc, pl, c = self.nc, self.plan, self.c
        p = Prog2()
        ld = self.sem("ld")
        v = self.sem("vv")
        s_ = self.sem("ss")
        m_ = self.sem("mm")
        for nm, dt in (("cCOS", BF16), ("cSIN", BF16), ("cR128T", BF16),
                       ("cID128", BF16), ("cMASKT", BF16), ("cONES128", BF16),
                       ("cMEANVEC", BF16), ("cONESROW", F32),
                       ("cONESROWB", BF16)):
            shape = list(CONSTS[nm].shape)
            c[nm] = pl.take(nm, shape, dt)
            dr = self.dram_in(nm, shape, dt)
            p.add("sync", dma(c[nm][:], dr), incs=[ld.inc(16)])
        c["r_row"] = pl.take("r_row", [1, T], BF16)
        c["negmu_row"] = pl.take("negmu_row", [1, T], BF16)
        c["rb"] = pl.take("rb", [128, T], F32)
        for i, val in enumerate((1e-5, 2.0)):
            t = pl.take(f"cstf{i}", [128, 1], F32)
            nc.const_aps.register(F32, val, t[:])
            p.add("gpsimd", lambda e, t=t, val=val: e.memset(t[:], val))
        if "l0f" in want:
            c["CSQ"] = pl.take("csq", [128, 16], F32)
            c["CSK"] = pl.take("csk", [128, 16], F32)
            c["BM"] = pl.take("bm", [128, 16], F32)
        if "l1a" in want:
            c["ADIAG"] = pl.take("adiag", [128, 16], F32)
            c["AB"] = pl.take("ab", [128, 16], F32)
        if "l1f" in want:
            c["LUTB"] = pl.take("lutb", [128, 16], F32)
        self.c["prepM"] = pl.mark()
        if "l0f" in want:
            for nm in ("cSTDMASK", "cTAILMASK"):
                c[nm] = pl.take(nm, [1, 128], F32)
                dr = self.dram_in(nm, [1, 128], F32)
                p.add("sync", dma(c[nm][:], dr), incs=[ld.inc(16)])
            wstd = self.dram_in("wstd", [1, 16], F32)
            wrec = self.dram_in("wrec", [1, 16], F32)
            srow = pl.take("srow", [1, 48], F32)
            p.add("sync", dma(srow[0:1, 0:16], wstd), incs=[ld.inc(16)])
            p.add("sync", dma(srow[0:1, 16:32], wrec), incs=[ld.inc(16)])
            p.add("sync", dma(srow[0:1, 32:48], wrec), incs=[ld.inc(16)])
            p.add("vector", lambda e: e.tensor_scalar_max(
                srow[0:1, 0:32], srow[0:1, 0:32], 1e-8),
                waits=[ld.at()], incs=[v.inc()])
            p.add("scalar", lambda e: e.activation(
                srow[0:1, 0:32], srow[0:1, 0:32], AF.Sqrt),
                waits=[v.at()], incs=[s_.inc()])
            p.add("vector", lambda e: e.tensor_single_scalar(
                srow[0:1, 32:48], srow[0:1, 32:48], 0.1, ALU.is_gt),
                waits=[ld.at()], incs=[v.inc()])
            p.add("tensor", mm(self.pb[0][:, 0:16], c["cSTDMASK"][:],
                               srow[0:1, 0:16], True, False),
                  waits=[s_.at(), v.at()])
            p.add("tensor", mm(self.pb[0][:, 0:16], c["cTAILMASK"][:],
                               srow[0:1, 16:32], False, True), incs=[m_.inc()])
            p.add("tensor", mm(self.pb[1][:, 0:16], c["cTAILMASK"][:],
                               srow[0:1, 32:48], True, True), incs=[m_.inc()])
            p.add("scalar", lambda e: e.copy(c["CSK"][:], self.pb[0][:, 0:16]),
                  waits=[m_.at()], incs=[s_.inc()])
            p.add("scalar", lambda e: e.copy(c["BM"][:], self.pb[1][:, 0:16]),
                  incs=[s_.inc()])
            p.add("vector", lambda e: e.tensor_scalar_mul(
                c["CSQ"][:], self.pb[0][:, 0:16], 1.0 / math.sqrt(128.0)),
                waits=[m_.at()], incs=[v.inc()])
        if "l1a" in want:
            ldr = self.dram_in("LD", [128, 16], F32)
            afm = self.dram_in("Afm", [128, 16, 64], F32)
            bfm = self.dram_in("Bfm", [128, 16, 64], F32)
            tmpA = pl.take("tmpA", [128, 16, 64], F32)
            tmpB = pl.take("tmpB", [128, 16, 64], F32)
            sp1 = pl.take("sp1", [128, 16], F32)
            sp2 = pl.take("sp2", [128, 16], F32)
            p.add("sync", dma(c["ADIAG"][:], ldr), incs=[ld.inc(16)])
            p.add("sync", dma(tmpA[:], afm), incs=[ld.inc(16)])
            p.add("sync", dma(tmpB[:], bfm), incs=[ld.inc(16)])
            p.add("scalar", lambda e: e.activation(sp1[:], c["ADIAG"][:], AF.Abs),
                  waits=[ld.at()])
            p.add("scalar", lambda e: e.activation(sp1[:], sp1[:], AF.Exp,
                                                   scale=-1.0))
            p.add("scalar", lambda e: e.activation(sp1[:], sp1[:], AF.Ln, bias=1.0))
            p.add("scalar", lambda e: e.activation(sp2[:], c["ADIAG"][:], AF.Relu),
                  incs=[s_.inc()])
            p.add("vector", lambda e: e.tensor_add(sp1[:], sp1[:], sp2[:]),
                  waits=[s_.at()], incs=[v.inc()])
            p.add("scalar", lambda e: e.activation(c["ADIAG"][:], sp1[:], AF.Exp,
                                                   scale=-1.0),
                  waits=[v.at()], incs=[s_.inc()])
            p.add("vector", lambda e: e.tensor_mul(tmpA[:], tmpA[:], tmpB[:]),
                  waits=[ld.at()])
            p.add("vector", lambda e: e.tensor_reduce(
                c["AB"][:], tmpA[:], AX.X, ALU.add), incs=[v.inc()])
        if "l1f" in want:
            lut = self.dram_in("lut", [1, 16], F32)
            lrow = pl.take("lrow", [1, 16], F32)
            p.add("sync", dma(lrow[:], lut), incs=[ld.inc(16)])
            p.add("tensor", mm(self.pb[2][:, 0:16], c["cONESROW"][:], lrow[:],
                               True, True), waits=[ld.at()], incs=[m_.inc()])
            p.add("scalar", lambda e: e.copy(c["LUTB"][:], self.pb[2][:, 0:16]),
                  waits=[m_.at()], incs=[s_.inc()])
        p.add("sync", None, waits=[ld.at()])
        p.emit(nc, "prep")
        if "prepM" in self.c:
            pl.reset(self.c.pop("prepM"))

    def stats_pass(self, p, xdr, xbf_full=None, pfx="st"):
        nc, pl, c = self.nc, self.plan, self.c
        mark = pl.mark()
        ld2 = [self.sem(pfx + "_ld0"), self.sem(pfx + "_ld1")]
        cast = self.sem(pfx + "_cast")
        sqd = self.sem(pfx + "_sq")
        pemm = self.sem(pfx + "_pe")
        fin = self.sem(pfx + "_fin")
        xf = pl.take(pfx + "xf", [128, 2, SLAB], F32)
        xb2 = None if xbf_full is not None else pl.take(pfx + "xb", [128, 2, SLAB], BF16)
        sq = pl.take(pfx + "sq", [128, 2, SLAB], BF16)
        rowbuf = pl.take(pfx + "row", [1, 2 * SLAB], F32)
        pbS, pbB = self.pb[6], self.pb[7]
        xdrv = xdr.rearrange("(c p) t -> p c t", p=128)
        cast0, sq0, pe0 = cast.v, sqd.v, pemm.v
        for s in range(NSLAB):
            for ch in range(NCHUNK):
                i = s * NCHUNK + ch
                b = i % 2
                p.add("sync", dma(xf[:, b], xdrv[:, ch, s * SLAB:(s + 1) * SLAB]),
                      waits=[(cast.h, cast0 + i - 1)], incs=[ld2[b].inc(16)])
                dst = (xbf_full[:, ch, s * SLAB:(s + 1) * SLAB]
                       if xbf_full is not None else xb2[:, b])
                p.add("vector",
                      lambda e, dst=dst, b=b: e.tensor_copy(dst, xf[:, b]),
                      waits=[ld2[b].at(), (pemm.h, pe0 + i - 1)], incs=[cast.inc()])
                p.add("scalar",
                      lambda e, dst=dst, b=b: e.activation(sq[:, b], dst, AF.Square),
                      waits=[cast.at(), (pemm.h, pe0 + i - 1)], incs=[sqd.inc()])
                p.add("tensor", mm(pbS[0:1, :], c["cMEANVEC"][:], dst,
                                   ch == 0, ch == 15), waits=[cast.at()])
                p.add("tensor", mm(pbS[32:33, :], c["cMEANVEC"][:], sq[:, b],
                                   ch == 0, ch == 15),
                      waits=[sqd.at()], incs=[pemm.inc()])
            sl = slice(s * SLAB, (s + 1) * SLAB)
            p.add("scalar", lambda e: e.copy(rowbuf[0:1, 0:SLAB], pbS[0:1, :]),
                  waits=[pemm.at()])
            p.add("scalar", lambda e: e.activation(rowbuf[0:1, SLAB:],
                                                   rowbuf[0:1, 0:SLAB], AF.Square),
                  incs=[fin.inc()])
            p.add("vector", lambda e: e.tensor_sub(rowbuf[0:1, SLAB:],
                                                   pbS[32:33, :], rowbuf[0:1, SLAB:]),
                  waits=[fin.at()], incs=[fin.inc()])
            p.add("vector", lambda e, sl=sl: e.tensor_scalar_mul(
                c["negmu_row"][0:1, sl], rowbuf[0:1, 0:SLAB], -1.0),
                incs=[fin.inc()])
            p.add("scalar", lambda e: e.activation(rowbuf[0:1, SLAB:],
                                                   rowbuf[0:1, SLAB:], AF.Ln,
                                                   bias=1e-5), waits=[fin.at()])
            p.add("scalar", lambda e, sl=sl: e.activation(
                c["r_row"][0:1, sl], rowbuf[0:1, SLAB:], AF.Exp, scale=-0.5),
                incs=[fin.inc()])
            p.add("scalar", lambda e, sl=sl: e.activation(
                c["invr_row"][0:1, sl], rowbuf[0:1, SLAB:], AF.Exp, scale=0.5),
                incs=[fin.inc()])
            p.add("tensor", mm(pbB[:, :], c["cONESROWB"][:], c["r_row"][0:1, sl],
                               True, True), waits=[fin.at()], incs=[pemm.inc()])
            p.add("scalar", lambda e, sl=sl: e.copy(c["rb"][:, sl], pbB[:, :]),
                  waits=[pemm.at()], incs=[fin.inc()])
            p.add("tensor", None, waits=[fin.at()])
        pl.reset(mark)

    def alloc_globals(self):
        self.c["xbf"] = self.plan.take("xbf", [128, NCHUNK, T], BF16)
        self.c["invr_row"] = self.plan.take("invr_row", [1, T], BF16)

    def finalize_ln(self, p, pbS, r_dst, negmu_dst, rowbuf, pe, fin,
                    invr_dst=None, rb_dst=None, pbB=None):
        c = self.c
        p.add("scalar", lambda e: e.copy(rowbuf[0:1, 0:SLAB], pbS[0:1, :]),
              waits=[pe.at()])
        p.add("scalar", lambda e: e.activation(rowbuf[0:1, SLAB:],
                                               rowbuf[0:1, 0:SLAB], AF.Square),
              incs=[fin.inc()])
        p.add("vector", lambda e: e.tensor_sub(rowbuf[0:1, SLAB:], pbS[32:33, :],
                                               rowbuf[0:1, SLAB:]),
              waits=[fin.at()], incs=[fin.inc()])
        p.add("vector", lambda e: e.tensor_scalar_mul(negmu_dst, rowbuf[0:1, 0:SLAB],
                                                      -1.0), incs=[fin.inc()])
        p.add("scalar", lambda e: e.activation(rowbuf[0:1, SLAB:], rowbuf[0:1, SLAB:],
                                               AF.Ln, bias=1e-5), waits=[fin.at()])
        p.add("scalar", lambda e: e.activation(r_dst, rowbuf[0:1, SLAB:], AF.Exp,
                                               scale=-0.5), incs=[fin.inc()])
        if invr_dst is not None:
            p.add("scalar", lambda e: e.activation(invr_dst, rowbuf[0:1, SLAB:],
                                                   AF.Exp, scale=0.5),
                  incs=[fin.inc()])
        if rb_dst is not None:
            p.add("tensor", mm(pbB[:, :], c["cONESROWB"][:], r_dst, True, True),
                  waits=[fin.at()], incs=[pe.inc()])
            p.add("scalar", lambda e: e.copy(rb_dst, pbB[:, :]),
                  waits=[pe.at()], incs=[fin.inc()])

    def load_w_tiles(self, p, src, col0, ncols, wf, wbf, b, ld, wcast, pe_done,
                     pe_need, kchunks=NCHUNK):
        v = src.rearrange("(kc kp) n -> kp kc n", kp=128)
        p.add("sync", dma(wf[:, 0:kchunks], v[:, :, col0:col0 + ncols]),
              waits=[wcast.at()], incs=[ld.inc(16)])
        p.add("vector", lambda e: e.tensor_copy(wbf[:, b, 0:kchunks], wf[:, 0:kchunks]),
              waits=[ld.at(), (pe_done.h, pe_need)], incs=[wcast.inc()])

    def stage_l0a(self, xdr, ydr, last=False):
        nc, pl, c = self.nc, self.plan, self.c
        qsp = self.dram_tmp("sp_a", [D, T], BF16)
        ysp = self.dram_tmp("sp_y", [D, T], BF16)
        wq = self.dram_in("Wq", [D, D])
        wk = self.dram_in("Wk", [D, 512])
        wv = self.dram_in("Wv", [D, 512])
        wo = self.dram_in("Wo", [D, D])
        xbf = c["xbf"]
        ao = c["xbf"]
        frame = pl.mark()
        kp = pl.take("kp", [128, 4, T], BF16)
        vtm = pl.take("vtm", [128, 16, 520], BF16)
        M1 = pl.mark()

        p = Prog2()
        w1row = pl.take("w1row", [1, 24 * 128], BF16)
        ld = self.sem("a_ld"); wcast = self.sem("a_wc"); pe1 = self.sem("a_pe1")
        w1ok = self.sem("a_w1"); gmm = self.sem("a_gmm"); ev = self.sem("a_ev")
        rot = self.sem("a_rot"); rp = self.sem("a_rp"); sp = self.sem("a_sp")
        vt = self.sem("a_vt"); vc = self.sem("a_vc")
        wf = pl.take("wfA", [128, NCHUNK, 128], F32)
        wbf = pl.take("wbA", [128, 2, NCHUNK, 128], BF16)
        qn = pl.take("qnA", [128, 2, SLAB], BF16)
        qo = pl.take("qoA", [128, 2, SLAB], BF16)
        rt = pl.take("rtA", [128, SLAB], BF16)
        vtm_h = vtm.rearrange("p k (h c) -> p k h c", c=65)
        p.add("gpsimd", lambda e: e.memset(vtm_h[:, :, :, 64:65], 1.0),
              incs=[vc.inc()])
        specs = [("q", wq, i, i) for i in range(16)] + \
                [("k", wk, j, 16 + j) for j in range(4)] + \
                [("v", wv, j, 20 + j) for j in range(4)]
        for kind, src, cc, idx in specs:
            b = idx % 2
            self.load_w_tiles(p, src, cc * 128, 128, wf, wbf, b, ld, wcast,
                              gmm, gmm.v - 4)
            for k in range(NCHUNK):
                p.add("tensor", mm(self.pb[5][0:1, 0:128], c["cONES128"][:],
                                   wbf[:, b, k, :], k == 0, k == 15),
                      waits=[wcast.at()] if k == 0 else [],
                      incs=[pe1.inc()] if k == 15 else [])
            p.add("scalar", lambda e, idx=idx: e.copy(
                w1row[0:1, idx * 128:(idx + 1) * 128], self.pb[5][0:1, 0:128]),
                waits=[pe1.at()], incs=[w1ok.inc()])
            for s in range(NSLAB):
                sl = slice(s * SLAB, (s + 1) * SLAB)
                bq = (idx * 4 + s) % 2
                pbp = self.pb[bq]
                for k in range(NCHUNK):
                    p.add("tensor", mm(pbp[:, :], wbf[:, b, k, :],
                                       xbf[:, k, sl], k == 0, False),
                          waits=([(ev.h, ev.v - 1)] if k == 0 else []))
                p.add("tensor", mm(pbp[:, :], w1row[0:1, idx * 128:(idx + 1) * 128],
                                   c["negmu_row"][0:1, sl], False, True),
                      waits=[w1ok.at()], incs=[gmm.inc()])
                if kind in ("q", "k"):
                    p.add("vector", lambda e, pbp=pbp, bq=bq, sl=sl: e.tensor_mul(
                        qn[:, bq], pbp[:, :], c["rb"][:, sl]),
                        waits=[gmm.at()], incs=[ev.inc()])
                    pbr = self.pb[2 + bq]
                    p.add("tensor", mm(pbr[:, :], c["cR128T"][:], qn[:, bq],
                                       True, True),
                          waits=[ev.at(), (rp.h, rp.v - 1)], incs=[rot.inc()])
                    p.add("vector", lambda e, bq=bq, sl=sl: e.tensor_mul(
                        qo[:, bq], qn[:, bq], c["cCOS"][:, sl]),
                        waits=[(sp.h, 16 * (sp.v // 16 - 1))])
                    p.add("vector", lambda e, pbr=pbr, sl=sl: e.tensor_mul(
                        rt[:, :], pbr[:, :], c["cSIN"][:, sl]),
                        waits=[rot.at()])
                    if kind == "q":
                        p.add("vector", lambda e, bq=bq: e.tensor_add(
                            qo[:, bq], qo[:, bq], rt[:, :]), incs=[rp.inc()])
                        p.add("gpsimd", lambda e, cc=cc, sl=sl, bq=bq: e.dma_start(
                            qsp[cc * 128:(cc + 1) * 128, sl], qo[:, bq]),
                            waits=[rp.at()], incs=[sp.inc(16)])
                    else:
                        p.add("vector", lambda e, cc=cc, sl=sl, bq=bq: e.tensor_add(
                            kp[:, cc, sl], qo[:, bq], rt[:, :]), incs=[rp.inc()])
                else:
                    p.add("vector", lambda e, pbp=pbp, bq=bq, sl=sl: e.tensor_mul(
                        qn[:, bq], pbp[:, :], c["rb"][:, sl]),
                        waits=[gmm.at()], incs=[ev.inc()])
                    pbt = self.pb[2 + bq]
                    for j in range(4):
                        p.add("tensor", mm(pbt[:, j * 128:(j + 1) * 128],
                                           qn[:, bq, j * 128:(j + 1) * 128],
                                           c["cID128"][:], True, True),
                              waits=([ev.at(), (vc.h, vc.v - 8), rp.at()]
                                     if j == 0 else []),
                              incs=[vt.inc()] if j == 3 else [])
                    for j in range(4):
                        kt = 4 * s + j
                        p.add("scalar", lambda e, pbt=pbt, j=j, kt=kt, cc=cc: e.copy(
                            vtm[:, kt, 130 * cc:130 * cc + 64],
                            pbt[:, j * 128:j * 128 + 64]),
                            waits=[vt.at()] if j == 0 else [])
                        p.add("scalar", lambda e, pbt=pbt, j=j, kt=kt, cc=cc: e.copy(
                            vtm[:, kt, 130 * cc + 65:130 * cc + 129],
                            pbt[:, j * 128 + 64:j * 128 + 128]),
                            incs=[vc.inc()])
        p.add("gpsimd", None, waits=[sp.at()])
        if getattr(self, "dbg_l0a", False):
            dq = self.sem("dbg_q")
            oq = self.dram_out("o_qsp", [D, T], BF16)
            p.add("gpsimd", dma(oq, qsp[:, :]), waits=[rp.at(), vc.at()],
                  incs=[dq.inc(16)])
            okp = self.dram_out("o_kp", [128, 4, T], BF16)
            p.add("gpsimd", dma(okp, kp[:]), incs=[dq.inc(16)])
            ovt = self.dram_out("o_vtm", [128, 16, 520], BF16)
            p.add("gpsimd", dma(ovt, vtm[:]), incs=[dq.inc(16)])
            orr = self.dram_out("o_rr", [1, T], BF16)
            p.add("gpsimd", dma(orr, c["r_row"][:]), incs=[dq.inc(16)])
            onm = self.dram_out("o_nm", [1, T], BF16)
            p.add("gpsimd", dma(onm, c["negmu_row"][:]), incs=[dq.inc(16)])
            orb = self.dram_out("o_rb", [128, T], F32)
            p.add("gpsimd", dma(orb, c["rb"][:]), incs=[dq.inc(16)])
            oxb = self.dram_out("o_xb", [128, 16, T], BF16)
            p.add("gpsimd", dma(oxb, c["xbf"][:]), incs=[dq.inc(16)])
            p.add("gpsimd", None, waits=[dq.at()])
        p.emit(nc, "l0a_A")
        pl.reset(M1)

        p = Prog2()
        qld = self.sem("b_qld")
        stm = [self.sem(f"b_st{i}") for i in range(4)]
        exs = [self.sem(f"b_ex{i}") for i in range(4)]
        pvs = [self.sem(f"b_pv{i}") for i in range(2)]
        aoe = [self.sem(f"b_ao{i}") for i in range(2)]
        dn = self.sem("b_dn"); rc = self.sem("b_rc")
        bc = [self.sem(f"b_bc{i}") for i in range(2)]
        qsl = pl.take("qslB", [128, NCHUNK, SLAB], BF16)
        pt = pl.take("ptB", [128, 2, 2, SLAB], BF16)
        den = pl.take("denB", [1, 2, SLAB], F32)
        tnn = pl.take("tnB", [1, 2, SLAB], F32)
        rcb = pl.take("rcbB", [128, 2, SLAB], F32)
        qv = qsp.rearrange("(cc p) t -> p cc t", p=128)
        for s in range(NSLAB):
            sl = slice(s * SLAB, (s + 1) * SLAB)
            nkc = 4 * (s + 1)
            p.add("sync", dma(qsl[:], qv[:, :, sl]),
                  waits=[x.at() for x in stm], incs=[qld.inc(16)])
            for cc in range(16):
                j = cc // 4
                for h2, (rl, rh) in enumerate(((0, 64), (64, 128))):
                    kh = j + 4 * h2
                    pba = self.pb[4 + h2]
                    for kc in range(nkc):
                        bank = 2 * h2 + kc % 2
                        pbst = self.pb[bank]
                        diag = kc >= 4 * s
                        w = [qld.at(), exs[bank].at()]
                        p.add("tensor", mm(pbst[:, :],
                                           kp[rl:rh, j, kc * 128:(kc + 1) * 128],
                                           qsl[rl:rh, cc, :], True, not diag),
                              waits=w,
                              incs=[] if diag else [stm[bank].inc()])
                        if diag:
                            p.add("tensor", mm(pbst[:, :], c["cID128"][:],
                                               c["cMASKT"][:, kc - 4 * s, :],
                                               False, True),
                                  incs=[stm[bank].inc()])
                        p.add("scalar",
                              lambda e, pbst=pbst, h2=h2, kc=kc: e.activation(
                                  pt[:, h2, kc % 2, :], pbst[:, :], AF.Exp,
                                  scale=0.125),
                              waits=[stm[bank].at(), (pvs[h2].h, pvs[h2].v - 1)],
                              incs=[exs[bank].inc()])
                        pvw = [exs[bank].at()]
                        if kc == 0:
                            pvw.append(aoe[h2].at())
                        p.add("tensor", mm(pba[0:65, :],
                                           vtm[:, kc, 65 * kh:65 * kh + 65],
                                           pt[:, h2, kc % 2, :], kc == 0,
                                           kc == nkc - 1),
                              waits=pvw, incs=[pvs[h2].inc()])
                    bmm = self.sem(f"b_bm{h2}")
                    p.add("scalar", lambda e, pba=pba, h2=h2: e.activation(
                        den[0:1, h2, :], pba[64:65, :], AF.Ln),
                        waits=[pvs[h2].at(), bmm.at()])
                    p.add("scalar", lambda e, h2=h2: e.activation(
                        den[0:1, h2, :], den[0:1, h2, :], AF.Exp, scale=-1.0),
                        incs=[dn.inc()])
                    p.add("vector", lambda e, pba=pba, h2=h2: e.tensor_mul(
                        tnn[0:1, h2, :], pba[64:65, :], den[0:1, h2, :]),
                        waits=[dn.at()])
                    p.add("vector", lambda e, h2=h2: e.tensor_scalar(
                        tnn[0:1, h2, :], tnn[0:1, h2, :], -1.0, 2.0, ALU.mult,
                        ALU.add))
                    p.add("vector", lambda e, h2=h2: e.tensor_mul(
                        den[0:1, h2, :], den[0:1, h2, :], tnn[0:1, h2, :]),
                        incs=[rc.inc()])
                    pbb = self.pb[6 + h2]
                    p.add("tensor", mm(pbb[:, :], c["cONESROW"][:], den[0:1, h2, :],
                                       True, True), waits=[rc.at(), bc[h2].at()],
                          incs=[bmm.inc()])
                    p.add("scalar", lambda e, pbb=pbb, h2=h2: e.copy(
                        rcb[:, h2, :], pbb[:, :]),
                        waits=[bmm.at(), aoe[h2].at()], incs=[bc[h2].inc()])
                    p.add("vector",
                          lambda e, pba=pba, h2=h2, cc=cc, sl=sl: e.tensor_mul(
                              ao[64 * h2:64 * (h2 + 1), cc, sl], pba[0:64, :],
                              rcb[0:64, h2, :]),
                          waits=[bc[h2].at()], incs=[aoe[h2].inc()])
        if getattr(self, "dbg_l0a", False):
            dq = self.sem("dbg_q")
            oao = self.dram_out("o_ao", [128, 16, T], BF16)
            p.add("gpsimd", dma(oao, ao[:]), waits=[aoe[0].at(), aoe[1].at()],
                  incs=[dq.inc(16)])
            p.add("gpsimd", None, waits=[dq.at()])
        p.emit(nc, "l0a_B")
        pl.reset(M1)

        self.proj_spill_phase(wo, ao, ysp, name="l0a_C")
        pl.reset(M1)
        self.residual_phase(ysp, xdr, ydr, last=last)
        pl.reset(frame)

    def stage_l0f(self, xdr, ydr, last=False):
        nc, pl, c = self.nc, self.plan, self.c
        qsp = self.dram_tmp("sp_a", [D, T], BF16)
        ksp = self.dram_tmp("sp_b", [D, T], BF16)
        vsp = self.dram_tmp("sp_c", [D, T], BF16)
        ysp = self.dram_tmp("sp_y", [D, T], BF16)
        waqk = self.dram_in("WaQK", [D, 4096])
        wav = self.dram_in("WaV", [D, D])
        wp = self.dram_in("Wp", [D, D])
        xbf = c["xbf"]
        ao = c["xbf"]
        frame = pl.mark()
        M1 = pl.mark()

        p = Prog2()
        w1row = pl.take("w1rF", [1, 48 * 128], BF16)
        ld = self.sem("a_ld"); wcast = self.sem("a_wc"); pe1 = self.sem("a_pe1")
        w1ok = self.sem("a_w1"); gmm = self.sem("a_gmm"); ev = self.sem("a_ev")
        bl = self.sem("f_bl"); sp = self.sem("a_sp")
        vt = self.sem("a_vt"); vc = self.sem("a_vc"); vs = self.sem("f_vs")
        wf = pl.take("wfF", [128, NCHUNK, 128], F32)
        wbf = pl.take("wbF", [128, 2, NCHUNK, 128], BF16)
        qtmp = pl.take("qtF", [128, NSLAB, SLAB], BF16)
        ktmp = pl.take("ktF", [128, 2, SLAB], BF16)
        dtmp = pl.take("dtF", [128, SLAB], BF16)
        qf = pl.take("qfF", [128, 2, SLAB], BF16)
        kf = pl.take("kfF", [128, 2, SLAB], BF16)
        vn = pl.take("vnF", [128, 2, SLAB], BF16)
        vsta = pl.take("vstF", [128, 2, 4, 128], BF16)
        vspv = vsp.rearrange("(kc p) d -> p kc d", p=128)
        for h in range(16):
            for qk in range(2):
                idx = 2 * h + qk
                b = idx % 2
                self.load_w_tiles(p, waqk, idx * 128, 128, wf, wbf, b, ld, wcast,
                                  gmm, gmm.v - 4)
                for k in range(NCHUNK):
                    p.add("tensor", mm(self.pb[5][0:1, 0:128], c["cONES128"][:],
                                       wbf[:, b, k, :], k == 0, k == 15),
                          waits=[wcast.at()] if k == 0 else [],
                          incs=[pe1.inc()] if k == 15 else [])
                p.add("scalar", lambda e, idx=idx: e.copy(
                    w1row[0:1, idx * 128:(idx + 1) * 128], self.pb[5][0:1, 0:128]),
                    waits=[pe1.at()], incs=[w1ok.inc()])
                for s in range(NSLAB):
                    sl = slice(s * SLAB, (s + 1) * SLAB)
                    bq = (idx * 4 + s) % 2
                    pbp = self.pb[bq]
                    for k in range(NCHUNK):
                        p.add("tensor", mm(pbp[:, :], wbf[:, b, k, :],
                                           xbf[:, k, sl], k == 0, False),
                              waits=([(ev.h, ev.v - 1)] if k == 0 else []))
                    p.add("tensor", mm(pbp[:, :],
                                       w1row[0:1, idx * 128:(idx + 1) * 128],
                                       c["negmu_row"][0:1, sl], False, True),
                          waits=[w1ok.at()], incs=[gmm.inc()])
                    if qk == 0:
                        p.add("vector",
                              lambda e, pbp=pbp, s=s, h=h, sl=sl: e.scalar_tensor_tensor(
                                  qtmp[:, s, :], pbp[:, :], c["CSK"][:, h:h + 1],
                                  c["rb"][:, sl], ALU.mult, ALU.mult),
                              waits=[gmm.at(), bl.at()], incs=[ev.inc()])
                    else:
                        bk = s % 2
                        p.add("vector",
                              lambda e, pbp=pbp, bk=bk, h=h, sl=sl: e.scalar_tensor_tensor(
                                  ktmp[:, bk, :], pbp[:, :], c["CSK"][:, h:h + 1],
                                  c["rb"][:, sl], ALU.mult, ALU.mult),
                              waits=[gmm.at(), (sp.h, 16 * (sp.v // 16 - 2))],
                              incs=[ev.inc()])
                        p.add("vector", lambda e, bk=bk, s=s: e.tensor_sub(
                            dtmp[:, :], ktmp[:, bk, :], qtmp[:, s, :]))
                        p.add("vector", lambda e, h=h: e.tensor_scalar(
                            dtmp[:, :], dtmp[:, :], c["BM"][:, h:h + 1], None,
                            ALU.mult))
                        p.add("vector", lambda e, bk=bk, s=s: e.tensor_add(
                            qf[:, bk, :], qtmp[:, s, :], dtmp[:, :]))
                        p.add("vector", lambda e, bk=bk: e.tensor_sub(
                            kf[:, bk, :], ktmp[:, bk, :], dtmp[:, :]),
                            incs=[bl.inc()])
                        p.add("gpsimd", lambda e, h=h, sl=sl, bk=bk: e.dma_start(
                            qsp[h * 128:(h + 1) * 128, sl], qf[:, bk, :]),
                            waits=[bl.at()], incs=[sp.inc(16)])
                        p.add("gpsimd", lambda e, h=h, sl=sl, bk=bk: e.dma_start(
                            ksp[h * 128:(h + 1) * 128, sl], kf[:, bk, :]),
                            incs=[sp.inc(16)])
        for cc in range(16):
            idx = 32 + cc
            b = idx % 2
            self.load_w_tiles(p, wav, cc * 128, 128, wf, wbf, b, ld, wcast,
                              gmm, gmm.v - 4)
            for k in range(NCHUNK):
                p.add("tensor", mm(self.pb[5][0:1, 0:128], c["cONES128"][:],
                                   wbf[:, b, k, :], k == 0, k == 15),
                      waits=[wcast.at()] if k == 0 else [],
                      incs=[pe1.inc()] if k == 15 else [])
            p.add("scalar", lambda e, idx=idx: e.copy(
                w1row[0:1, idx * 128:(idx + 1) * 128], self.pb[5][0:1, 0:128]),
                waits=[pe1.at()], incs=[w1ok.inc()])
            for s in range(NSLAB):
                sl = slice(s * SLAB, (s + 1) * SLAB)
                bq = (idx * 4 + s) % 2
                pbp = self.pb[bq]
                for k in range(NCHUNK):
                    p.add("tensor", mm(pbp[:, :], wbf[:, b, k, :],
                                       xbf[:, k, sl], k == 0, False),
                          waits=([(ev.h, ev.v - 1)] if k == 0 else []))
                p.add("tensor", mm(pbp[:, :], w1row[0:1, idx * 128:(idx + 1) * 128],
                                   c["negmu_row"][0:1, sl], False, True),
                      waits=[w1ok.at()], incs=[gmm.inc()])
                p.add("vector", lambda e, pbp=pbp, bq=bq, sl=sl: e.tensor_mul(
                    vn[:, bq], pbp[:, :], c["rb"][:, sl]),
                    waits=[gmm.at()], incs=[ev.inc()])
                pbt = self.pb[2 + bq]
                for j in range(4):
                    p.add("tensor", mm(pbt[:, j * 128:(j + 1) * 128],
                                       vn[:, bq, j * 128:(j + 1) * 128],
                                       c["cID128"][:], True, True),
                          waits=([ev.at(), (vc.h, vc.v - 4)] if j == 0 else []),
                          incs=[vt.inc()] if j == 3 else [])
                for j in range(4):
                    p.add("scalar", lambda e, pbt=pbt, j=j, bq=bq: e.copy(
                        vsta[:, bq, j, :], pbt[:, j * 128:(j + 1) * 128]),
                        waits=([vt.at(), (vs.h, 16 * (vs.v // 16 - 1))]
                               if j == 0 else []),
                        incs=[vc.inc()])
                p.add("gpsimd", lambda e, s=s, cc=cc, bq=bq: e.dma_start(
                    vspv[:, 4 * s:4 * s + 4, cc * 128:(cc + 1) * 128],
                    vsta[:, bq, :, :]), waits=[vc.at()], incs=[vs.inc(16)])
        p.add("gpsimd", None, waits=[sp.at(), vs.at()])
        p.emit(nc, "l0f_A")
        pl.reset(M1)

        p = Prog2()
        qld = self.sem("g_qld"); kld = self.sem("g_kld")
        vld2 = [self.sem("g_vld0"), self.sem("g_vld1")]
        stm = [self.sem(f"g_st{i}") for i in range(2)]
        exs = [self.sem(f"g_ex{i}") for i in range(2)]
        pvs = self.sem("g_pv"); ape = self.sem("g_ape")
        dnm = self.sem("g_dnm"); dne = self.sem("g_dne")
        rc = self.sem("g_rc"); bmm = self.sem("g_bmm"); bc = self.sem("g_bc")
        aoe = self.sem("g_aoe")
        qsl = pl.take("qslG", [128, NCHUNK, SLAB], BF16)
        kfsl = pl.take("kslG", [128, NCHUNK, SLAB], BF16)
        vsl = pl.take("vslG", [128, 2, 4, D], BF16)
        aopart = pl.take("aopG", [128, 8, SLAB], F32)
        denpart = pl.take("denG", [1, 8 * SLAB], F32)
        pt = pl.take("ptG", [128, 2, SLAB], BF16)
        dnr = pl.take("dnrG", [1, SLAB], F32)
        tnn = pl.take("tnG", [1, SLAB], F32)
        rcb = pl.take("rcbG", [128, SLAB], F32)
        qv = qsp.rearrange("(cc p) t -> p cc t", p=128)
        kv_ = ksp.rearrange("(cc p) t -> p cc t", p=128)
        vv_ = vsp.rearrange("(kc p) d -> p kc d", p=128)
        sc128 = 1.0 / math.sqrt(128.0)
        vsl_last = [0, 0]
        nblk = 0
        for s in range(NSLAB):
            sl = slice(s * SLAB, (s + 1) * SLAB)
            p.add("sync", dma(qsl[:], qv[:, :, sl]),
                  waits=[x.at() for x in stm], incs=[qld.inc(16)])
            for hg in range(2):
                for kk in range(s + 1):
                    bv = nblk % 2
                    nblk += 1
                    p.add("sync", dma(kfsl[:], kv_[:, :, kk * SLAB:(kk + 1) * SLAB]),
                          waits=[x.at() for x in stm], incs=[kld.inc(16)])
                    p.add("sync", dma(vsl[:, bv], vv_[:, 4 * kk:4 * kk + 4, :]),
                          waits=[(pvs.h, vsl_last[bv])], incs=[vld2[bv].inc(16)])
                    vsl_last[bv] = pvs.v + 32
                    for hh in range(8):
                        h = 8 * hg + hh
                        pba = self.pb[4 + hh % 2]
                        dsl = slice(hh * SLAB, (hh + 1) * SLAB)
                        for kcl in range(4):
                            bank = (hh * 4 + kcl) % 2
                            pbst = self.pb[bank]
                            diag = kk == s
                            p.add("tensor", mm(pbst[:, :],
                                               kfsl[:, h, kcl * 128:(kcl + 1) * 128],
                                               qsl[:, h, :], True, not diag),
                                  waits=[qld.at(), kld.at(), exs[bank].at()],
                                  incs=[] if diag else [stm[bank].inc()])
                            if diag:
                                p.add("tensor", mm(pbst[:, :], c["cID128"][:],
                                                   c["cMASKT"][:, kcl, :],
                                                   False, True),
                                      incs=[stm[bank].inc()])
                            p.add("scalar",
                                  lambda e, pbst=pbst, bank=bank: e.activation(
                                      pt[:, bank, :], pbst[:, :], AF.Exp,
                                      scale=sc128),
                                  waits=[stm[bank].at(), (pvs.h, pvs.v - 1)],
                                  incs=[exs[bank].inc()])
                            p.add("tensor", mm(self.pb[6][0:1, :], c["cONES128"][:],
                                               pt[:, bank, :], kcl == 0, kcl == 3),
                                  waits=[exs[bank].at()] +
                                        ([ape.at()] if kcl == 0 else []))
                            p.add("tensor", mm(pba[:, :],
                                               vsl[:, bv, kcl, h * 128:(h + 1) * 128],
                                               pt[:, bank, :], kcl == 0, kcl == 3),
                                  waits=[vld2[bv].at()], incs=[pvs.inc()])
                        if kk == 0:
                            p.add("scalar", lambda e, pba=pba, hh=hh: e.copy(
                                aopart[:, hh, :], pba[:, :]),
                                waits=[pvs.at(), aoe.at()], incs=[ape.inc()])
                            p.add("scalar", lambda e, dsl=dsl: e.copy(
                                denpart[0:1, dsl], self.pb[6][0:1, :]),
                                incs=[ape.inc()])
                        else:
                            p.add("vector", lambda e, pba=pba, hh=hh: e.tensor_add(
                                aopart[:, hh, :], aopart[:, hh, :], pba[:, :]),
                                waits=[pvs.at()], incs=[ape.inc()])
                            p.add("vector", lambda e, dsl=dsl: e.tensor_add(
                                denpart[0:1, dsl], denpart[0:1, dsl],
                                self.pb[6][0:1, :]),
                                waits=[pvs.at()], incs=[ape.inc()])
                for hh in range(8):
                    h = 8 * hg + hh
                    dsl = slice(hh * SLAB, (hh + 1) * SLAB)
                    p.add("scalar", lambda e, dsl=dsl: e.activation(
                        dnr[0:1, :], denpart[0:1, dsl], AF.Ln),
                        waits=[ape.at(), bmm.at()])
                    p.add("scalar", lambda e: e.activation(
                        dnr[0:1, :], dnr[0:1, :], AF.Exp, scale=-1.0),
                        incs=[dne.inc()])
                    p.add("vector", lambda e, dsl=dsl: e.tensor_mul(
                        tnn[0:1, :], denpart[0:1, dsl], dnr[0:1, :]),
                        waits=[dne.at()])
                    p.add("vector", lambda e: e.tensor_scalar(
                        tnn[0:1, :], tnn[0:1, :], -1.0, 2.0, ALU.mult, ALU.add))
                    p.add("vector", lambda e: e.tensor_mul(
                        dnr[0:1, :], dnr[0:1, :], tnn[0:1, :]), incs=[rc.inc()])
                    p.add("tensor", mm(self.pb[7][:, :], c["cONESROW"][:],
                                       dnr[0:1, :], True, True),
                          waits=[rc.at(), bc.at()], incs=[bmm.inc()])
                    p.add("scalar", lambda e: e.copy(rcb[:, :], self.pb[7][:, :]),
                          waits=[bmm.at(), aoe.at()], incs=[bc.inc()])
                    p.add("vector", lambda e, h=h, hh=hh, sl=sl: e.tensor_mul(
                        ao[:, h, sl], aopart[:, hh, :], rcb[:, :]),
                        waits=[bc.at()], incs=[aoe.inc()])
        p.emit(nc, "l0f_B")
        pl.reset(M1)

        self.proj_spill_phase(wp, ao, ysp)
        pl.reset(M1)
        self.residual_phase(ysp, xdr, ydr, last=last)
        pl.reset(frame)

    def proj_spill_phase(self, wdr, ao, ysp, name="projC"):
        nc, pl, c = self.nc, self.plan, self.c
        p = Prog2()
        M = pl.mark()
        ld = self.sem("c_ld"); wcast = self.sem("c_wc"); gmm = self.sem("c_gmm")
        yev = self.sem("c_yev"); csp = self.sem("c_sp")
        wf = pl.take("wfC", [128, NCHUNK, 128], F32)
        wbf = pl.take("wbC", [128, 2, NCHUNK, 128], BF16)
        yb = pl.take("ybC", [128, 2, SLAB], BF16)
        for c2 in range(16):
            b = c2 % 2
            self.load_w_tiles(p, wdr, c2 * 128, 128, wf, wbf, b, ld, wcast,
                              gmm, gmm.v - 4)
            for s in range(NSLAB):
                sl = slice(s * SLAB, (s + 1) * SLAB)
                bq = (c2 * 4 + s) % 2
                pbp = self.pb[bq]
                for k in range(NCHUNK):
                    p.add("tensor", mm(pbp[:, :], wbf[:, b, k, :], ao[:, k, sl],
                                       k == 0, k == 15),
                          waits=([wcast.at(), (yev.h, yev.v - 1)] if k == 0 else []),
                          incs=[gmm.inc()] if k == 15 else [])
                p.add("scalar", lambda e, pbp=pbp, bq=bq: e.copy(yb[:, bq], pbp[:, :]),
                      waits=[gmm.at(), (csp.h, csp.v - 16)], incs=[yev.inc()])
                p.add("gpsimd", lambda e, c2=c2, sl=sl, bq=bq: e.dma_start(
                    ysp[c2 * 128:(c2 + 1) * 128, sl], yb[:, bq]),
                    waits=[yev.at()], incs=[csp.inc(16)])
        p.add("gpsimd", None, waits=[csp.at()])
        p.emit(nc, name)
        pl.reset(M)

    def stage_l1a(self, xdr, ydr, last=False):
        nc, pl, c = self.nc, self.plan, self.c
        ysp = self.dram_tmp("sp_y", [D, T], BF16)
        cwt = self.dram_in("CwT", [D, D])
        dwt = self.dram_in("DwT", [D, D])
        cb = self.dram_in("Cb", [1, D])
        db = self.dram_in("Db", [1, D])
        xbf = c["xbf"]
        frame = pl.mark()
        p = Prog2()
        ld = self.sem("a_ld"); wcast = self.sem("a_wc"); w1ok = self.sem("a_w1")
        pe1 = self.sem("a_pe1"); gmm = self.sem("a_gmm"); ev = self.sem("a_ev")
        xpd = self.sem("h_xp"); sc_ = self.sem("h_sc"); sp = self.sem("a_sp")
        sg = self.sem("h_sg")
        w1row = pl.take("w1rH", [1, 32 * 128], BF16)
        cbrow = pl.take("cbH", [1, D], BF16)
        dbrow = pl.take("dbH", [1, D], BF16)
        cbf = pl.take("cbfH", [1, D], F32)
        wf = pl.take("wfH", [128, NCHUNK, 128], F32)
        wc_ = pl.take("wcH", [128, 2, NCHUNK, 128], BF16)
        wd_ = pl.take("wdH", [128, 2, NCHUNK, 128], BF16)
        uf = pl.take("ufH", [128, SLAB], F32)
        gs = pl.take("gsH", [128, SLAB], F32)
        xp = pl.take("xpH", [128, T], F32)
        ysc = pl.take("yscH", [128, T], F32)
        ybo = pl.take("yboH", [128, 2, T], BF16)
        cbl = self.sem("h_cbl")
        p.add("sync", dma(cbf[:], cb), incs=[cbl.inc(16)])
        p.add("vector", lambda e: e.tensor_copy(cbrow[:], cbf[:]),
              waits=[cbl.at()], incs=[ev.inc()])
        p.add("sync", dma(cbf[:], db), waits=[ev.at()], incs=[cbl.inc(16)])
        p.add("vector", lambda e: e.tensor_copy(dbrow[:], cbf[:]),
              waits=[cbl.at()], incs=[ev.inc()])
        bias_rdy = (ev.h, ev.v)
        for cc in range(NCHUNK):
            for wi, (src, dst) in enumerate(((cwt, wc_), (dwt, wd_))):
                idx = 2 * cc + wi
                b = cc % 2
                v = src.rearrange("(kc kp) n -> kp kc n", kp=128)
                p.add("sync", dma(wf[:], v[:, :, cc * 128:(cc + 1) * 128]),
                      waits=[wcast.at()], incs=[ld.inc(16)])
                p.add("vector", lambda e, dst=dst, b=b: e.tensor_copy(
                    dst[:, b], wf[:]),
                    waits=[ld.at(), (gmm.h, gmm.v - 8)], incs=[wcast.inc()])
                for k in range(NCHUNK):
                    p.add("tensor", mm(self.pb[5][0:1, 0:128], c["cONES128"][:],
                                       dst[:, b, k, :], k == 0, k == 15),
                          waits=[wcast.at()] if k == 0 else [],
                          incs=[pe1.inc()] if k == 15 else [])
                p.add("scalar", lambda e, idx=idx: e.copy(
                    w1row[0:1, idx * 128:(idx + 1) * 128], self.pb[5][0:1, 0:128]),
                    waits=[pe1.at()], incs=[w1ok.inc()])
            b = cc % 2
            b2 = cc % 2
            for s in range(NSLAB):
                sl = slice(s * SLAB, (s + 1) * SLAB)
                pbu = self.pb[s % 2]
                pbg = self.pb[2 + s % 2]
                for k in range(NCHUNK):
                    p.add("tensor", mm(pbu[:, :], wc_[:, b, k, :], xbf[:, k, sl],
                                       k == 0, False),
                          waits=([(ev.h, ev.v - 3)] if k == 0 else []))
                p.add("tensor", mm(pbu[:, :], w1row[0:1, 2 * cc * 128:
                                                    (2 * cc + 1) * 128],
                                   c["negmu_row"][0:1, sl], False, False),
                      waits=[w1ok.at()])
                p.add("tensor", mm(pbu[:, :], cbrow[0:1, cc * 128:(cc + 1) * 128],
                                   c["invr_row"][0:1, sl], False, True),
                      waits=[bias_rdy], incs=[gmm.inc()])
                for k in range(NCHUNK):
                    p.add("tensor", mm(pbg[:, :], wd_[:, b2, k, :], xbf[:, k, sl],
                                       k == 0, False),
                          waits=([(sg.h, sg.v - 1)] if k == 0 else []))
                p.add("tensor", mm(pbg[:, :], w1row[0:1, (2 * cc + 1) * 128:
                                                    (2 * cc + 2) * 128],
                                   c["negmu_row"][0:1, sl], False, False),
                      waits=[w1ok.at()])
                p.add("tensor", mm(pbg[:, :], dbrow[0:1, cc * 128:(cc + 1) * 128],
                                   c["invr_row"][0:1, sl], False, True),
                      waits=[bias_rdy], incs=[gmm.inc()])
                p.add("vector", lambda e, pbu=pbu, sl=sl: e.tensor_mul(
                    uf[:, :], pbu[:, :], c["rb"][:, sl]),
                    waits=[gmm.at(), xpd.at()], incs=[ev.inc()])
                p.add("vector", lambda e, pbg=pbg, sl=sl: e.tensor_mul(
                    gs[:, :], pbg[:, :], c["rb"][:, sl]),
                    waits=[sg.at()], incs=[ev.inc()])
                p.add("scalar", lambda e: e.activation(gs[:, :], gs[:, :],
                                                       AF.Sigmoid),
                      waits=[ev.at()], incs=[sg.inc()])
                p.add("vector", lambda e, sl=sl: e.tensor_mul(
                    xp[:, sl], uf[:, :], gs[:, :]),
                    waits=[sg.at()], incs=[xpd.inc()])
            p.add("vector", lambda e, cc=cc: e.tensor_tensor_scan(
                ysc[:, :], c["ADIAG"][:, cc:cc + 1].broadcast_to([128, T]),
                xp[:, :], 0.0, ALU.mult, ALU.add),
                waits=[xpd.at(), (sc_.h, sc_.v - 1)], incs=[sc_.inc()])
            bo = cc % 2
            p.add("vector", lambda e, cc=cc, bo=bo: e.tensor_scalar(
                ybo[:, bo, :], ysc[:, :], c["AB"][:, cc:cc + 1], None, ALU.mult),
                waits=[(sp.h, 16 * (sp.v // 16 - 1))], incs=[sc_.inc()])
            p.add("gpsimd", lambda e, cc=cc, bo=bo: e.dma_start(
                ysp[cc * 128:(cc + 1) * 128, :], ybo[:, bo, :]),
                waits=[sc_.at()], incs=[sp.inc(16)])
        p.add("gpsimd", None, waits=[sp.at()])
        p.emit(nc, "l1a_A")
        pl.reset(frame)
        self.residual_phase(ysp, xdr, ydr, last=last)
        pl.reset(frame)

    def stage_ghost(self):
        nc, pl, c = self.nc, self.plan, self.c
        gi = self.dram_in("gidxT", [D, INTER], I32)
        ui = self.dram_in("uidxT", [D, INTER], I32)
        di = self.dram_in("didxT", [INTER, D], I32)
        gw = self.dram_tmp("gw", [D, INTER], BF16)
        uw = self.dram_tmp("uw", [D, INTER], BF16)
        dw = self.dram_tmp("dw", [INTER, D], BF16)
        w1g = self.dram_tmp("w1g", [1, INTER], F32)
        w1u = self.dram_tmp("w1u", [1, INTER], F32)
        frame = pl.mark()
        p = Prog2()
        ld = self.sem("gh_ld"); cst = self.sem("gh_cs"); acc = self.sem("gh_ac")
        pe = self.sem("gh_pe"); ws = self.sem("gh_ws"); sp = self.sem("gh_sp")
        idxt = pl.take("idxt", [128, INTER], I32)
        idxf = pl.take("idxf", [128, 2, INTER], BF16)
        accb = pl.take("accb", [128, 2, INTER], BF16)
        tb = pl.take("tb", [128, 2, INTER], BF16)
        w1acc = pl.take("w1acc", [1, INTER], F32)
        ti = 0
        for src, dst, w1d, nr, ncols in ((gi, gw, w1g, 16, INTER),
                                         (ui, uw, w1u, 16, INTER),
                                         (di, dw, None, ICHUNK, D)):
            for rc in range(nr):
                b = ti % 2
                eng = "vector" if ti % 3 != 2 else "gpsimd"
                p.add("sync", dma(idxt[:, 0:ncols],
                                  src[rc * 128:(rc + 1) * 128, :]),
                      waits=[cst.at()], incs=[ld.inc(16)])
                p.add(eng, lambda e, b=b, ncols=ncols: e.tensor_copy(
                    idxf[:, b, 0:ncols], idxt[:, 0:ncols]),
                    waits=[ld.at(), (acc.h, acc.v - 1)], incs=[cst.inc()])
                for j in range(16):
                    if j == 0:
                        p.add(eng, lambda e, b=b, ncols=ncols, j=j: e.tensor_scalar(
                            accb[:, b, 0:ncols], idxf[:, b, 0:ncols], float(j),
                            c["LUTB"][:, j:j + 1], ALU.is_equal, ALU.mult),
                            waits=[(sp.h, 16 * (sp.v // 16 - 1)),
                                   (pe.h, pe.v - 11)])
                    else:
                        p.add(eng, lambda e, b=b, ncols=ncols, j=j: e.tensor_scalar(
                            tb[:, b, 0:ncols], idxf[:, b, 0:ncols], float(j),
                            c["LUTB"][:, j:j + 1], ALU.is_equal, ALU.mult))
                        p.add(eng, lambda e, b=b, ncols=ncols: e.tensor_tensor(
                            accb[:, b, 0:ncols], accb[:, b, 0:ncols],
                            tb[:, b, 0:ncols], ALU.add),
                            incs=[acc.inc()] if j == 15 else [])
                p.add("gpsimd", lambda e, dst=dst, rc=rc, b=b, ncols=ncols:
                      e.dma_start(dst[rc * 128:(rc + 1) * 128, :],
                                  accb[:, b, 0:ncols]),
                      waits=[acc.at()], incs=[sp.inc(16)])
                if w1d is not None:
                    for bi in range(11):
                        c0, c1 = bi * 512, min((bi + 1) * 512, ncols)
                        p.add("tensor", mm(self.pb[0][0:1, 0:c1 - c0],
                                           c["cONES128"][:],
                                           accb[:, b, c0:c1], True, True),
                              waits=[acc.at(), ws.at()], incs=[pe.inc()])
                        if rc == 0:
                            p.add("scalar", lambda e, c0=c0, c1=c1:
                                  e.copy(w1acc[0:1, c0:c1],
                                         self.pb[0][0:1, 0:c1 - c0]),
                                  waits=[pe.at()], incs=[ws.inc()])
                        else:
                            p.add("vector", lambda e, c0=c0, c1=c1:
                                  e.tensor_add(w1acc[0:1, c0:c1],
                                               w1acc[0:1, c0:c1],
                                               self.pb[0][0:1, 0:c1 - c0]),
                                  waits=[pe.at()], incs=[ws.inc()])
                if w1d is not None and rc == nr - 1:
                    p.add("gpsimd", dma(w1d, w1acc[0:1, :]), waits=[ws.at()],
                          incs=[sp.inc(16)])
                    p.add("scalar", None, waits=[(sp.h, sp.v)])
                ti += 1
        p.add("gpsimd", None, waits=[sp.at()])
        p.emit(nc, "ghost")
        pl.reset(frame)

    def stage_l1f(self, xdr, ydr, last=True, dbg=False):
        nc, pl, c = self.nc, self.plan, self.c
        gw = self.dram_tmp("gw", [D, INTER], BF16)
        uw = self.dram_tmp("uw", [D, INTER], BF16)
        dw = self.dram_tmp("dw", [INTER, D], BF16)
        w1gd = self.dram_tmp("w1g", [1, INTER], F32)
        w1ud = self.dram_tmp("w1u", [1, INTER], F32)
        xbf = c["xbf"]
        frame = pl.mark()
        w1g = pl.take("w1gL", [1, INTER], BF16)
        w1u = pl.take("w1uL", [1, INTER], BF16)
        Mw = pl.mark()
        w1f = pl.take("w1fL", [1, INTER], F32)
        pld = Prog2()
        ld0 = self.sem("l_ld0"); cv0 = self.sem("l_cv0")
        pld.add("sync", dma(w1f[:], w1gd), incs=[ld0.inc(16)])
        pld.add("vector", lambda e: e.tensor_copy(w1g[:], w1f[:]),
                waits=[ld0.at()], incs=[cv0.inc()])
        pld.add("sync", dma(w1f[:], w1ud), waits=[cv0.at()], incs=[ld0.inc(16)])
        pld.add("vector", lambda e: e.tensor_copy(w1u[:], w1f[:]),
                waits=[ld0.at()], incs=[cv0.inc()])
        pld.emit(nc, "l1f_w1")
        pl.reset(Mw)
        M1 = pl.mark()
        gv = gw.rearrange("(kc kp) n -> kp kc n", kp=128)
        uv = uw.rearrange("(kc kp) n -> kp kc n", kp=128)
        dv_ = dw.rearrange("(kc kp) n -> kp kc n", kp=128)
        xv = xdr.rearrange("(cc p) t -> p cc t", p=128)
        ov = ydr.rearrange("(cc p) t -> p cc t", p=128)
        for s in range(NSLAB):
            sl = slice(s * SLAB, (s + 1) * SLAB)
            p = Prog2()
            ld = self.sem("l_ld"); gmm = self.sem("l_gmm"); ev = self.sem("l_ev")
            sil = self.sem("l_sil"); hm = self.sem("l_hm")
            ybsl = pl.take("ybfL", [128, NCHUNK, SLAB], BF16)
            hmid = pl.take("hmidL", [128, ICHUNK, SLAB], BF16)
            Mh = pl.mark()
            wgt = pl.take("wgtL", [128, 2, NCHUNK, 128], BF16)
            wut = pl.take("wutL", [128, 2, NCHUNK, 128], BF16)
            gf = pl.take("gfL", [128, SLAB], F32)
            sgb = pl.take("sgbL", [128, SLAB], BF16)
            ubf = pl.take("ubfL", [128, SLAB], BF16)
            ld2 = [self.sem("l_lda"), self.sem("l_ldb")]
            for ic in range(ICHUNK):
                b = ic % 2
                p.add("sync", dma(wgt[:, b], gv[:, :, ic * 128:(ic + 1) * 128]),
                      waits=[(gmm.h, gmm.v - 2)], incs=[ld2[b].inc(16)])
                p.add("sync", dma(wut[:, b], uv[:, :, ic * 128:(ic + 1) * 128]),
                      incs=[ld2[b].inc(16)])
                pbg = self.pb[b]
                pbu = self.pb[2 + b]
                for k in range(NCHUNK):
                    p.add("tensor", mm(pbg[:, :], wgt[:, b, k, :], xbf[:, k, sl],
                                       k == 0, False),
                          waits=([ld2[b].at(), (ev.h, ev.v - 2)] if k == 0 else []))
                p.add("tensor", mm(pbg[:, :], w1g[0:1, ic * 128:(ic + 1) * 128],
                                   c["negmu_row"][0:1, sl], False, True),
                      incs=[gmm.inc()])
                for k in range(NCHUNK):
                    p.add("tensor", mm(pbu[:, :], wut[:, b, k, :], xbf[:, k, sl],
                                       k == 0, False),
                          waits=([ld2[b].at(), (sil.h, sil.v - 1)]
                                 if k == 0 else []))
                p.add("tensor", mm(pbu[:, :], w1u[0:1, ic * 128:(ic + 1) * 128],
                                   c["negmu_row"][0:1, sl], False, True),
                      incs=[gmm.inc()])
                p.add("vector", lambda e, pbg=pbg, sl=sl: e.tensor_mul(
                    gf[:, :], pbg[:, :], c["rb"][:, sl]),
                    waits=[gmm.at(), sil.at()], incs=[ev.inc()])
                p.add("scalar", lambda e: e.activation(sgb[:, :], gf[:, :],
                                                       AF.Silu),
                      waits=[ev.at(), hm.at()], incs=[sil.inc()])
                p.add("vector", lambda e, pbu=pbu, sl=sl: e.tensor_mul(
                    ubf[:, :], pbu[:, :], c["rb"][:, sl]),
                    waits=[(hm.h, hm.v - 1)], incs=[ev.inc()])
                p.add("vector", lambda e, ic=ic: e.tensor_mul(
                    hmid[:, ic, :], sgb[:, :], ubf[:, :]),
                    waits=[sil.at()], incs=[hm.inc()])
                if dbg:
                    hdbg = self.dram_tmp("hdbg", [INTER, T], BF16)
                    p.add("gpsimd", lambda e, ic=ic, sl=sl: e.dma_start(
                        hdbg[ic * 128:(ic + 1) * 128, sl], hmid[:, ic, :]),
                        waits=[hm.at()],
                        incs=[self.sem("l_dbg").inc(16)])
            pl.reset(Mh)
            KH = 22
            wdt = pl.take("wdtL", [128, 2, KH, 128], BF16)
            dgm = self.sem("l_dgm"); dev = self.sem("l_dev")
            dev0 = dev.v
            half_last = getattr(self, "_l1f_half_last", [0, 0])
            for oc in range(NCHUNK):
                pbp = self.pb[4 + oc % 2]
                for half in range(2):
                    k0 = half * KH
                    kn = min(KH, ICHUNK - k0)
                    dld = self.sem(f"l_dld{half}")
                    dw_waits = [(dgm.h, half_last[half])]
                    if oc == 0 and half == 0:
                        dw_waits += [gmm.at(), hm.at()]
                    p.add("sync", dma(wdt[:, half, 0:kn],
                                      dv_[:, k0:k0 + kn, oc * 128:(oc + 1) * 128]),
                          waits=dw_waits, incs=[dld.inc(16)])
                    for kq in range(kn):
                        k = k0 + kq
                        p.add("tensor", mm(pbp[:, :], wdt[:, half, kq, :],
                                           hmid[:, k, :],
                                           k == 0, k == ICHUNK - 1),
                              waits=([dld.at(), hm.at(), (dev.h, dev.v - 1)]
                                     if k == 0 else ([dld.at()] if kq == 0 else [])),
                              incs=[dgm.inc()] if kq == kn - 1 else [])
                    half_last[half] = dgm.v
                self._l1f_half_last = half_last
                p.add("scalar", lambda e, pbp=pbp, oc=oc: e.copy(
                    ybsl[:, oc, :], pbp[:, :]),
                    waits=[dgm.at()], incs=[dev.inc()])
                if dbg:
                    ydbg = self.dram_tmp("ydbg", [D, T], BF16)
                    p.add("gpsimd", lambda e, oc=oc, sl=sl: e.dma_start(
                        ydbg[oc * 128:(oc + 1) * 128, sl], ybsl[:, oc, :]),
                        waits=[dev.at()],
                        incs=[self.sem("l_dbg").inc(16)])
            yready = [(dev.h, dev0 + k + 1) for k in range(NCHUNK)]
            self.residual_slab(p, ybsl, s, xv, ov, last=last, yready=yready)
            p.emit(nc, f"l1f_s{s}")
            pl.reset(M1)
        pl.reset(frame)

    def residual_slab(self, p, ybsl, s, xv, ov, last, yready=None):
        nc, pl, c = self.nc, self.plan, self.c
        M = pl.mark()
        sq = self.sem("d_sq"); pe = self.sem("d_pe"); fin = self.sem("d_fin")
        xld = self.sem("d_xld"); st = self.sem("d_st"); xa = self.sem("d_xa")
        xc = self.sem("d_xc"); nxt = self.sem("d_nxt")
        ysq = pl.take("ysqS", [128, 1, SLAB], BF16)
        yrow = pl.take("yrowS", [1, 2 * SLAB], F32)
        rowbuf = yrow
        aysb = pl.take("ayS", [128, SLAB], F32)
        bysb = pl.take("byS", [128, SLAB], F32)
        xf = pl.take("xfS", [128, 1, SLAB], F32)
        xo = pl.take("xoS", [128, 1, SLAB], F32)
        xsq = pl.take("xsqS", [128, 1, SLAB], BF16)
        pbS, pbN, pbB = self.pb[6], self.pb[7], self.pb[6]
        sl = slice(s * SLAB, (s + 1) * SLAB)
        for k in range(NCHUNK):
            b = 0
            yw = [yready[k]] if yready else []
            p.add("scalar", lambda e, k=k, b=b: e.activation(
                ysq[:, b], ybsl[:, k, :], AF.Square),
                waits=[pe.at()] + yw, incs=[sq.inc()])
            p.add("tensor", mm(pbS[0:1, :], c["cMEANVEC"][:], ybsl[:, k, :],
                               k == 0, k == 15),
                  waits=([fin.at()] if k == 0 else []) + yw)
            p.add("tensor", mm(pbS[32:33, :], c["cMEANVEC"][:], ysq[:, b],
                               k == 0, k == 15), waits=[sq.at()],
                  incs=[pe.inc()])
        p.add("scalar", lambda e: e.copy(yrow[0:1, 0:SLAB], pbS[0:1, :]),
              waits=[pe.at()])
        p.add("scalar", lambda e: e.activation(yrow[0:1, SLAB:],
                                               yrow[0:1, 0:SLAB], AF.Square),
              incs=[fin.inc()])
        p.add("vector", lambda e: e.tensor_sub(yrow[0:1, SLAB:], pbS[32:33, :],
                                               yrow[0:1, SLAB:]),
              waits=[fin.at()], incs=[fin.inc()])
        p.add("scalar", lambda e: e.activation(yrow[0:1, SLAB:], yrow[0:1, SLAB:],
                                               AF.Ln, bias=1e-5),
              waits=[fin.at()])
        p.add("scalar", lambda e: e.activation(yrow[0:1, SLAB:], yrow[0:1, SLAB:],
                                               AF.Exp, scale=-0.5),
              incs=[fin.inc()])
        p.add("vector", lambda e: e.tensor_mul(yrow[0:1, 0:SLAB],
                                               yrow[0:1, 0:SLAB],
                                               yrow[0:1, SLAB:]),
              waits=[fin.at()], incs=[fin.inc()])
        p.add("tensor", mm(pbB[:, :], c["cONESROW"][:], yrow[0:1, SLAB:],
                           True, True), waits=[fin.at()], incs=[pe.inc()])
        p.add("scalar", lambda e: e.copy(aysb[:, :], pbB[:, :]),
              waits=[pe.at(), xa.at()], incs=[fin.inc()])
        p.add("tensor", mm(pbB[:, :], c["cONESROW"][:], yrow[0:1, 0:SLAB],
                           True, True), waits=[fin.at()], incs=[pe.inc()])
        p.add("scalar", lambda e: e.copy(bysb[:, :], pbB[:, :]),
              waits=[pe.at()], incs=[fin.inc()])
        for k in range(NCHUNK):
            b = 0
            p.add("sync", dma(xf[:, b], xv[:, k, sl]),
                  waits=[xa.at()], incs=[xld.inc(16)])
            p.add("vector", lambda e, k=k, b=b: e.tensor_mul(
                xo[:, b], ybsl[:, k, :], aysb[:, :]),
                waits=[fin.at(), xld.at(), st.at()])
            p.add("vector", lambda e, b=b: e.tensor_sub(xo[:, b], xo[:, b],
                                                        bysb[:, :]))
            p.add("vector", lambda e, b=b: e.tensor_add(xo[:, b], xo[:, b],
                                                        xf[:, b]),
                  incs=[xa.inc()])
            p.add("gpsimd", lambda e, k=k, sl=sl, b=b: e.dma_start(
                ov[:, k, sl], xo[:, b]), waits=[xa.at()], incs=[st.inc(16)])
            if not last:
                p.add("vector", lambda e, k=k, sl=sl, b=b: e.tensor_copy(
                    c["xbf"][:, k, sl], xo[:, b]), incs=[xc.inc()])
                p.add("scalar", lambda e, k=k, sl=sl, b=b: e.activation(
                    xsq[:, b], c["xbf"][:, k, sl], AF.Square),
                    waits=[xc.at(), nxt.at()], incs=[sq.inc()])
                p.add("tensor", mm(pbN[0:1, :], c["cMEANVEC"][:],
                                   c["xbf"][:, k, sl], k == 0, k == 15),
                      waits=[xc.at()])
                p.add("tensor", mm(pbN[32:33, :], c["cMEANVEC"][:], xsq[:, b],
                                   k == 0, k == 15), waits=[sq.at()],
                      incs=[nxt.inc()])
        if not last:
            self.finalize_ln(p, pbN, c["r_row"][0:1, sl],
                             c["negmu_row"][0:1, sl], rowbuf, nxt, fin,
                             invr_dst=c["invr_row"][0:1, sl],
                             rb_dst=c["rb"][:, sl], pbB=self.pb[6])
            p.add("tensor", None, waits=[fin.at()])
        p.add("gpsimd", None, waits=[st.at()])
        pl.reset(M)

    def residual_phase(self, ysp, xdr, ydr, last=False):
        nc, pl, c = self.nc, self.plan, self.c
        p = Prog2()
        M = pl.mark()
        ld = self.sem("d_ld"); sq = self.sem("d_sq"); pe = self.sem("d_pe")
        fin = self.sem("d_fin"); ev = self.sem("d_ev"); st = self.sem("d_st")
        xld2 = [self.sem("d_xld0"), self.sem("d_xld1")]; nxt = self.sem("d_nxt")
        ybsl = pl.take("ybD", [128, NCHUNK, SLAB], BF16)
        ysq = pl.take("ysqD", [128, 2, SLAB], BF16)
        rowbuf = pl.take("rowD", [1, 2 * SLAB], F32)
        yrow = pl.take("yrowD", [1, 2 * SLAB], F32)
        aysb = pl.take("ayD", [128, SLAB], F32)
        bysb = pl.take("byD", [128, SLAB], F32)
        xf = pl.take("xfD", [128, 2, SLAB], F32)
        xo = pl.take("xoD", [128, 2, SLAB], F32)
        xsq = pl.take("xsqD", [128, 2, SLAB], BF16)
        yv = ysp.rearrange("(cc p) t -> p cc t", p=128)
        xv = xdr.rearrange("(cc p) t -> p cc t", p=128)
        ov = ydr.rearrange("(cc p) t -> p cc t", p=128)
        pbS, pbN, pbB = self.pb[2], self.pb[3], self.pb[6]
        xa = self.sem("d_xa")
        xc = self.sem("d_xc")
        for s in range(NSLAB):
            sl = slice(s * SLAB, (s + 1) * SLAB)
            p.add("sync", dma(ybsl[:], yv[:, :, sl]),
                  waits=[sq.at(), pe.at(), xa.at()], incs=[ld.inc(16)])
            for k in range(NCHUNK):
                b = k % 2
                p.add("scalar", lambda e, k=k, b=b: e.activation(
                    ysq[:, b], ybsl[:, k, :], AF.Square),
                    waits=[ld.at(), (pe.h, pe.v - 1)], incs=[sq.inc()])
                p.add("tensor", mm(pbS[0:1, :], c["cMEANVEC"][:], ybsl[:, k, :],
                                   k == 0, k == 15),
                      waits=[ld.at(), fin.at()] if k == 0 else [])
                p.add("tensor", mm(pbS[32:33, :], c["cMEANVEC"][:], ysq[:, b],
                                   k == 0, k == 15), waits=[sq.at()],
                      incs=[pe.inc()])
            p.add("scalar", lambda e: e.copy(yrow[0:1, 0:SLAB], pbS[0:1, :]),
                  waits=[pe.at()])
            p.add("scalar", lambda e: e.activation(yrow[0:1, SLAB:],
                                                   yrow[0:1, 0:SLAB], AF.Square),
                  incs=[fin.inc()])
            p.add("vector", lambda e: e.tensor_sub(yrow[0:1, SLAB:], pbS[32:33, :],
                                                   yrow[0:1, SLAB:]),
                  waits=[fin.at()], incs=[fin.inc()])
            p.add("scalar", lambda e: e.activation(yrow[0:1, SLAB:], yrow[0:1, SLAB:],
                                                   AF.Ln, bias=1e-5),
                  waits=[fin.at()])
            p.add("scalar", lambda e: e.activation(yrow[0:1, SLAB:], yrow[0:1, SLAB:],
                                                   AF.Exp, scale=-0.5),
                  incs=[fin.inc()])
            p.add("vector", lambda e: e.tensor_mul(yrow[0:1, 0:SLAB],
                                                   yrow[0:1, 0:SLAB],
                                                   yrow[0:1, SLAB:]),
                  waits=[fin.at()], incs=[fin.inc()])
            p.add("tensor", mm(pbB[:, :], c["cONESROW"][:], yrow[0:1, SLAB:],
                               True, True), waits=[fin.at()], incs=[pe.inc()])
            p.add("scalar", lambda e: e.copy(aysb[:, :], pbB[:, :]),
                  waits=[pe.at(), xa.at()], incs=[fin.inc()])
            p.add("tensor", mm(pbB[:, :], c["cONESROW"][:], yrow[0:1, 0:SLAB],
                               True, True), waits=[fin.at()], incs=[pe.inc()])
            p.add("scalar", lambda e: e.copy(bysb[:, :], pbB[:, :]),
                  waits=[pe.at()], incs=[fin.inc()])
            for k in range(NCHUNK):
                b = k % 2
                p.add("sync", dma(xf[:, b], xv[:, k, sl]),
                      waits=[(xa.h, xa.v - 1)], incs=[xld2[b].inc(16)])
                p.add("vector", lambda e, k=k, b=b: e.tensor_mul(
                    xo[:, b], ybsl[:, k, :], aysb[:, :]),
                    waits=[fin.at(), xld2[b].at(), (st.h, 16 * (st.v // 16 - 1))])
                p.add("vector", lambda e, b=b: e.tensor_sub(xo[:, b], xo[:, b],
                                                            bysb[:, :]))
                p.add("vector", lambda e, b=b: e.tensor_add(xo[:, b], xo[:, b],
                                                            xf[:, b]),
                      incs=[xa.inc()])
                p.add("gpsimd", lambda e, k=k, sl=sl, b=b: e.dma_start(
                    ov[:, k, sl], xo[:, b]), waits=[xa.at()], incs=[st.inc(16)])
                if not last:
                    p.add("vector", lambda e, k=k, sl=sl, b=b: e.tensor_copy(
                        c["xbf"][:, k, sl], xo[:, b]), incs=[xc.inc()])
                    p.add("scalar", lambda e, k=k, sl=sl, b=b: e.activation(
                        xsq[:, b], c["xbf"][:, k, sl], AF.Square),
                        waits=[xc.at(), (nxt.h, nxt.v - 1)], incs=[sq.inc()])
                    p.add("tensor", mm(pbN[0:1, :], c["cMEANVEC"][:],
                                       c["xbf"][:, k, sl], k == 0, k == 15),
                          waits=[xc.at()])
                    p.add("tensor", mm(pbN[32:33, :], c["cMEANVEC"][:], xsq[:, b],
                                       k == 0, k == 15), waits=[sq.at()],
                          incs=[nxt.inc()])
            if not last:
                sl2 = slice(s * SLAB, (s + 1) * SLAB)
                self.finalize_ln(p, pbN, c["r_row"][0:1, sl2],
                                 c["negmu_row"][0:1, sl2], rowbuf, nxt, fin,
                                 invr_dst=c["invr_row"][0:1, sl2],
                                 rb_dst=c["rb"][:, sl2], pbB=self.pb[7])
                p.add("tensor", None, waits=[fin.at()])
        p.add("gpsimd", None, waits=[st.at()])
        p.emit(nc, "resid")
        pl.reset(M)


N_CORES = 8

def _build_launch(kind):
    nc = bass.Bass("TRN2", target_bir_lowering=False)
    kb = KB(nc)
    want = {"l0a": (), "l0f": ("l0f",), "l1a": ("l1a",), "l1f": ("l1f",)}[kind]
    kb.stage_prep(want=want)
    kb.alloc_globals()
    x0 = kb.dram_in("x0", [D, T]); y0 = kb.dram_out("y0", [D, T])
    x1 = kb.dram_in("x1", [D, T]); y1 = kb.dram_out("y1", [D, T])
    if kind == "l1f":
        kb.stage_ghost()
    for seq, (xdr, ydr) in enumerate(((x0, y0), (x1, y1))):
        p = Prog2()
        kb.stats_pass(p, xdr, xbf_full=kb.c["xbf"], pfx="st")
        p.emit(nc, f"stats{seq}")
        if kind == "l0a":
            kb.stage_l0a(xdr, ydr, last=True)
        elif kind == "l0f":
            kb.stage_l0f(xdr, ydr, last=True)
        elif kind == "l1a":
            kb.stage_l1a(xdr, ydr, last=True)
        else:
            kb.stage_l1f(xdr, ydr, last=True)
    nc.finalize()
    return nc

_LAUNCH_INPUTS = {
    "l0a": ("Wq", "Wk", "Wv", "Wo"),
    "l0f": ("WaQK", "WaV", "Wp", "wstd", "wrec"),
    "l1a": ("CwT", "DwT", "Cb", "Db", "LD", "Afm", "Bfm"),
    "l1f": ("gidxT", "uidxT", "didxT", "lut"),
}
_BASE_CONSTS = ("cCOS", "cSIN", "cR128T", "cID128", "cMASKT", "cONES128",
                "cMEANVEC", "cONESROW", "cONESROWB")
_LAUNCH_CONSTS = {
    "l0a": _BASE_CONSTS,
    "l0f": _BASE_CONSTS + ("cSTDMASK", "cTAILMASK"),
    "l1a": _BASE_CONSTS,
    "l1f": _BASE_CONSTS,
}

def kernel(**inputs):
    w = layout_weights(inputs)
    x = np.asarray(inputs["L0_attn_x"])
    xfm = np.ascontiguousarray(x.transpose(0, 2, 1))
    cur0 = [np.ascontiguousarray(xfm[2 * cid]) for cid in range(N_CORES)]
    cur1 = [np.ascontiguousarray(xfm[2 * cid + 1]) for cid in range(N_CORES)]
    for kind in ("l0a", "l0f", "l1a", "l1f"):
        nc = _build_launch(kind)
        in_maps = []
        for cid in range(N_CORES):
            im = {"x0": cur0[cid], "x1": cur1[cid]}
            for nm in _LAUNCH_INPUTS[kind]:
                im[nm] = w[nm]
            for nm in _LAUNCH_CONSTS[kind]:
                im[nm] = CONSTS[nm]
            in_maps.append(im)
        res = run_bass_kernel_spmd(nc, in_maps, core_ids=list(range(N_CORES)))
        cur0 = [res.results[cid]["y0"] for cid in range(N_CORES)]
        cur1 = [res.results[cid]["y1"] for cid in range(N_CORES)]
    out = np.empty((16, T, D), dtype=np.float32)
    for cid in range(N_CORES):
        out[2 * cid] = cur0[cid].T
        out[2 * cid + 1] = cur1[cid].T
    return out


# version 05
# speedup vs baseline: 1.0157x; 1.0157x over previous
import sys
if "/opt/trn_rl_repo" not in sys.path:
    sys.path.insert(0, "/opt/trn_rl_repo")
import math
import numpy as np
import ml_dtypes
import concourse.bass as bass
import concourse.mybir as mybir
from concourse.bass_utils import run_bass_kernel_spmd

F32 = mybir.dt.float32
BF16 = mybir.dt.bfloat16
I32 = mybir.dt.int32
AF = mybir.ActivationFunctionType
ALU = mybir.AluOpType
AX = mybir.AxisListType

D = 2048
T = 2048
NCHUNK = 16
SLAB = 512
NSLAB = T // SLAB
NEG = -1e30
INTER = 5504
ICHUNK = INTER // 128


def _bf(x):
    return np.asarray(x, dtype=np.float32).astype(ml_dtypes.bfloat16)

def build_consts():
    c = {}
    dh = 64
    pos = np.arange(T, dtype=np.float64)
    invf = 1.0 / (10000.0 ** (np.arange(0, dh, 2, dtype=np.float64) / dh))
    p = np.arange(128)
    d_in_head = p % 64
    freq_idx = np.where(d_in_head < 32, d_in_head, d_in_head - 32)
    ang = pos[None, :] * invf[freq_idx][:, None]
    c["cCOS"] = _bf(np.cos(ang))
    c["cSIN"] = _bf(np.sin(ang))
    R = np.zeros((64, 64), np.float64)
    for d_ in range(32):
        R[d_, d_ + 32] = -1.0
    for d_ in range(32, 64):
        R[d_, d_ - 32] = 1.0
    R128 = np.zeros((128, 128), np.float64)
    R128[:64, :64] = R
    R128[64:, 64:] = R
    c["cR128T"] = _bf(R128.T)
    c["cID128"] = _bf(np.eye(128))
    masks = np.zeros((128, 4, SLAB), np.float64)
    ktl = np.arange(128)
    qtl = np.arange(SLAB)
    for j in range(4):
        bad = (ktl[:, None] + j * 128) > qtl[None, :]
        masks[:, j, :] = np.where(bad, NEG, 0.0)
    c["cMASKT"] = _bf(masks)
    c["cONES128"] = _bf(np.ones((128, 1)))
    c["cMEANVEC"] = _bf(np.full((128, 1), 1.0 / D))
    c["cONESROW"] = np.ones((1, 128), np.float32)
    std_mask = np.zeros((1, 128), np.float32); std_mask[0, :120] = 1.0
    tail_mask = np.zeros((1, 128), np.float32); tail_mask[0, 120:] = 1.0
    c["cSTDMASK"] = std_mask
    c["cTAILMASK"] = tail_mask
    c["cONESROWB"] = _bf(np.ones((1, 128)))
    return c

CONSTS = build_consts()

def layout_weights(inp):
    w = {}
    Wq = np.asarray(inp["L0_attn_Wq"])
    cols = []
    for cchunk in range(16):
        for h in (cchunk, cchunk + 16):
            cols.append(Wq[:, h * 64:(h + 1) * 64])
    w["Wq"] = np.ascontiguousarray(np.concatenate(cols, axis=1))
    Wkv = np.asarray(inp["L0_attn_Wkv"])
    kc = Wkv[:, :512]; vc = Wkv[:, 512:]
    cols = []
    for j in range(4):
        cols.append(kc[:, j * 64:(j + 1) * 64])
        cols.append(kc[:, (j + 4) * 64:(j + 5) * 64])
    w["Wk"] = np.ascontiguousarray(np.concatenate(cols, axis=1))
    w["Wv"] = np.ascontiguousarray(vc)
    Wo = np.asarray(inp["L0_attn_Wo"])
    rows = []
    for cchunk in range(16):
        for h in (cchunk, cchunk + 16):
            rows.append(Wo[h * 64:(h + 1) * 64, :])
    w["Wo"] = np.ascontiguousarray(np.concatenate(rows, axis=0))
    Wa = np.asarray(inp["L0_ffn_Wa"])
    H, Ds, r = 16, 120, 8
    q_std_end = H * Ds; k_std_end = 2 * H * Ds; v_end = k_std_end + D
    cols = []
    for h in range(H):
        cols.append(Wa[:, h * Ds:(h + 1) * Ds])
        cols.append(Wa[:, v_end + h * r: v_end + (h + 1) * r])
        cols.append(Wa[:, q_std_end + h * Ds: q_std_end + (h + 1) * Ds])
        cols.append(Wa[:, v_end + H * r + h * r: v_end + H * r + (h + 1) * r])
    w["WaQK"] = np.ascontiguousarray(np.concatenate(cols, axis=1))
    w["WaV"] = np.ascontiguousarray(Wa[:, k_std_end:v_end])
    w["Wp"] = np.ascontiguousarray(np.asarray(inp["L0_ffn_Wp"]))
    w["CwT"] = np.ascontiguousarray(np.asarray(inp["L1_attn_C_w"]).T)
    w["DwT"] = np.ascontiguousarray(np.asarray(inp["L1_attn_D_w"]).T)
    w["Cb"] = np.ascontiguousarray(np.asarray(inp["L1_attn_C_b"]).reshape(1, D))
    w["Db"] = np.ascontiguousarray(np.asarray(inp["L1_attn_D_b"]).reshape(1, D))
    w["LD"] = np.ascontiguousarray(
        np.asarray(inp["L1_attn_log_delta"]).reshape(16, 128).T)
    w["Afm"] = np.ascontiguousarray(
        np.asarray(inp["L1_attn_A"]).reshape(16, 128, 64).transpose(1, 0, 2))
    w["Bfm"] = np.ascontiguousarray(
        np.asarray(inp["L1_attn_B"]).reshape(16, 128, 64).transpose(1, 0, 2))
    w["wstd"] = np.ascontiguousarray(
        np.asarray(inp["L0_ffn_w_std"], dtype=np.float32).reshape(1, 16))
    w["wrec"] = np.ascontiguousarray(
        np.asarray(inp["L0_ffn_w_rec"], dtype=np.float32).reshape(1, 16))
    w["gidxT"] = np.ascontiguousarray(np.asarray(inp["L1_ffn_gate_idx"]).T)
    w["uidxT"] = np.ascontiguousarray(np.asarray(inp["L1_ffn_up_idx"]).T)
    w["didxT"] = np.ascontiguousarray(np.asarray(inp["L1_ffn_down_idx"]).T)
    w["lut"] = np.ascontiguousarray(
        np.asarray(inp["L1_ffn_lut"], dtype=np.float32).reshape(1, 16))
    return w


ENGS = ("sync", "tensor", "vector", "scalar", "gpsimd")

class Sem:
    def __init__(self, nc, name):
        self.h = nc.alloc_semaphore(name)
        self.v = 0
    def inc(self, n=1):
        self.v += n
        return (self.h, n)
    def at(self):
        return (self.h, self.v)

_BLK_N = [0]

class Prog2:
    def __init__(self):
        self.ops = {e: [] for e in ENGS}
    def add(self, eng, fn, waits=(), incs=()):
        self.ops[eng].append((fn, tuple(waits), tuple(incs)))
    def emit(self, nc, name=None):
        if name is not None:
            _BLK_N[0] += 1
            name = f"{name}_{_BLK_N[0]}"
        with nc.Block(name) as block:
            for e in ENGS:
                oplist = self.ops[e]
                if not oplist:
                    continue
                def body(eng, oplist=oplist):
                    for fn, waits, incs in oplist:
                        for (sem, val) in waits:
                            if val > 0:
                                eng.wait_ge(sem, val)
                        ins = fn(eng) if fn is not None else None
                        for (sem, delta) in incs:
                            ins = ins.then_inc(sem, delta)
                getattr(block, e)(body)

class SbufPlan:
    def __init__(self, nc, base=18432, top=229248):
        self.nc = nc
        self.cur = base
        self.top = top
        self.n = 0
    def take(self, name, shape, dtype):
        szb = {F32: 4, BF16: 2, I32: 4}[dtype]
        per_part = int(np.prod(shape[1:])) * szb
        off = (self.cur + 31) & ~31
        assert off + per_part <= self.top, (
            f"SBUF overflow at {name}: {off}+{per_part} > {self.top}")
        self.cur = off + per_part
        self.n += 1
        return self.nc.alloc_sbuf_tensor_at(f"{name}_{self.n}", list(shape), dtype,
                                            offset=off)
    def mark(self):
        return self.cur
    def reset(self, mark):
        self.cur = mark

def dma(ap_out, ap_in, **kw):
    def fn(eng):
        return eng.dma_start(ap_out, ap_in, **kw)
    return fn

def mm(out, lhsT, rhs, start, stop, **kw):
    def fn(eng):
        return eng.matmul(out, lhsT, rhs, start=start, stop=stop,
                          skip_group_check=True, **kw)
    return fn

class KB:
    def __init__(self, nc):
        self.nc = nc
        self.plan = SbufPlan(nc)
        self.sems = {}
        self.pb = [nc.alloc_psum_tensor(f"pb{i}", [128, 512], F32) for i in range(8)]
        self.d = {}
        self.c = {}

    def sem(self, name):
        if name not in self.sems:
            self.sems[name] = Sem(self.nc, name)
        return self.sems[name]

    def dram_in(self, name, shape, dtype=F32):
        if name not in self.d:
            self.d[name] = self.nc.dram_tensor(name, list(shape), dtype,
                                               kind="ExternalInput").ap()
        return self.d[name]

    def dram_out(self, name, shape, dtype=F32):
        self.d[name] = self.nc.dram_tensor(name, list(shape), dtype,
                                           kind="ExternalOutput").ap()
        return self.d[name]

    def dram_tmp(self, name, shape, dtype=F32):
        if name not in self.d:
            self.d[name] = self.nc.dram_tensor(name, list(shape), dtype,
                                               kind="Internal").ap()
        return self.d[name]

    def stage_prep(self, want=("l0a", "l0f", "l1a", "l1f")):
        nc, pl, c = self.nc, self.plan, self.c
        p = Prog2()
        ld = self.sem("ld")
        v = self.sem("vv")
        s_ = self.sem("ss")
        m_ = self.sem("mm")
        for nm, dt in (("cCOS", BF16), ("cSIN", BF16), ("cR128T", BF16),
                       ("cID128", BF16), ("cMASKT", BF16), ("cONES128", BF16),
                       ("cMEANVEC", BF16), ("cONESROW", F32),
                       ("cONESROWB", BF16)):
            shape = list(CONSTS[nm].shape)
            c[nm] = pl.take(nm, shape, dt)
            dr = self.dram_in(nm, shape, dt)
            p.add("sync", dma(c[nm][:], dr), incs=[ld.inc(16)])
        c["r_row"] = pl.take("r_row", [1, T], BF16)
        c["negmu_row"] = pl.take("negmu_row", [1, T], BF16)
        c["rb"] = pl.take("rb", [128, T], F32)
        for i, val in enumerate((1e-5, 2.0)):
            t = pl.take(f"cstf{i}", [128, 1], F32)
            nc.const_aps.register(F32, val, t[:])
            p.add("gpsimd", lambda e, t=t, val=val: e.memset(t[:], val))
        if "l0f" in want:
            c["CSQ"] = pl.take("csq", [128, 16], F32)
            c["CSK"] = pl.take("csk", [128, 16], F32)
            c["BM"] = pl.take("bm", [128, 16], F32)
        if "l1a" in want:
            c["ADIAG"] = pl.take("adiag", [128, 16], F32)
            c["AB"] = pl.take("ab", [128, 16], F32)
        if "l1f" in want:
            c["LUTB"] = pl.take("lutb", [128, 16], F32)
        self.c["prepM"] = pl.mark()
        if "l0f" in want:
            for nm in ("cSTDMASK", "cTAILMASK"):
                c[nm] = pl.take(nm, [1, 128], F32)
                dr = self.dram_in(nm, [1, 128], F32)
                p.add("sync", dma(c[nm][:], dr), incs=[ld.inc(16)])
            wstd = self.dram_in("wstd", [1, 16], F32)
            wrec = self.dram_in("wrec", [1, 16], F32)
            srow = pl.take("srow", [1, 48], F32)
            p.add("sync", dma(srow[0:1, 0:16], wstd), incs=[ld.inc(16)])
            p.add("sync", dma(srow[0:1, 16:32], wrec), incs=[ld.inc(16)])
            p.add("sync", dma(srow[0:1, 32:48], wrec), incs=[ld.inc(16)])
            p.add("vector", lambda e: e.tensor_scalar_max(
                srow[0:1, 0:32], srow[0:1, 0:32], 1e-8),
                waits=[ld.at()], incs=[v.inc()])
            p.add("scalar", lambda e: e.activation(
                srow[0:1, 0:32], srow[0:1, 0:32], AF.Sqrt),
                waits=[v.at()], incs=[s_.inc()])
            p.add("vector", lambda e: e.tensor_single_scalar(
                srow[0:1, 32:48], srow[0:1, 32:48], 0.1, ALU.is_gt),
                waits=[ld.at()], incs=[v.inc()])
            p.add("tensor", mm(self.pb[0][:, 0:16], c["cSTDMASK"][:],
                               srow[0:1, 0:16], True, False),
                  waits=[s_.at(), v.at()])
            p.add("tensor", mm(self.pb[0][:, 0:16], c["cTAILMASK"][:],
                               srow[0:1, 16:32], False, True), incs=[m_.inc()])
            p.add("tensor", mm(self.pb[1][:, 0:16], c["cTAILMASK"][:],
                               srow[0:1, 32:48], True, True), incs=[m_.inc()])
            p.add("scalar", lambda e: e.copy(c["CSK"][:], self.pb[0][:, 0:16]),
                  waits=[m_.at()], incs=[s_.inc()])
            p.add("scalar", lambda e: e.copy(c["BM"][:], self.pb[1][:, 0:16]),
                  incs=[s_.inc()])
            p.add("vector", lambda e: e.tensor_scalar_mul(
                c["CSQ"][:], self.pb[0][:, 0:16], 1.0 / math.sqrt(128.0)),
                waits=[m_.at()], incs=[v.inc()])
        if "l1a" in want:
            ldr = self.dram_in("LD", [128, 16], F32)
            afm = self.dram_in("Afm", [128, 16, 64], F32)
            bfm = self.dram_in("Bfm", [128, 16, 64], F32)
            tmpA = pl.take("tmpA", [128, 16, 64], F32)
            tmpB = pl.take("tmpB", [128, 16, 64], F32)
            sp1 = pl.take("sp1", [128, 16], F32)
            sp2 = pl.take("sp2", [128, 16], F32)
            p.add("sync", dma(c["ADIAG"][:], ldr), incs=[ld.inc(16)])
            p.add("sync", dma(tmpA[:], afm), incs=[ld.inc(16)])
            p.add("sync", dma(tmpB[:], bfm), incs=[ld.inc(16)])
            p.add("scalar", lambda e: e.activation(sp1[:], c["ADIAG"][:], AF.Abs),
                  waits=[ld.at()])
            p.add("scalar", lambda e: e.activation(sp1[:], sp1[:], AF.Exp,
                                                   scale=-1.0))
            p.add("scalar", lambda e: e.activation(sp1[:], sp1[:], AF.Ln, bias=1.0))
            p.add("scalar", lambda e: e.activation(sp2[:], c["ADIAG"][:], AF.Relu),
                  incs=[s_.inc()])
            p.add("vector", lambda e: e.tensor_add(sp1[:], sp1[:], sp2[:]),
                  waits=[s_.at()], incs=[v.inc()])
            p.add("scalar", lambda e: e.activation(c["ADIAG"][:], sp1[:], AF.Exp,
                                                   scale=-1.0),
                  waits=[v.at()], incs=[s_.inc()])
            p.add("vector", lambda e: e.tensor_mul(tmpA[:], tmpA[:], tmpB[:]),
                  waits=[ld.at()])
            p.add("vector", lambda e: e.tensor_reduce(
                c["AB"][:], tmpA[:], AX.X, ALU.add), incs=[v.inc()])
        if "l1f" in want:
            lut = self.dram_in("lut", [1, 16], F32)
            lrow = pl.take("lrow", [1, 16], F32)
            p.add("sync", dma(lrow[:], lut), incs=[ld.inc(16)])
            p.add("tensor", mm(self.pb[2][:, 0:16], c["cONESROW"][:], lrow[:],
                               True, True), waits=[ld.at()], incs=[m_.inc()])
            p.add("scalar", lambda e: e.copy(c["LUTB"][:], self.pb[2][:, 0:16]),
                  waits=[m_.at()], incs=[s_.inc()])
        p.add("sync", None, waits=[ld.at()])
        p.emit(nc, "prep")
        if "prepM" in self.c:
            pl.reset(self.c.pop("prepM"))

    def stats_pass(self, p, xdr, xbf_full=None, pfx="st"):
        nc, pl, c = self.nc, self.plan, self.c
        mark = pl.mark()
        ld2 = [self.sem(pfx + "_ld0"), self.sem(pfx + "_ld1")]
        cast = self.sem(pfx + "_cast")
        sqd = self.sem(pfx + "_sq")
        pemm = self.sem(pfx + "_pe")
        fin = self.sem(pfx + "_fin")
        xf = pl.take(pfx + "xf", [128, 2, SLAB], F32)
        xb2 = None if xbf_full is not None else pl.take(pfx + "xb", [128, 2, SLAB], BF16)
        sq = pl.take(pfx + "sq", [128, 2, SLAB], BF16)
        rowbuf = pl.take(pfx + "row", [1, 2 * SLAB], F32)
        pbS, pbB = self.pb[6], self.pb[7]
        xdrv = xdr.rearrange("(c p) t -> p c t", p=128)
        cast0, sq0, pe0 = cast.v, sqd.v, pemm.v
        for s in range(NSLAB):
            for ch in range(NCHUNK):
                i = s * NCHUNK + ch
                b = i % 2
                p.add("sync", dma(xf[:, b], xdrv[:, ch, s * SLAB:(s + 1) * SLAB]),
                      waits=[(cast.h, cast0 + i - 1)], incs=[ld2[b].inc(16)])
                dst = (xbf_full[:, ch, s * SLAB:(s + 1) * SLAB]
                       if xbf_full is not None else xb2[:, b])
                p.add("vector",
                      lambda e, dst=dst, b=b: e.tensor_copy(dst, xf[:, b]),
                      waits=[ld2[b].at(), (pemm.h, pe0 + i - 1)], incs=[cast.inc()])
                p.add("scalar",
                      lambda e, dst=dst, b=b: e.activation(sq[:, b], dst, AF.Square),
                      waits=[cast.at(), (pemm.h, pe0 + i - 1)], incs=[sqd.inc()])
                p.add("tensor", mm(pbS[0:1, :], c["cMEANVEC"][:], dst,
                                   ch == 0, ch == 15), waits=[cast.at()])
                p.add("tensor", mm(pbS[32:33, :], c["cMEANVEC"][:], sq[:, b],
                                   ch == 0, ch == 15),
                      waits=[sqd.at()], incs=[pemm.inc()])
            sl = slice(s * SLAB, (s + 1) * SLAB)
            p.add("scalar", lambda e: e.copy(rowbuf[0:1, 0:SLAB], pbS[0:1, :]),
                  waits=[pemm.at()])
            p.add("scalar", lambda e: e.activation(rowbuf[0:1, SLAB:],
                                                   rowbuf[0:1, 0:SLAB], AF.Square),
                  incs=[fin.inc()])
            p.add("vector", lambda e: e.tensor_sub(rowbuf[0:1, SLAB:],
                                                   pbS[32:33, :], rowbuf[0:1, SLAB:]),
                  waits=[fin.at()], incs=[fin.inc()])
            p.add("vector", lambda e, sl=sl: e.tensor_scalar_mul(
                c["negmu_row"][0:1, sl], rowbuf[0:1, 0:SLAB], -1.0),
                incs=[fin.inc()])
            p.add("scalar", lambda e: e.activation(rowbuf[0:1, SLAB:],
                                                   rowbuf[0:1, SLAB:], AF.Ln,
                                                   bias=1e-5), waits=[fin.at()])
            p.add("scalar", lambda e, sl=sl: e.activation(
                c["r_row"][0:1, sl], rowbuf[0:1, SLAB:], AF.Exp, scale=-0.5),
                incs=[fin.inc()])
            p.add("scalar", lambda e, sl=sl: e.activation(
                c["invr_row"][0:1, sl], rowbuf[0:1, SLAB:], AF.Exp, scale=0.5),
                incs=[fin.inc()])
            p.add("tensor", mm(pbB[:, :], c["cONESROWB"][:], c["r_row"][0:1, sl],
                               True, True), waits=[fin.at()], incs=[pemm.inc()])
            p.add("scalar", lambda e, sl=sl: e.copy(c["rb"][:, sl], pbB[:, :]),
                  waits=[pemm.at()], incs=[fin.inc()])
            p.add("tensor", None, waits=[fin.at()])
        pl.reset(mark)

    def alloc_globals(self):
        self.c["xbf"] = self.plan.take("xbf", [128, NCHUNK, T], BF16)
        self.c["invr_row"] = self.plan.take("invr_row", [1, T], BF16)

    def finalize_ln(self, p, pbS, r_dst, negmu_dst, rowbuf, pe, fin,
                    invr_dst=None, rb_dst=None, pbB=None):
        c = self.c
        p.add("scalar", lambda e: e.copy(rowbuf[0:1, 0:SLAB], pbS[0:1, :]),
              waits=[pe.at()])
        p.add("scalar", lambda e: e.activation(rowbuf[0:1, SLAB:],
                                               rowbuf[0:1, 0:SLAB], AF.Square),
              incs=[fin.inc()])
        p.add("vector", lambda e: e.tensor_sub(rowbuf[0:1, SLAB:], pbS[32:33, :],
                                               rowbuf[0:1, SLAB:]),
              waits=[fin.at()], incs=[fin.inc()])
        p.add("vector", lambda e: e.tensor_scalar_mul(negmu_dst, rowbuf[0:1, 0:SLAB],
                                                      -1.0), incs=[fin.inc()])
        p.add("scalar", lambda e: e.activation(rowbuf[0:1, SLAB:], rowbuf[0:1, SLAB:],
                                               AF.Ln, bias=1e-5), waits=[fin.at()])
        p.add("scalar", lambda e: e.activation(r_dst, rowbuf[0:1, SLAB:], AF.Exp,
                                               scale=-0.5), incs=[fin.inc()])
        if invr_dst is not None:
            p.add("scalar", lambda e: e.activation(invr_dst, rowbuf[0:1, SLAB:],
                                                   AF.Exp, scale=0.5),
                  incs=[fin.inc()])
        if rb_dst is not None:
            p.add("tensor", mm(pbB[:, :], c["cONESROWB"][:], r_dst, True, True),
                  waits=[fin.at()], incs=[pe.inc()])
            p.add("scalar", lambda e: e.copy(rb_dst, pbB[:, :]),
                  waits=[pe.at()], incs=[fin.inc()])

    def load_w_tiles(self, p, src, col0, ncols, wf, wbf, b, ld, wcast, pe_done,
                     pe_need, kchunks=NCHUNK):
        v = src.rearrange("(kc kp) n -> kp kc n", kp=128)
        p.add("sync", dma(wf[:, 0:kchunks], v[:, :, col0:col0 + ncols]),
              waits=[wcast.at()], incs=[ld.inc(16)])
        p.add("vector", lambda e: e.tensor_copy(wbf[:, b, 0:kchunks], wf[:, 0:kchunks]),
              waits=[ld.at(), (pe_done.h, pe_need)], incs=[wcast.inc()])

    def stage_l0a(self, xdr, ydr, last=False):
        nc, pl, c = self.nc, self.plan, self.c
        qsp = self.dram_tmp("sp_a", [D, T], BF16)
        ysp = self.dram_tmp("sp_y", [D, T], BF16)
        wq = self.dram_in("Wq", [D, D])
        wk = self.dram_in("Wk", [D, 512])
        wv = self.dram_in("Wv", [D, 512])
        wo = self.dram_in("Wo", [D, D])
        xbf = c["xbf"]
        ao = c["xbf"]
        frame = pl.mark()
        kp = pl.take("kp", [128, 4, T], BF16)
        vtm = pl.take("vtm", [128, 16, 520], BF16)
        M1 = pl.mark()

        p = Prog2()
        w1row = pl.take("w1row", [1, 24 * 128], BF16)
        ld = self.sem("a_ld"); wcast = self.sem("a_wc"); pe1 = self.sem("a_pe1")
        w1ok = self.sem("a_w1"); gmm = self.sem("a_gmm"); ev = self.sem("a_ev")
        rot = self.sem("a_rot"); rp = self.sem("a_rp"); sp = self.sem("a_sp")
        vt = self.sem("a_vt"); vc = self.sem("a_vc")
        wf = pl.take("wfA", [128, NCHUNK, 128], F32)
        wbf = pl.take("wbA", [128, 2, NCHUNK, 128], BF16)
        qn = pl.take("qnA", [128, 2, SLAB], BF16)
        qo = pl.take("qoA", [128, 2, SLAB], BF16)
        rt = pl.take("rtA", [128, SLAB], BF16)
        vtm_h = vtm.rearrange("p k (h c) -> p k h c", c=65)
        p.add("gpsimd", lambda e: e.memset(vtm_h[:, :, :, 64:65], 1.0),
              incs=[vc.inc()])
        specs = [("q", wq, i, i) for i in range(16)] + \
                [("k", wk, j, 16 + j) for j in range(4)] + \
                [("v", wv, j, 20 + j) for j in range(4)]
        for kind, src, cc, idx in specs:
            b = idx % 2
            self.load_w_tiles(p, src, cc * 128, 128, wf, wbf, b, ld, wcast,
                              gmm, gmm.v - 4)
            for k in range(NCHUNK):
                p.add("tensor", mm(self.pb[5][0:1, 0:128], c["cONES128"][:],
                                   wbf[:, b, k, :], k == 0, k == 15),
                      waits=[wcast.at()] if k == 0 else [],
                      incs=[pe1.inc()] if k == 15 else [])
            p.add("scalar", lambda e, idx=idx: e.copy(
                w1row[0:1, idx * 128:(idx + 1) * 128], self.pb[5][0:1, 0:128]),
                waits=[pe1.at()], incs=[w1ok.inc()])
            for s in range(NSLAB):
                sl = slice(s * SLAB, (s + 1) * SLAB)
                bq = (idx * 4 + s) % 2
                pbp = self.pb[bq]
                for k in range(NCHUNK):
                    p.add("tensor", mm(pbp[:, :], wbf[:, b, k, :],
                                       xbf[:, k, sl], k == 0, False),
                          waits=([(ev.h, ev.v - 1)] if k == 0 else []))
                p.add("tensor", mm(pbp[:, :], w1row[0:1, idx * 128:(idx + 1) * 128],
                                   c["negmu_row"][0:1, sl], False, True),
                      waits=[w1ok.at()], incs=[gmm.inc()])
                if kind in ("q", "k"):
                    p.add("vector", lambda e, pbp=pbp, bq=bq, sl=sl: e.tensor_mul(
                        qn[:, bq], pbp[:, :], c["rb"][:, sl]),
                        waits=[gmm.at()], incs=[ev.inc()])
                    pbr = self.pb[2 + bq]
                    p.add("tensor", mm(pbr[:, :], c["cR128T"][:], qn[:, bq],
                                       True, True),
                          waits=[ev.at(), (rp.h, rp.v - 1)], incs=[rot.inc()])
                    p.add("vector", lambda e, bq=bq, sl=sl: e.tensor_mul(
                        qo[:, bq], qn[:, bq], c["cCOS"][:, sl]),
                        waits=[(sp.h, 16 * (sp.v // 16 - 1))])
                    p.add("vector", lambda e, pbr=pbr, sl=sl: e.tensor_mul(
                        rt[:, :], pbr[:, :], c["cSIN"][:, sl]),
                        waits=[rot.at()])
                    if kind == "q":
                        p.add("vector", lambda e, bq=bq: e.tensor_add(
                            qo[:, bq], qo[:, bq], rt[:, :]), incs=[rp.inc()])
                        p.add("gpsimd", lambda e, cc=cc, sl=sl, bq=bq: e.dma_start(
                            qsp[cc * 128:(cc + 1) * 128, sl], qo[:, bq]),
                            waits=[rp.at()], incs=[sp.inc(16)])
                    else:
                        p.add("vector", lambda e, cc=cc, sl=sl, bq=bq: e.tensor_add(
                            kp[:, cc, sl], qo[:, bq], rt[:, :]), incs=[rp.inc()])
                else:
                    p.add("vector", lambda e, pbp=pbp, bq=bq, sl=sl: e.tensor_mul(
                        qn[:, bq], pbp[:, :], c["rb"][:, sl]),
                        waits=[gmm.at()], incs=[ev.inc()])
                    pbt = self.pb[2 + bq]
                    for j in range(4):
                        p.add("tensor", mm(pbt[:, j * 128:(j + 1) * 128],
                                           qn[:, bq, j * 128:(j + 1) * 128],
                                           c["cID128"][:], True, True),
                              waits=([ev.at(), (vc.h, vc.v - 8), rp.at()]
                                     if j == 0 else []),
                              incs=[vt.inc()] if j == 3 else [])
                    for j in range(4):
                        kt = 4 * s + j
                        p.add("scalar", lambda e, pbt=pbt, j=j, kt=kt, cc=cc: e.copy(
                            vtm[:, kt, 130 * cc:130 * cc + 64],
                            pbt[:, j * 128:j * 128 + 64]),
                            waits=[vt.at()] if j == 0 else [])
                        p.add("scalar", lambda e, pbt=pbt, j=j, kt=kt, cc=cc: e.copy(
                            vtm[:, kt, 130 * cc + 65:130 * cc + 129],
                            pbt[:, j * 128 + 64:j * 128 + 128]),
                            incs=[vc.inc()])
        p.add("gpsimd", None, waits=[sp.at()])
        if getattr(self, "dbg_l0a", False):
            dq = self.sem("dbg_q")
            oq = self.dram_out("o_qsp", [D, T], BF16)
            p.add("gpsimd", dma(oq, qsp[:, :]), waits=[rp.at(), vc.at()],
                  incs=[dq.inc(16)])
            okp = self.dram_out("o_kp", [128, 4, T], BF16)
            p.add("gpsimd", dma(okp, kp[:]), incs=[dq.inc(16)])
            ovt = self.dram_out("o_vtm", [128, 16, 520], BF16)
            p.add("gpsimd", dma(ovt, vtm[:]), incs=[dq.inc(16)])
            orr = self.dram_out("o_rr", [1, T], BF16)
            p.add("gpsimd", dma(orr, c["r_row"][:]), incs=[dq.inc(16)])
            onm = self.dram_out("o_nm", [1, T], BF16)
            p.add("gpsimd", dma(onm, c["negmu_row"][:]), incs=[dq.inc(16)])
            orb = self.dram_out("o_rb", [128, T], F32)
            p.add("gpsimd", dma(orb, c["rb"][:]), incs=[dq.inc(16)])
            oxb = self.dram_out("o_xb", [128, 16, T], BF16)
            p.add("gpsimd", dma(oxb, c["xbf"][:]), incs=[dq.inc(16)])
            p.add("gpsimd", None, waits=[dq.at()])
        p.emit(nc, "l0a_A")
        pl.reset(M1)

        p = Prog2()
        qld = self.sem("b_qld")
        stm = [self.sem(f"b_st{i}") for i in range(4)]
        exs = [self.sem(f"b_ex{i}") for i in range(4)]
        pvs = [self.sem(f"b_pv{i}") for i in range(2)]
        aoe = [self.sem(f"b_ao{i}") for i in range(2)]
        dn = self.sem("b_dn"); rc = self.sem("b_rc")
        bc = [self.sem(f"b_bc{i}") for i in range(2)]
        qsl = pl.take("qslB", [128, NCHUNK, SLAB], BF16)
        pt = pl.take("ptB", [128, 2, 2, SLAB], BF16)
        den = pl.take("denB", [1, 2, SLAB], F32)
        tnn = pl.take("tnB", [1, 2, SLAB], F32)
        rcb = pl.take("rcbB", [128, 2, SLAB], F32)
        qv = qsp.rearrange("(cc p) t -> p cc t", p=128)
        for s in range(NSLAB):
            sl = slice(s * SLAB, (s + 1) * SLAB)
            nkc = 4 * (s + 1)
            p.add("sync", dma(qsl[:], qv[:, :, sl]),
                  waits=[x.at() for x in stm], incs=[qld.inc(16)])
            for cc in range(16):
                j = cc // 4
                for h2, (rl, rh) in enumerate(((0, 64), (64, 128))):
                    kh = j + 4 * h2
                    pba = self.pb[4 + h2]
                    for kc in range(nkc):
                        bank = 2 * h2 + kc % 2
                        pbst = self.pb[bank]
                        diag = kc >= 4 * s
                        w = [qld.at(), exs[bank].at()]
                        p.add("tensor", mm(pbst[:, :],
                                           kp[rl:rh, j, kc * 128:(kc + 1) * 128],
                                           qsl[rl:rh, cc, :], True, not diag),
                              waits=w,
                              incs=[] if diag else [stm[bank].inc()])
                        if diag:
                            p.add("tensor", mm(pbst[:, :], c["cID128"][:],
                                               c["cMASKT"][:, kc - 4 * s, :],
                                               False, True),
                                  incs=[stm[bank].inc()])
                        p.add("scalar",
                              lambda e, pbst=pbst, h2=h2, kc=kc: e.activation(
                                  pt[:, h2, kc % 2, :], pbst[:, :], AF.Exp,
                                  scale=0.125),
                              waits=[stm[bank].at(), (pvs[h2].h, pvs[h2].v - 1)],
                              incs=[exs[bank].inc()])
                        pvw = [exs[bank].at()]
                        if kc == 0:
                            pvw.append(aoe[h2].at())
                        p.add("tensor", mm(pba[0:65, :],
                                           vtm[:, kc, 65 * kh:65 * kh + 65],
                                           pt[:, h2, kc % 2, :], kc == 0,
                                           kc == nkc - 1),
                              waits=pvw, incs=[pvs[h2].inc()])
                    bmm = self.sem(f"b_bm{h2}")
                    p.add("scalar", lambda e, pba=pba, h2=h2: e.activation(
                        den[0:1, h2, :], pba[64:65, :], AF.Ln),
                        waits=[pvs[h2].at(), bmm.at()])
                    p.add("scalar", lambda e, h2=h2: e.activation(
                        den[0:1, h2, :], den[0:1, h2, :], AF.Exp, scale=-1.0),
                        incs=[dn.inc()])
                    p.add("vector", lambda e, pba=pba, h2=h2: e.tensor_mul(
                        tnn[0:1, h2, :], pba[64:65, :], den[0:1, h2, :]),
                        waits=[dn.at()])
                    p.add("vector", lambda e, h2=h2: e.tensor_scalar(
                        tnn[0:1, h2, :], tnn[0:1, h2, :], -1.0, 2.0, ALU.mult,
                        ALU.add))
                    p.add("vector", lambda e, h2=h2: e.tensor_mul(
                        den[0:1, h2, :], den[0:1, h2, :], tnn[0:1, h2, :]),
                        incs=[rc.inc()])
                    pbb = self.pb[6 + h2]
                    p.add("tensor", mm(pbb[:, :], c["cONESROW"][:], den[0:1, h2, :],
                                       True, True), waits=[rc.at(), bc[h2].at()],
                          incs=[bmm.inc()])
                    p.add("scalar", lambda e, pbb=pbb, h2=h2: e.copy(
                        rcb[:, h2, :], pbb[:, :]),
                        waits=[bmm.at(), aoe[h2].at()], incs=[bc[h2].inc()])
                    p.add("vector",
                          lambda e, pba=pba, h2=h2, cc=cc, sl=sl: e.tensor_mul(
                              ao[64 * h2:64 * (h2 + 1), cc, sl], pba[0:64, :],
                              rcb[0:64, h2, :]),
                          waits=[bc[h2].at()], incs=[aoe[h2].inc()])
        if getattr(self, "dbg_l0a", False):
            dq = self.sem("dbg_q")
            oao = self.dram_out("o_ao", [128, 16, T], BF16)
            p.add("gpsimd", dma(oao, ao[:]), waits=[aoe[0].at(), aoe[1].at()],
                  incs=[dq.inc(16)])
            p.add("gpsimd", None, waits=[dq.at()])
        p.emit(nc, "l0a_B")
        pl.reset(M1)

        self.proj_spill_phase(wo, ao, ysp, name="l0a_C")
        pl.reset(M1)
        self.residual_phase(ysp, xdr, ydr, last=last)
        pl.reset(frame)

    def stage_l0f(self, xdr, ydr, last=False):
        nc, pl, c = self.nc, self.plan, self.c
        qsp = self.dram_tmp("sp_a", [D, T], BF16)
        ksp = self.dram_tmp("sp_b", [D, T], BF16)
        vsp = self.dram_tmp("sp_c", [D, T], BF16)
        ysp = self.dram_tmp("sp_y", [D, T], BF16)
        waqk = self.dram_in("WaQK", [D, 4096])
        wav = self.dram_in("WaV", [D, D])
        wp = self.dram_in("Wp", [D, D])
        xbf = c["xbf"]
        ao = c["xbf"]
        frame = pl.mark()
        M1 = pl.mark()

        p = Prog2()
        w1row = pl.take("w1rF", [1, 48 * 128], BF16)
        ld = self.sem("a_ld"); wcast = self.sem("a_wc"); pe1 = self.sem("a_pe1")
        w1ok = self.sem("a_w1"); gmm = self.sem("a_gmm"); ev = self.sem("a_ev")
        bl = self.sem("f_bl"); sp = self.sem("a_sp")
        vt = self.sem("a_vt"); vc = self.sem("a_vc"); vs = self.sem("f_vs")
        wf = pl.take("wfF", [128, NCHUNK, 128], F32)
        wbf = pl.take("wbF", [128, 2, NCHUNK, 128], BF16)
        qtmp = pl.take("qtF", [128, NSLAB, SLAB], BF16)
        ktmp = pl.take("ktF", [128, 2, SLAB], BF16)
        dtmp = pl.take("dtF", [128, SLAB], BF16)
        qf = pl.take("qfF", [128, 2, SLAB], BF16)
        kf = pl.take("kfF", [128, 2, SLAB], BF16)
        vn = pl.take("vnF", [128, 2, SLAB], BF16)
        vsta = pl.take("vstF", [128, 2, 4, 128], BF16)
        vspv = vsp.rearrange("(kc p) d -> p kc d", p=128)
        for h in range(16):
            for qk in range(2):
                idx = 2 * h + qk
                b = idx % 2
                self.load_w_tiles(p, waqk, idx * 128, 128, wf, wbf, b, ld, wcast,
                                  gmm, gmm.v - 4)
                for k in range(NCHUNK):
                    p.add("tensor", mm(self.pb[5][0:1, 0:128], c["cONES128"][:],
                                       wbf[:, b, k, :], k == 0, k == 15),
                          waits=[wcast.at()] if k == 0 else [],
                          incs=[pe1.inc()] if k == 15 else [])
                p.add("scalar", lambda e, idx=idx: e.copy(
                    w1row[0:1, idx * 128:(idx + 1) * 128], self.pb[5][0:1, 0:128]),
                    waits=[pe1.at()], incs=[w1ok.inc()])
                for s in range(NSLAB):
                    sl = slice(s * SLAB, (s + 1) * SLAB)
                    bq = (idx * 4 + s) % 2
                    pbp = self.pb[bq]
                    for k in range(NCHUNK):
                        p.add("tensor", mm(pbp[:, :], wbf[:, b, k, :],
                                           xbf[:, k, sl], k == 0, False),
                              waits=([(ev.h, ev.v - 1)] if k == 0 else []))
                    p.add("tensor", mm(pbp[:, :],
                                       w1row[0:1, idx * 128:(idx + 1) * 128],
                                       c["negmu_row"][0:1, sl], False, True),
                          waits=[w1ok.at()], incs=[gmm.inc()])
                    if qk == 0:
                        p.add("vector",
                              lambda e, pbp=pbp, s=s, h=h, sl=sl: e.scalar_tensor_tensor(
                                  qtmp[:, s, :], pbp[:, :], c["CSK"][:, h:h + 1],
                                  c["rb"][:, sl], ALU.mult, ALU.mult),
                              waits=[gmm.at(), bl.at()], incs=[ev.inc()])
                    else:
                        bk = s % 2
                        p.add("vector",
                              lambda e, pbp=pbp, bk=bk, h=h, sl=sl: e.scalar_tensor_tensor(
                                  ktmp[:, bk, :], pbp[:, :], c["CSK"][:, h:h + 1],
                                  c["rb"][:, sl], ALU.mult, ALU.mult),
                              waits=[gmm.at(), (sp.h, 16 * (sp.v // 16 - 2))],
                              incs=[ev.inc()])
                        p.add("vector", lambda e, bk=bk, s=s: e.tensor_sub(
                            dtmp[:, :], ktmp[:, bk, :], qtmp[:, s, :]))
                        p.add("vector", lambda e, h=h: e.tensor_scalar(
                            dtmp[:, :], dtmp[:, :], c["BM"][:, h:h + 1], None,
                            ALU.mult))
                        p.add("vector", lambda e, bk=bk, s=s: e.tensor_add(
                            qf[:, bk, :], qtmp[:, s, :], dtmp[:, :]))
                        p.add("vector", lambda e, bk=bk: e.tensor_sub(
                            kf[:, bk, :], ktmp[:, bk, :], dtmp[:, :]),
                            incs=[bl.inc()])
                        p.add("gpsimd", lambda e, h=h, sl=sl, bk=bk: e.dma_start(
                            qsp[h * 128:(h + 1) * 128, sl], qf[:, bk, :]),
                            waits=[bl.at()], incs=[sp.inc(16)])
                        p.add("gpsimd", lambda e, h=h, sl=sl, bk=bk: e.dma_start(
                            ksp[h * 128:(h + 1) * 128, sl], kf[:, bk, :]),
                            incs=[sp.inc(16)])
        for cc in range(16):
            idx = 32 + cc
            b = idx % 2
            self.load_w_tiles(p, wav, cc * 128, 128, wf, wbf, b, ld, wcast,
                              gmm, gmm.v - 4)
            for k in range(NCHUNK):
                p.add("tensor", mm(self.pb[5][0:1, 0:128], c["cONES128"][:],
                                   wbf[:, b, k, :], k == 0, k == 15),
                      waits=[wcast.at()] if k == 0 else [],
                      incs=[pe1.inc()] if k == 15 else [])
            p.add("scalar", lambda e, idx=idx: e.copy(
                w1row[0:1, idx * 128:(idx + 1) * 128], self.pb[5][0:1, 0:128]),
                waits=[pe1.at()], incs=[w1ok.inc()])
            for s in range(NSLAB):
                sl = slice(s * SLAB, (s + 1) * SLAB)
                bq = (idx * 4 + s) % 2
                pbp = self.pb[bq]
                for k in range(NCHUNK):
                    p.add("tensor", mm(pbp[:, :], wbf[:, b, k, :],
                                       xbf[:, k, sl], k == 0, False),
                          waits=([(ev.h, ev.v - 1)] if k == 0 else []))
                p.add("tensor", mm(pbp[:, :], w1row[0:1, idx * 128:(idx + 1) * 128],
                                   c["negmu_row"][0:1, sl], False, True),
                      waits=[w1ok.at()], incs=[gmm.inc()])
                p.add("vector", lambda e, pbp=pbp, bq=bq, sl=sl: e.tensor_mul(
                    vn[:, bq], pbp[:, :], c["rb"][:, sl]),
                    waits=[gmm.at()], incs=[ev.inc()])
                pbt = self.pb[2 + bq]
                for j in range(4):
                    p.add("tensor", mm(pbt[:, j * 128:(j + 1) * 128],
                                       vn[:, bq, j * 128:(j + 1) * 128],
                                       c["cID128"][:], True, True),
                          waits=([ev.at(), (vc.h, vc.v - 4)] if j == 0 else []),
                          incs=[vt.inc()] if j == 3 else [])
                for j in range(4):
                    p.add("scalar", lambda e, pbt=pbt, j=j, bq=bq: e.copy(
                        vsta[:, bq, j, :], pbt[:, j * 128:(j + 1) * 128]),
                        waits=([vt.at(), (vs.h, 16 * (vs.v // 16 - 1))]
                               if j == 0 else []),
                        incs=[vc.inc()])
                p.add("gpsimd", lambda e, s=s, cc=cc, bq=bq: e.dma_start(
                    vspv[:, 4 * s:4 * s + 4, cc * 128:(cc + 1) * 128],
                    vsta[:, bq, :, :]), waits=[vc.at()], incs=[vs.inc(16)])
        p.add("gpsimd", None, waits=[sp.at(), vs.at()])
        p.emit(nc, "l0f_A")
        pl.reset(M1)

        p = Prog2()
        qld = self.sem("g_qld"); kld = self.sem("g_kld")
        vld2 = [self.sem("g_vld0"), self.sem("g_vld1")]
        stm = [self.sem(f"g_st{i}") for i in range(2)]
        exs = [self.sem(f"g_ex{i}") for i in range(2)]
        pvs = self.sem("g_pv"); ape = self.sem("g_ape")
        dnm = self.sem("g_dnm"); dne = self.sem("g_dne")
        rc = self.sem("g_rc"); bmm = self.sem("g_bmm"); bc = self.sem("g_bc")
        aoe = self.sem("g_aoe")
        qsl = pl.take("qslG", [128, NCHUNK, SLAB], BF16)
        kfsl = pl.take("kslG", [128, NCHUNK, SLAB], BF16)
        vsl = pl.take("vslG", [128, 2, 4, D], BF16)
        aopart = pl.take("aopG", [128, 8, SLAB], F32)
        denpart = pl.take("denG", [1, 8 * SLAB], F32)
        pt = pl.take("ptG", [128, 2, SLAB], BF16)
        dnr = pl.take("dnrG", [1, SLAB], F32)
        tnn = pl.take("tnG", [1, SLAB], F32)
        rcb = pl.take("rcbG", [128, SLAB], F32)
        qv = qsp.rearrange("(cc p) t -> p cc t", p=128)
        kv_ = ksp.rearrange("(cc p) t -> p cc t", p=128)
        vv_ = vsp.rearrange("(kc p) d -> p kc d", p=128)
        sc128 = 1.0 / math.sqrt(128.0)
        vsl_last = [0, 0]
        nblk = 0
        for s in range(NSLAB):
            sl = slice(s * SLAB, (s + 1) * SLAB)
            p.add("sync", dma(qsl[:], qv[:, :, sl]),
                  waits=[x.at() for x in stm], incs=[qld.inc(16)])
            for hg in range(2):
                for kk in range(s + 1):
                    bv = nblk % 2
                    nblk += 1
                    p.add("sync", dma(kfsl[:], kv_[:, :, kk * SLAB:(kk + 1) * SLAB]),
                          waits=[x.at() for x in stm], incs=[kld.inc(16)])
                    p.add("sync", dma(vsl[:, bv], vv_[:, 4 * kk:4 * kk + 4, :]),
                          waits=[(pvs.h, vsl_last[bv])], incs=[vld2[bv].inc(16)])
                    vsl_last[bv] = pvs.v + 32
                    for hh in range(8):
                        h = 8 * hg + hh
                        pba = self.pb[4 + hh % 2]
                        dsl = slice(hh * SLAB, (hh + 1) * SLAB)
                        for kcl in range(4):
                            bank = (hh * 4 + kcl) % 2
                            pbst = self.pb[bank]
                            diag = kk == s
                            p.add("tensor", mm(pbst[:, :],
                                               kfsl[:, h, kcl * 128:(kcl + 1) * 128],
                                               qsl[:, h, :], True, not diag),
                                  waits=[qld.at(), kld.at(), exs[bank].at()],
                                  incs=[] if diag else [stm[bank].inc()])
                            if diag:
                                p.add("tensor", mm(pbst[:, :], c["cID128"][:],
                                                   c["cMASKT"][:, kcl, :],
                                                   False, True),
                                      incs=[stm[bank].inc()])
                            p.add("scalar",
                                  lambda e, pbst=pbst, bank=bank: e.activation(
                                      pt[:, bank, :], pbst[:, :], AF.Exp,
                                      scale=sc128),
                                  waits=[stm[bank].at(), (pvs.h, pvs.v - 1)],
                                  incs=[exs[bank].inc()])
                            p.add("tensor", mm(self.pb[6][0:1, :], c["cONES128"][:],
                                               pt[:, bank, :], kcl == 0, kcl == 3),
                                  waits=[exs[bank].at()] +
                                        ([ape.at()] if kcl == 0 else []))
                            p.add("tensor", mm(pba[:, :],
                                               vsl[:, bv, kcl, h * 128:(h + 1) * 128],
                                               pt[:, bank, :], kcl == 0, kcl == 3),
                                  waits=[vld2[bv].at()], incs=[pvs.inc()])
                        if kk == 0:
                            p.add("scalar", lambda e, pba=pba, hh=hh: e.copy(
                                aopart[:, hh, :], pba[:, :]),
                                waits=[pvs.at(), aoe.at()], incs=[ape.inc()])
                            p.add("scalar", lambda e, dsl=dsl: e.copy(
                                denpart[0:1, dsl], self.pb[6][0:1, :]),
                                incs=[ape.inc()])
                        else:
                            p.add("vector", lambda e, pba=pba, hh=hh: e.tensor_add(
                                aopart[:, hh, :], aopart[:, hh, :], pba[:, :]),
                                waits=[pvs.at()], incs=[ape.inc()])
                            p.add("vector", lambda e, dsl=dsl: e.tensor_add(
                                denpart[0:1, dsl], denpart[0:1, dsl],
                                self.pb[6][0:1, :]),
                                waits=[pvs.at()], incs=[ape.inc()])
                for hh in range(8):
                    h = 8 * hg + hh
                    dsl = slice(hh * SLAB, (hh + 1) * SLAB)
                    p.add("scalar", lambda e, dsl=dsl: e.activation(
                        dnr[0:1, :], denpart[0:1, dsl], AF.Ln),
                        waits=[ape.at(), bmm.at()])
                    p.add("scalar", lambda e: e.activation(
                        dnr[0:1, :], dnr[0:1, :], AF.Exp, scale=-1.0),
                        incs=[dne.inc()])
                    p.add("vector", lambda e, dsl=dsl: e.tensor_mul(
                        tnn[0:1, :], denpart[0:1, dsl], dnr[0:1, :]),
                        waits=[dne.at()])
                    p.add("vector", lambda e: e.tensor_scalar(
                        tnn[0:1, :], tnn[0:1, :], -1.0, 2.0, ALU.mult, ALU.add))
                    p.add("vector", lambda e: e.tensor_mul(
                        dnr[0:1, :], dnr[0:1, :], tnn[0:1, :]), incs=[rc.inc()])
                    p.add("tensor", mm(self.pb[7][:, :], c["cONESROW"][:],
                                       dnr[0:1, :], True, True),
                          waits=[rc.at(), bc.at()], incs=[bmm.inc()])
                    p.add("scalar", lambda e: e.copy(rcb[:, :], self.pb[7][:, :]),
                          waits=[bmm.at(), aoe.at()], incs=[bc.inc()])
                    p.add("vector", lambda e, h=h, hh=hh, sl=sl: e.tensor_mul(
                        ao[:, h, sl], aopart[:, hh, :], rcb[:, :]),
                        waits=[bc.at()], incs=[aoe.inc()])
        p.emit(nc, "l0f_B")
        pl.reset(M1)

        self.proj_spill_phase(wp, ao, ysp)
        pl.reset(M1)
        self.residual_phase(ysp, xdr, ydr, last=last)
        pl.reset(frame)

    def proj_spill_phase(self, wdr, ao, ysp, name="projC"):
        nc, pl, c = self.nc, self.plan, self.c
        p = Prog2()
        M = pl.mark()
        ld = self.sem("c_ld"); wcast = self.sem("c_wc"); gmm = self.sem("c_gmm")
        yev = self.sem("c_yev"); csp = self.sem("c_sp")
        wf = pl.take("wfC", [128, NCHUNK, 128], F32)
        wbf = pl.take("wbC", [128, 2, NCHUNK, 128], BF16)
        yb = pl.take("ybC", [128, 2, SLAB], BF16)
        for c2 in range(16):
            b = c2 % 2
            self.load_w_tiles(p, wdr, c2 * 128, 128, wf, wbf, b, ld, wcast,
                              gmm, gmm.v - 4)
            for s in range(NSLAB):
                sl = slice(s * SLAB, (s + 1) * SLAB)
                bq = (c2 * 4 + s) % 2
                pbp = self.pb[bq]
                for k in range(NCHUNK):
                    p.add("tensor", mm(pbp[:, :], wbf[:, b, k, :], ao[:, k, sl],
                                       k == 0, k == 15),
                          waits=([wcast.at(), (yev.h, yev.v - 1)] if k == 0 else []),
                          incs=[gmm.inc()] if k == 15 else [])
                p.add("scalar", lambda e, pbp=pbp, bq=bq: e.copy(yb[:, bq], pbp[:, :]),
                      waits=[gmm.at(), (csp.h, csp.v - 16)], incs=[yev.inc()])
                p.add("gpsimd", lambda e, c2=c2, sl=sl, bq=bq: e.dma_start(
                    ysp[c2 * 128:(c2 + 1) * 128, sl], yb[:, bq]),
                    waits=[yev.at()], incs=[csp.inc(16)])
        p.add("gpsimd", None, waits=[csp.at()])
        p.emit(nc, name)
        pl.reset(M)

    def stage_l1a(self, xdr, ydr, last=False):
        nc, pl, c = self.nc, self.plan, self.c
        ysp = self.dram_tmp("sp_y", [D, T], BF16)
        cwt = self.dram_in("CwT", [D, D])
        dwt = self.dram_in("DwT", [D, D])
        cb = self.dram_in("Cb", [1, D])
        db = self.dram_in("Db", [1, D])
        xbf = c["xbf"]
        frame = pl.mark()
        p = Prog2()
        ld = self.sem("a_ld"); wcast = self.sem("a_wc"); w1ok = self.sem("a_w1")
        pe1 = self.sem("a_pe1"); gmm = self.sem("a_gmm"); ev = self.sem("a_ev")
        xpd = self.sem("h_xp"); sc_ = self.sem("h_sc"); sp = self.sem("a_sp")
        sg = self.sem("h_sg")
        w1row = pl.take("w1rH", [1, 32 * 128], BF16)
        cbrow = pl.take("cbH", [1, D], BF16)
        dbrow = pl.take("dbH", [1, D], BF16)
        cbf = pl.take("cbfH", [1, D], F32)
        wf = pl.take("wfH", [128, NCHUNK, 128], F32)
        wc_ = pl.take("wcH", [128, 2, NCHUNK, 128], BF16)
        wd_ = pl.take("wdH", [128, 2, NCHUNK, 128], BF16)
        uf = pl.take("ufH", [128, SLAB], F32)
        gs = pl.take("gsH", [128, SLAB], F32)
        xp = pl.take("xpH", [128, T], F32)
        ysc = pl.take("yscH", [128, T], F32)
        ybo = pl.take("yboH", [128, 2, T], BF16)
        cbl = self.sem("h_cbl")
        p.add("sync", dma(cbf[:], cb), incs=[cbl.inc(16)])
        p.add("vector", lambda e: e.tensor_copy(cbrow[:], cbf[:]),
              waits=[cbl.at()], incs=[ev.inc()])
        p.add("sync", dma(cbf[:], db), waits=[ev.at()], incs=[cbl.inc(16)])
        p.add("vector", lambda e: e.tensor_copy(dbrow[:], cbf[:]),
              waits=[cbl.at()], incs=[ev.inc()])
        bias_rdy = (ev.h, ev.v)
        for cc in range(NCHUNK):
            for wi, (src, dst) in enumerate(((cwt, wc_), (dwt, wd_))):
                idx = 2 * cc + wi
                b = cc % 2
                v = src.rearrange("(kc kp) n -> kp kc n", kp=128)
                p.add("sync", dma(wf[:], v[:, :, cc * 128:(cc + 1) * 128]),
                      waits=[wcast.at()], incs=[ld.inc(16)])
                p.add("vector", lambda e, dst=dst, b=b: e.tensor_copy(
                    dst[:, b], wf[:]),
                    waits=[ld.at(), (gmm.h, gmm.v - 8)], incs=[wcast.inc()])
                for k in range(NCHUNK):
                    p.add("tensor", mm(self.pb[5][0:1, 0:128], c["cONES128"][:],
                                       dst[:, b, k, :], k == 0, k == 15),
                          waits=[wcast.at()] if k == 0 else [],
                          incs=[pe1.inc()] if k == 15 else [])
                p.add("scalar", lambda e, idx=idx: e.copy(
                    w1row[0:1, idx * 128:(idx + 1) * 128], self.pb[5][0:1, 0:128]),
                    waits=[pe1.at()], incs=[w1ok.inc()])
            b = cc % 2
            b2 = cc % 2
            for s in range(NSLAB):
                sl = slice(s * SLAB, (s + 1) * SLAB)
                pbu = self.pb[s % 2]
                pbg = self.pb[2 + s % 2]
                for k in range(NCHUNK):
                    p.add("tensor", mm(pbu[:, :], wc_[:, b, k, :], xbf[:, k, sl],
                                       k == 0, False),
                          waits=([(ev.h, ev.v - 3)] if k == 0 else []))
                p.add("tensor", mm(pbu[:, :], w1row[0:1, 2 * cc * 128:
                                                    (2 * cc + 1) * 128],
                                   c["negmu_row"][0:1, sl], False, False),
                      waits=[w1ok.at()])
                p.add("tensor", mm(pbu[:, :], cbrow[0:1, cc * 128:(cc + 1) * 128],
                                   c["invr_row"][0:1, sl], False, True),
                      waits=[bias_rdy], incs=[gmm.inc()])
                for k in range(NCHUNK):
                    p.add("tensor", mm(pbg[:, :], wd_[:, b2, k, :], xbf[:, k, sl],
                                       k == 0, False),
                          waits=([(sg.h, sg.v - 1)] if k == 0 else []))
                p.add("tensor", mm(pbg[:, :], w1row[0:1, (2 * cc + 1) * 128:
                                                    (2 * cc + 2) * 128],
                                   c["negmu_row"][0:1, sl], False, False),
                      waits=[w1ok.at()])
                p.add("tensor", mm(pbg[:, :], dbrow[0:1, cc * 128:(cc + 1) * 128],
                                   c["invr_row"][0:1, sl], False, True),
                      waits=[bias_rdy], incs=[gmm.inc()])
                p.add("vector", lambda e, pbu=pbu, sl=sl: e.tensor_mul(
                    uf[:, :], pbu[:, :], c["rb"][:, sl]),
                    waits=[gmm.at(), xpd.at()], incs=[ev.inc()])
                p.add("vector", lambda e, pbg=pbg, sl=sl: e.tensor_mul(
                    gs[:, :], pbg[:, :], c["rb"][:, sl]),
                    waits=[sg.at()], incs=[ev.inc()])
                p.add("scalar", lambda e: e.activation(gs[:, :], gs[:, :],
                                                       AF.Sigmoid),
                      waits=[ev.at()], incs=[sg.inc()])
                p.add("vector", lambda e, sl=sl: e.tensor_mul(
                    xp[:, sl], uf[:, :], gs[:, :]),
                    waits=[sg.at()], incs=[xpd.inc()])
            p.add("vector", lambda e, cc=cc: e.tensor_tensor_scan(
                ysc[:, :], c["ADIAG"][:, cc:cc + 1].broadcast_to([128, T]),
                xp[:, :], 0.0, ALU.mult, ALU.add),
                waits=[xpd.at(), (sc_.h, sc_.v - 1)], incs=[sc_.inc()])
            bo = cc % 2
            p.add("vector", lambda e, cc=cc, bo=bo: e.tensor_scalar(
                ybo[:, bo, :], ysc[:, :], c["AB"][:, cc:cc + 1], None, ALU.mult),
                waits=[(sp.h, 16 * (sp.v // 16 - 1))], incs=[sc_.inc()])
            p.add("gpsimd", lambda e, cc=cc, bo=bo: e.dma_start(
                ysp[cc * 128:(cc + 1) * 128, :], ybo[:, bo, :]),
                waits=[sc_.at()], incs=[sp.inc(16)])
        p.add("gpsimd", None, waits=[sp.at()])
        p.emit(nc, "l1a_A")
        pl.reset(frame)
        self.residual_phase(ysp, xdr, ydr, last=last)
        pl.reset(frame)

    def stage_ghost(self):
        nc, pl, c = self.nc, self.plan, self.c
        gi = self.dram_in("gidxT", [D, INTER], I32)
        ui = self.dram_in("uidxT", [D, INTER], I32)
        di = self.dram_in("didxT", [INTER, D], I32)
        gw = self.dram_tmp("gw", [D, INTER], BF16)
        uw = self.dram_tmp("uw", [D, INTER], BF16)
        dw = self.dram_tmp("dw", [INTER, D], BF16)
        w1g = self.dram_tmp("w1g", [1, INTER], F32)
        w1u = self.dram_tmp("w1u", [1, INTER], F32)
        frame = pl.mark()
        p = Prog2()
        ld = self.sem("gh_ld"); cst = self.sem("gh_cs"); acc = self.sem("gh_ac")
        pe = self.sem("gh_pe"); ws = self.sem("gh_ws"); sp = self.sem("gh_sp")
        idxt = pl.take("idxt", [128, INTER], I32)
        idxf = pl.take("idxf", [128, 2, INTER], BF16)
        accb = pl.take("accb", [128, 2, INTER], BF16)
        tb = pl.take("tb", [128, 2, INTER], BF16)
        w1acc = pl.take("w1acc", [1, INTER], F32)
        ti = 0
        for src, dst, w1d, nr, ncols in ((gi, gw, w1g, 16, INTER),
                                         (ui, uw, w1u, 16, INTER),
                                         (di, dw, None, ICHUNK, D)):
            for rc in range(nr):
                b = ti % 2
                eng = "vector" if ti % 3 != 2 else "gpsimd"
                p.add("sync", dma(idxt[:, 0:ncols],
                                  src[rc * 128:(rc + 1) * 128, :]),
                      waits=[cst.at()], incs=[ld.inc(16)])
                p.add(eng, lambda e, b=b, ncols=ncols: e.tensor_copy(
                    idxf[:, b, 0:ncols], idxt[:, 0:ncols]),
                    waits=[ld.at(), (acc.h, acc.v - 1)], incs=[cst.inc()])
                for j in range(16):
                    if j == 0:
                        p.add(eng, lambda e, b=b, ncols=ncols, j=j: e.tensor_scalar(
                            accb[:, b, 0:ncols], idxf[:, b, 0:ncols], float(j),
                            c["LUTB"][:, j:j + 1], ALU.is_equal, ALU.mult),
                            waits=[(sp.h, 16 * (sp.v // 16 - 1)),
                                   (pe.h, pe.v - 11)])
                    else:
                        p.add(eng, lambda e, b=b, ncols=ncols, j=j: e.tensor_scalar(
                            tb[:, b, 0:ncols], idxf[:, b, 0:ncols], float(j),
                            c["LUTB"][:, j:j + 1], ALU.is_equal, ALU.mult))
                        p.add(eng, lambda e, b=b, ncols=ncols: e.tensor_tensor(
                            accb[:, b, 0:ncols], accb[:, b, 0:ncols],
                            tb[:, b, 0:ncols], ALU.add),
                            incs=[acc.inc()] if j == 15 else [])
                p.add("gpsimd", lambda e, dst=dst, rc=rc, b=b, ncols=ncols:
                      e.dma_start(dst[rc * 128:(rc + 1) * 128, :],
                                  accb[:, b, 0:ncols]),
                      waits=[acc.at()], incs=[sp.inc(16)])
                if w1d is not None:
                    for bi in range(11):
                        c0, c1 = bi * 512, min((bi + 1) * 512, ncols)
                        p.add("tensor", mm(self.pb[0][0:1, 0:c1 - c0],
                                           c["cONES128"][:],
                                           accb[:, b, c0:c1], True, True),
                              waits=[acc.at(), ws.at()], incs=[pe.inc()])
                        if rc == 0:
                            p.add("scalar", lambda e, c0=c0, c1=c1:
                                  e.copy(w1acc[0:1, c0:c1],
                                         self.pb[0][0:1, 0:c1 - c0]),
                                  waits=[pe.at()], incs=[ws.inc()])
                        else:
                            p.add("vector", lambda e, c0=c0, c1=c1:
                                  e.tensor_add(w1acc[0:1, c0:c1],
                                               w1acc[0:1, c0:c1],
                                               self.pb[0][0:1, 0:c1 - c0]),
                                  waits=[pe.at()], incs=[ws.inc()])
                if w1d is not None and rc == nr - 1:
                    p.add("gpsimd", dma(w1d, w1acc[0:1, :]), waits=[ws.at()],
                          incs=[sp.inc(16)])
                    p.add("scalar", None, waits=[(sp.h, sp.v)])
                ti += 1
        p.add("gpsimd", None, waits=[sp.at()])
        p.emit(nc, "ghost")
        pl.reset(frame)

    def stage_l1f(self, xdr, ydr, last=True, dbg=False):
        nc, pl, c = self.nc, self.plan, self.c
        gw = self.dram_tmp("gw", [D, INTER], BF16)
        uw = self.dram_tmp("uw", [D, INTER], BF16)
        dw = self.dram_tmp("dw", [INTER, D], BF16)
        w1gd = self.dram_tmp("w1g", [1, INTER], F32)
        w1ud = self.dram_tmp("w1u", [1, INTER], F32)
        xbf = c["xbf"]
        frame = pl.mark()
        w1g = pl.take("w1gL", [1, INTER], BF16)
        w1u = pl.take("w1uL", [1, INTER], BF16)
        Mw = pl.mark()
        w1f = pl.take("w1fL", [1, INTER], F32)
        pld = Prog2()
        ld0 = self.sem("l_ld0"); cv0 = self.sem("l_cv0")
        pld.add("sync", dma(w1f[:], w1gd), incs=[ld0.inc(16)])
        pld.add("vector", lambda e: e.tensor_copy(w1g[:], w1f[:]),
                waits=[ld0.at()], incs=[cv0.inc()])
        pld.add("sync", dma(w1f[:], w1ud), waits=[cv0.at()], incs=[ld0.inc(16)])
        pld.add("vector", lambda e: e.tensor_copy(w1u[:], w1f[:]),
                waits=[ld0.at()], incs=[cv0.inc()])
        pld.emit(nc, "l1f_w1")
        pl.reset(Mw)
        M1 = pl.mark()
        gv = gw.rearrange("(kc kp) n -> kp kc n", kp=128)
        uv = uw.rearrange("(kc kp) n -> kp kc n", kp=128)
        dv_ = dw.rearrange("(kc kp) n -> kp kc n", kp=128)
        xv = xdr.rearrange("(cc p) t -> p cc t", p=128)
        ov = ydr.rearrange("(cc p) t -> p cc t", p=128)
        for s in range(NSLAB):
            sl = slice(s * SLAB, (s + 1) * SLAB)
            p = Prog2()
            ld = self.sem("l_ld"); gmm = self.sem("l_gmm"); ev = self.sem("l_ev")
            sil = self.sem("l_sil"); hm = self.sem("l_hm")
            ybsl = pl.take("ybfL", [128, NCHUNK, SLAB], BF16)
            hmid = pl.take("hmidL", [128, ICHUNK, SLAB], BF16)
            Mh = pl.mark()
            wgt = pl.take("wgtL", [128, 2, NCHUNK, 128], BF16)
            wut = pl.take("wutL", [128, 2, NCHUNK, 128], BF16)
            gf = pl.take("gfL", [128, SLAB], F32)
            sgb = pl.take("sgbL", [128, SLAB], BF16)
            ubf = pl.take("ubfL", [128, SLAB], BF16)
            ld2 = [self.sem("l_lda"), self.sem("l_ldb")]
            for ic in range(ICHUNK):
                b = ic % 2
                p.add("sync", dma(wgt[:, b], gv[:, :, ic * 128:(ic + 1) * 128]),
                      waits=[(gmm.h, gmm.v - 2)], incs=[ld2[b].inc(16)])
                p.add("sync", dma(wut[:, b], uv[:, :, ic * 128:(ic + 1) * 128]),
                      incs=[ld2[b].inc(16)])
                pbg = self.pb[b]
                pbu = self.pb[2 + b]
                for k in range(NCHUNK):
                    p.add("tensor", mm(pbg[:, :], wgt[:, b, k, :], xbf[:, k, sl],
                                       k == 0, False),
                          waits=([ld2[b].at(), (ev.h, ev.v - 2)] if k == 0 else []))
                p.add("tensor", mm(pbg[:, :], w1g[0:1, ic * 128:(ic + 1) * 128],
                                   c["negmu_row"][0:1, sl], False, True),
                      incs=[gmm.inc()])
                for k in range(NCHUNK):
                    p.add("tensor", mm(pbu[:, :], wut[:, b, k, :], xbf[:, k, sl],
                                       k == 0, False),
                          waits=([ld2[b].at(), (sil.h, sil.v - 1)]
                                 if k == 0 else []))
                p.add("tensor", mm(pbu[:, :], w1u[0:1, ic * 128:(ic + 1) * 128],
                                   c["negmu_row"][0:1, sl], False, True),
                      incs=[gmm.inc()])
                p.add("vector", lambda e, pbg=pbg, sl=sl: e.tensor_mul(
                    gf[:, :], pbg[:, :], c["rb"][:, sl]),
                    waits=[gmm.at(), sil.at()], incs=[ev.inc()])
                p.add("scalar", lambda e: e.activation(sgb[:, :], gf[:, :],
                                                       AF.Silu),
                      waits=[ev.at(), hm.at()], incs=[sil.inc()])
                p.add("vector", lambda e, pbu=pbu, sl=sl: e.tensor_mul(
                    ubf[:, :], pbu[:, :], c["rb"][:, sl]),
                    waits=[(hm.h, hm.v - 1)], incs=[ev.inc()])
                p.add("vector", lambda e, ic=ic: e.tensor_mul(
                    hmid[:, ic, :], sgb[:, :], ubf[:, :]),
                    waits=[sil.at()], incs=[hm.inc()])
                if dbg:
                    hdbg = self.dram_tmp("hdbg", [INTER, T], BF16)
                    p.add("gpsimd", lambda e, ic=ic, sl=sl: e.dma_start(
                        hdbg[ic * 128:(ic + 1) * 128, sl], hmid[:, ic, :]),
                        waits=[hm.at()],
                        incs=[self.sem("l_dbg").inc(16)])
            pl.reset(Mh)
            KH = 22
            wdt = pl.take("wdtL", [128, 2, KH, 128], BF16)
            dgm = self.sem("l_dgm"); dev = self.sem("l_dev")
            dev0 = dev.v
            half_last = getattr(self, "_l1f_half_last", [0, 0])
            for oc in range(NCHUNK):
                pbp = self.pb[4 + oc % 2]
                for half in range(2):
                    k0 = half * KH
                    kn = min(KH, ICHUNK - k0)
                    dld = self.sem(f"l_dld{half}")
                    dw_waits = [(dgm.h, half_last[half])]
                    if oc == 0 and half == 0:
                        dw_waits += [gmm.at(), hm.at()]
                    p.add("sync", dma(wdt[:, half, 0:kn],
                                      dv_[:, k0:k0 + kn, oc * 128:(oc + 1) * 128]),
                          waits=dw_waits, incs=[dld.inc(16)])
                    for kq in range(kn):
                        k = k0 + kq
                        p.add("tensor", mm(pbp[:, :], wdt[:, half, kq, :],
                                           hmid[:, k, :],
                                           k == 0, k == ICHUNK - 1),
                              waits=([dld.at(), hm.at(), (dev.h, dev.v - 1)]
                                     if k == 0 else ([dld.at()] if kq == 0 else [])),
                              incs=[dgm.inc()] if kq == kn - 1 else [])
                    half_last[half] = dgm.v
                self._l1f_half_last = half_last
                p.add("scalar", lambda e, pbp=pbp, oc=oc: e.copy(
                    ybsl[:, oc, :], pbp[:, :]),
                    waits=[dgm.at()], incs=[dev.inc()])
                if dbg:
                    ydbg = self.dram_tmp("ydbg", [D, T], BF16)
                    p.add("gpsimd", lambda e, oc=oc, sl=sl: e.dma_start(
                        ydbg[oc * 128:(oc + 1) * 128, sl], ybsl[:, oc, :]),
                        waits=[dev.at()],
                        incs=[self.sem("l_dbg").inc(16)])
            yready = [(dev.h, dev0 + k + 1) for k in range(NCHUNK)]
            self.residual_slab(p, ybsl, s, xv, ov, last=last, yready=yready)
            p.emit(nc, f"l1f_s{s}")
            pl.reset(M1)
        pl.reset(frame)

    def residual_slab(self, p, ybsl, s, xv, ov, last, yready=None):
        nc, pl, c = self.nc, self.plan, self.c
        M = pl.mark()
        sq = self.sem("d_sq"); pe = self.sem("d_pe"); fin = self.sem("d_fin")
        xld = self.sem("d_xld"); st = self.sem("d_st"); xa = self.sem("d_xa")
        xc = self.sem("d_xc"); nxt = self.sem("d_nxt")
        ysq = pl.take("ysqS", [128, 1, SLAB], BF16)
        yrow = pl.take("yrowS", [1, 2 * SLAB], F32)
        rowbuf = yrow
        aysb = pl.take("ayS", [128, SLAB], F32)
        bysb = pl.take("byS", [128, SLAB], F32)
        xf = pl.take("xfS", [128, 1, SLAB], F32)
        xo = pl.take("xoS", [128, 1, SLAB], F32)
        xsq = pl.take("xsqS", [128, 1, SLAB], BF16)
        pbS, pbN, pbB = self.pb[6], self.pb[7], self.pb[6]
        sl = slice(s * SLAB, (s + 1) * SLAB)
        for k in range(NCHUNK):
            b = 0
            yw = [yready[k]] if yready else []
            p.add("scalar", lambda e, k=k, b=b: e.activation(
                ysq[:, b], ybsl[:, k, :], AF.Square),
                waits=[pe.at()] + yw, incs=[sq.inc()])
            p.add("tensor", mm(pbS[0:1, :], c["cMEANVEC"][:], ybsl[:, k, :],
                               k == 0, k == 15),
                  waits=([fin.at()] if k == 0 else []) + yw)
            p.add("tensor", mm(pbS[32:33, :], c["cMEANVEC"][:], ysq[:, b],
                               k == 0, k == 15), waits=[sq.at()],
                  incs=[pe.inc()])
        p.add("scalar", lambda e: e.copy(yrow[0:1, 0:SLAB], pbS[0:1, :]),
              waits=[pe.at()])
        p.add("scalar", lambda e: e.activation(yrow[0:1, SLAB:],
                                               yrow[0:1, 0:SLAB], AF.Square),
              incs=[fin.inc()])
        p.add("vector", lambda e: e.tensor_sub(yrow[0:1, SLAB:], pbS[32:33, :],
                                               yrow[0:1, SLAB:]),
              waits=[fin.at()], incs=[fin.inc()])
        p.add("scalar", lambda e: e.activation(yrow[0:1, SLAB:], yrow[0:1, SLAB:],
                                               AF.Ln, bias=1e-5),
              waits=[fin.at()])
        p.add("scalar", lambda e: e.activation(yrow[0:1, SLAB:], yrow[0:1, SLAB:],
                                               AF.Exp, scale=-0.5),
              incs=[fin.inc()])
        p.add("vector", lambda e: e.tensor_mul(yrow[0:1, 0:SLAB],
                                               yrow[0:1, 0:SLAB],
                                               yrow[0:1, SLAB:]),
              waits=[fin.at()], incs=[fin.inc()])
        p.add("tensor", mm(pbB[:, :], c["cONESROW"][:], yrow[0:1, SLAB:],
                           True, True), waits=[fin.at()], incs=[pe.inc()])
        p.add("scalar", lambda e: e.copy(aysb[:, :], pbB[:, :]),
              waits=[pe.at(), xa.at()], incs=[fin.inc()])
        p.add("tensor", mm(pbB[:, :], c["cONESROW"][:], yrow[0:1, 0:SLAB],
                           True, True), waits=[fin.at()], incs=[pe.inc()])
        p.add("scalar", lambda e: e.copy(bysb[:, :], pbB[:, :]),
              waits=[pe.at()], incs=[fin.inc()])
        for k in range(NCHUNK):
            b = 0
            p.add("sync", dma(xf[:, b], xv[:, k, sl]),
                  waits=[xa.at()], incs=[xld.inc(16)])
            p.add("vector", lambda e, k=k, b=b: e.tensor_mul(
                xo[:, b], ybsl[:, k, :], aysb[:, :]),
                waits=[fin.at(), xld.at(), st.at()])
            p.add("vector", lambda e, b=b: e.tensor_sub(xo[:, b], xo[:, b],
                                                        bysb[:, :]))
            p.add("vector", lambda e, b=b: e.tensor_add(xo[:, b], xo[:, b],
                                                        xf[:, b]),
                  incs=[xa.inc()])
            p.add("gpsimd", lambda e, k=k, sl=sl, b=b: e.dma_start(
                ov[:, k, sl], xo[:, b]), waits=[xa.at()], incs=[st.inc(16)])
            if not last:
                p.add("vector", lambda e, k=k, sl=sl, b=b: e.tensor_copy(
                    c["xbf"][:, k, sl], xo[:, b]), incs=[xc.inc()])
                p.add("scalar", lambda e, k=k, sl=sl, b=b: e.activation(
                    xsq[:, b], c["xbf"][:, k, sl], AF.Square),
                    waits=[xc.at(), nxt.at()], incs=[sq.inc()])
                p.add("tensor", mm(pbN[0:1, :], c["cMEANVEC"][:],
                                   c["xbf"][:, k, sl], k == 0, k == 15),
                      waits=[xc.at()])
                p.add("tensor", mm(pbN[32:33, :], c["cMEANVEC"][:], xsq[:, b],
                                   k == 0, k == 15), waits=[sq.at()],
                      incs=[nxt.inc()])
        if not last:
            self.finalize_ln(p, pbN, c["r_row"][0:1, sl],
                             c["negmu_row"][0:1, sl], rowbuf, nxt, fin,
                             invr_dst=c["invr_row"][0:1, sl],
                             rb_dst=c["rb"][:, sl], pbB=self.pb[6])
            p.add("tensor", None, waits=[fin.at()])
        p.add("gpsimd", None, waits=[st.at()])
        pl.reset(M)

    def residual_phase(self, ysp, xdr, ydr, last=False):
        nc, pl, c = self.nc, self.plan, self.c
        p = Prog2()
        M = pl.mark()
        ld = self.sem("d_ld"); sq = self.sem("d_sq"); pe = self.sem("d_pe")
        fin = self.sem("d_fin"); ev = self.sem("d_ev"); st = self.sem("d_st")
        xld2 = [self.sem("d_xld0"), self.sem("d_xld1")]; nxt = self.sem("d_nxt")
        ybsl = pl.take("ybD", [128, NCHUNK, SLAB], BF16)
        ysq = pl.take("ysqD", [128, 2, SLAB], BF16)
        rowbuf = pl.take("rowD", [1, 2 * SLAB], F32)
        yrow = pl.take("yrowD", [1, 2 * SLAB], F32)
        aysb = pl.take("ayD", [128, SLAB], F32)
        bysb = pl.take("byD", [128, SLAB], F32)
        xf = pl.take("xfD", [128, 2, SLAB], F32)
        xo = pl.take("xoD", [128, 2, SLAB], F32)
        xsq = pl.take("xsqD", [128, 2, SLAB], BF16)
        yv = ysp.rearrange("(cc p) t -> p cc t", p=128)
        xv = xdr.rearrange("(cc p) t -> p cc t", p=128)
        ov = ydr.rearrange("(cc p) t -> p cc t", p=128)
        pbS, pbN, pbB = self.pb[2], self.pb[3], self.pb[6]
        xa = self.sem("d_xa")
        xc = self.sem("d_xc")
        for s in range(NSLAB):
            sl = slice(s * SLAB, (s + 1) * SLAB)
            p.add("sync", dma(ybsl[:], yv[:, :, sl]),
                  waits=[sq.at(), pe.at(), xa.at()], incs=[ld.inc(16)])
            for k in range(NCHUNK):
                b = k % 2
                p.add("scalar", lambda e, k=k, b=b: e.activation(
                    ysq[:, b], ybsl[:, k, :], AF.Square),
                    waits=[ld.at(), (pe.h, pe.v - 1)], incs=[sq.inc()])
                p.add("tensor", mm(pbS[0:1, :], c["cMEANVEC"][:], ybsl[:, k, :],
                                   k == 0, k == 15),
                      waits=[ld.at(), fin.at()] if k == 0 else [])
                p.add("tensor", mm(pbS[32:33, :], c["cMEANVEC"][:], ysq[:, b],
                                   k == 0, k == 15), waits=[sq.at()],
                      incs=[pe.inc()])
            p.add("scalar", lambda e: e.copy(yrow[0:1, 0:SLAB], pbS[0:1, :]),
                  waits=[pe.at()])
            p.add("scalar", lambda e: e.activation(yrow[0:1, SLAB:],
                                                   yrow[0:1, 0:SLAB], AF.Square),
                  incs=[fin.inc()])
            p.add("vector", lambda e: e.tensor_sub(yrow[0:1, SLAB:], pbS[32:33, :],
                                                   yrow[0:1, SLAB:]),
                  waits=[fin.at()], incs=[fin.inc()])
            p.add("scalar", lambda e: e.activation(yrow[0:1, SLAB:], yrow[0:1, SLAB:],
                                                   AF.Ln, bias=1e-5),
                  waits=[fin.at()])
            p.add("scalar", lambda e: e.activation(yrow[0:1, SLAB:], yrow[0:1, SLAB:],
                                                   AF.Exp, scale=-0.5),
                  incs=[fin.inc()])
            p.add("vector", lambda e: e.tensor_mul(yrow[0:1, 0:SLAB],
                                                   yrow[0:1, 0:SLAB],
                                                   yrow[0:1, SLAB:]),
                  waits=[fin.at()], incs=[fin.inc()])
            p.add("tensor", mm(pbB[:, :], c["cONESROW"][:], yrow[0:1, SLAB:],
                               True, True), waits=[fin.at()], incs=[pe.inc()])
            p.add("scalar", lambda e: e.copy(aysb[:, :], pbB[:, :]),
                  waits=[pe.at(), xa.at()], incs=[fin.inc()])
            p.add("tensor", mm(pbB[:, :], c["cONESROW"][:], yrow[0:1, 0:SLAB],
                               True, True), waits=[fin.at()], incs=[pe.inc()])
            p.add("scalar", lambda e: e.copy(bysb[:, :], pbB[:, :]),
                  waits=[pe.at()], incs=[fin.inc()])
            for k in range(NCHUNK):
                b = k % 2
                p.add("sync", dma(xf[:, b], xv[:, k, sl]),
                      waits=[(xa.h, xa.v - 1)], incs=[xld2[b].inc(16)])
                p.add("vector", lambda e, k=k, b=b: e.tensor_mul(
                    xo[:, b], ybsl[:, k, :], aysb[:, :]),
                    waits=[fin.at(), xld2[b].at(), (st.h, 16 * (st.v // 16 - 1))])
                p.add("vector", lambda e, b=b: e.tensor_sub(xo[:, b], xo[:, b],
                                                            bysb[:, :]))
                p.add("vector", lambda e, b=b: e.tensor_add(xo[:, b], xo[:, b],
                                                            xf[:, b]),
                      incs=[xa.inc()])
                p.add("gpsimd", lambda e, k=k, sl=sl, b=b: e.dma_start(
                    ov[:, k, sl], xo[:, b]), waits=[xa.at()], incs=[st.inc(16)])
                if not last:
                    p.add("vector", lambda e, k=k, sl=sl, b=b: e.tensor_copy(
                        c["xbf"][:, k, sl], xo[:, b]), incs=[xc.inc()])
                    p.add("scalar", lambda e, k=k, sl=sl, b=b: e.activation(
                        xsq[:, b], c["xbf"][:, k, sl], AF.Square),
                        waits=[xc.at(), (nxt.h, nxt.v - 1)], incs=[sq.inc()])
                    p.add("tensor", mm(pbN[0:1, :], c["cMEANVEC"][:],
                                       c["xbf"][:, k, sl], k == 0, k == 15),
                          waits=[xc.at()])
                    p.add("tensor", mm(pbN[32:33, :], c["cMEANVEC"][:], xsq[:, b],
                                       k == 0, k == 15), waits=[sq.at()],
                          incs=[nxt.inc()])
            if not last:
                sl2 = slice(s * SLAB, (s + 1) * SLAB)
                self.finalize_ln(p, pbN, c["r_row"][0:1, sl2],
                                 c["negmu_row"][0:1, sl2], rowbuf, nxt, fin,
                                 invr_dst=c["invr_row"][0:1, sl2],
                                 rb_dst=c["rb"][:, sl2], pbB=self.pb[7])
                p.add("tensor", None, waits=[fin.at()])
        p.add("gpsimd", None, waits=[st.at()])
        p.emit(nc, "resid")
        pl.reset(M)


N_CORES = 8

def _build_launch(kind):
    nc = bass.Bass("TRN2", target_bir_lowering=False)
    kb = KB(nc)
    want = {"l0a": (), "l0f": ("l0f",), "l1a": ("l1a",), "l1f": ("l1f",)}[kind]
    kb.stage_prep(want=want)
    kb.alloc_globals()
    x0 = kb.dram_in("x0", [D, T]); y0 = kb.dram_out("y0", [D, T])
    x1 = kb.dram_in("x1", [D, T]); y1 = kb.dram_out("y1", [D, T])
    if kind == "l1f":
        kb.stage_ghost()
    for seq, (xdr, ydr) in enumerate(((x0, y0), (x1, y1))):
        p = Prog2()
        kb.stats_pass(p, xdr, xbf_full=kb.c["xbf"], pfx="st")
        p.emit(nc, f"stats{seq}")
        if kind == "l0a":
            kb.stage_l0a(xdr, ydr, last=True)
        elif kind == "l0f":
            kb.stage_l0f(xdr, ydr, last=True)
        elif kind == "l1a":
            kb.stage_l1a(xdr, ydr, last=True)
        else:
            kb.stage_l1f(xdr, ydr, last=True)
    nc.finalize()
    return nc

_LAUNCH_INPUTS = {
    "l0a": ("Wq", "Wk", "Wv", "Wo"),
    "l0f": ("WaQK", "WaV", "Wp", "wstd", "wrec"),
    "l1a": ("CwT", "DwT", "Cb", "Db", "LD", "Afm", "Bfm"),
    "l1f": ("gidxT", "uidxT", "didxT", "lut"),
}
_BASE_CONSTS = ("cCOS", "cSIN", "cR128T", "cID128", "cMASKT", "cONES128",
                "cMEANVEC", "cONESROW", "cONESROWB")
_LAUNCH_CONSTS = {
    "l0a": _BASE_CONSTS,
    "l0f": _BASE_CONSTS + ("cSTDMASK", "cTAILMASK"),
    "l1a": _BASE_CONSTS,
    "l1f": _BASE_CONSTS,
}

def _build_fused():
    nc = bass.Bass("TRN2", target_bir_lowering=False)
    kb = KB(nc)
    kb.stage_prep(want=("l0f", "l1a", "l1f"))
    kb.alloc_globals()
    x0 = kb.dram_in("x0", [D, T]); y0 = kb.dram_out("y0", [D, T])
    x1 = kb.dram_in("x1", [D, T]); y1 = kb.dram_out("y1", [D, T])
    kb.stage_ghost()
    for seq, (xin, yout) in enumerate(((x0, y0), (x1, y1))):
        p = Prog2()
        kb.stats_pass(p, xin, xbf_full=kb.c["xbf"], pfx="st")
        p.emit(nc, f"stats{seq}")
        xa = kb.dram_tmp(f"xa{seq}", [D, T])
        xb = kb.dram_tmp(f"xb{seq}", [D, T])
        xc = kb.dram_tmp(f"xc{seq}", [D, T])
        kb.stage_l0a(xin, xa, last=False)
        kb.stage_l0f(xa, xb, last=False)
        kb.stage_l1a(xb, xc, last=False)
        kb.stage_l1f(xc, yout, last=True)
    nc.finalize()
    return nc

_FUSED = True

def kernel(**inputs):
    w = layout_weights(inputs)
    x = np.asarray(inputs["L0_attn_x"])
    xfm = np.ascontiguousarray(x.transpose(0, 2, 1))
    cur0 = [np.ascontiguousarray(xfm[2 * cid]) for cid in range(N_CORES)]
    cur1 = [np.ascontiguousarray(xfm[2 * cid + 1]) for cid in range(N_CORES)]
    if _FUSED:
        nc = _build_fused()
        in_maps = []
        for cid in range(N_CORES):
            im = {"x0": cur0[cid], "x1": cur1[cid]}
            for kind in ("l0a", "l0f", "l1a", "l1f"):
                for nm in _LAUNCH_INPUTS[kind]:
                    im[nm] = w[nm]
            for nm in CONSTS:
                im[nm] = CONSTS[nm]
            in_maps.append(im)
        res = run_bass_kernel_spmd(nc, in_maps, core_ids=list(range(N_CORES)))
        out = np.empty((16, T, D), dtype=np.float32)
        for cid in range(N_CORES):
            out[2 * cid] = res.results[cid]["y0"].T
            out[2 * cid + 1] = res.results[cid]["y1"].T
        return out
    for kind in ("l0a", "l0f", "l1a", "l1f"):
        nc = _build_launch(kind)
        in_maps = []
        for cid in range(N_CORES):
            im = {"x0": cur0[cid], "x1": cur1[cid]}
            for nm in _LAUNCH_INPUTS[kind]:
                im[nm] = w[nm]
            for nm in _LAUNCH_CONSTS[kind]:
                im[nm] = CONSTS[nm]
            in_maps.append(im)
        res = run_bass_kernel_spmd(nc, in_maps, core_ids=list(range(N_CORES)))
        cur0 = [res.results[cid]["y0"] for cid in range(N_CORES)]
        cur1 = [res.results[cid]["y1"] for cid in range(N_CORES)]
    out = np.empty((16, T, D), dtype=np.float32)
    for cid in range(N_CORES):
        out[2 * cid] = cur0[cid].T
        out[2 * cid + 1] = cur1[cid].T
    return out


# version 08
# speedup vs baseline: 1.7600x; 1.7327x over previous
import sys
if "/opt/trn_rl_repo" not in sys.path:
    sys.path.insert(0, "/opt/trn_rl_repo")
import math
import numpy as np
import ml_dtypes
import concourse.bass as bass
import concourse.mybir as mybir
from concourse.bass_utils import run_bass_kernel_spmd

F32 = mybir.dt.float32
BF16 = mybir.dt.bfloat16
I32 = mybir.dt.int32
AF = mybir.ActivationFunctionType
ALU = mybir.AluOpType
AX = mybir.AxisListType

D = 2048
T = 2048
NCHUNK = 16
SLAB = 512
NSLAB = T // SLAB
NEG = -1e30
INTER = 5504
ICHUNK = INTER // 128


def _bf(x):
    return np.asarray(x, dtype=np.float32).astype(ml_dtypes.bfloat16)

def build_consts():
    c = {}
    dh = 64
    pos = np.arange(T, dtype=np.float64)
    invf = 1.0 / (10000.0 ** (np.arange(0, dh, 2, dtype=np.float64) / dh))
    p = np.arange(128)
    d_in_head = p % 64
    freq_idx = np.where(d_in_head < 32, d_in_head, d_in_head - 32)
    ang = pos[None, :] * invf[freq_idx][:, None]
    c["cCOS"] = _bf(np.cos(ang))
    c["cSIN"] = _bf(np.sin(ang))
    R = np.zeros((64, 64), np.float64)
    for d_ in range(32):
        R[d_, d_ + 32] = -1.0
    for d_ in range(32, 64):
        R[d_, d_ - 32] = 1.0
    R128 = np.zeros((128, 128), np.float64)
    R128[:64, :64] = R
    R128[64:, 64:] = R
    c["cR128T"] = _bf(R128.T)
    c["cID128"] = _bf(np.eye(128))
    masks = np.zeros((128, 4, SLAB), np.float64)
    ktl = np.arange(128)
    qtl = np.arange(SLAB)
    for j in range(4):
        bad = (ktl[:, None] + j * 128) > qtl[None, :]
        masks[:, j, :] = np.where(bad, NEG, 0.0)
    c["cMASKT"] = _bf(masks)
    c["cONES128"] = _bf(np.ones((128, 1)))
    c["cMEANVEC"] = _bf(np.full((128, 1), 1.0 / D))
    c["cONESROW"] = np.ones((1, 128), np.float32)
    std_mask = np.zeros((1, 128), np.float32); std_mask[0, :120] = 1.0
    tail_mask = np.zeros((1, 128), np.float32); tail_mask[0, 120:] = 1.0
    c["cSTDMASK"] = std_mask
    c["cTAILMASK"] = tail_mask
    c["cONESROWB"] = _bf(np.ones((1, 128)))
    return c

CONSTS = build_consts()

def layout_weights(inp):
    w = {}
    Wq = np.asarray(inp["L0_attn_Wq"])
    cols = []
    for cchunk in range(16):
        for h in (cchunk, cchunk + 16):
            cols.append(Wq[:, h * 64:(h + 1) * 64])
    w["Wq"] = np.ascontiguousarray(np.concatenate(cols, axis=1))
    Wkv = np.asarray(inp["L0_attn_Wkv"])
    kc = Wkv[:, :512]; vc = Wkv[:, 512:]
    cols = []
    for j in range(4):
        cols.append(kc[:, j * 64:(j + 1) * 64])
        cols.append(kc[:, (j + 4) * 64:(j + 5) * 64])
    w["Wk"] = np.ascontiguousarray(np.concatenate(cols, axis=1))
    w["Wv"] = np.ascontiguousarray(vc)
    Wo = np.asarray(inp["L0_attn_Wo"])
    rows = []
    for cchunk in range(16):
        for h in (cchunk, cchunk + 16):
            rows.append(Wo[h * 64:(h + 1) * 64, :])
    w["Wo"] = np.ascontiguousarray(np.concatenate(rows, axis=0))
    Wa = np.asarray(inp["L0_ffn_Wa"])
    H, Ds, r = 16, 120, 8
    q_std_end = H * Ds; k_std_end = 2 * H * Ds; v_end = k_std_end + D
    cols = []
    for h in range(H):
        cols.append(Wa[:, h * Ds:(h + 1) * Ds])
        cols.append(Wa[:, v_end + h * r: v_end + (h + 1) * r])
        cols.append(Wa[:, q_std_end + h * Ds: q_std_end + (h + 1) * Ds])
        cols.append(Wa[:, v_end + H * r + h * r: v_end + H * r + (h + 1) * r])
    w["WaQK"] = np.ascontiguousarray(np.concatenate(cols, axis=1))
    w["WaV"] = np.ascontiguousarray(Wa[:, k_std_end:v_end])
    w["Wp"] = np.ascontiguousarray(np.asarray(inp["L0_ffn_Wp"]))
    w["CwT"] = np.ascontiguousarray(np.asarray(inp["L1_attn_C_w"]).T)
    w["DwT"] = np.ascontiguousarray(np.asarray(inp["L1_attn_D_w"]).T)
    w["Cb"] = np.ascontiguousarray(np.asarray(inp["L1_attn_C_b"]).reshape(1, D))
    w["Db"] = np.ascontiguousarray(np.asarray(inp["L1_attn_D_b"]).reshape(1, D))
    w["LD"] = np.ascontiguousarray(
        np.asarray(inp["L1_attn_log_delta"]).reshape(16, 128).T)
    w["Afm"] = np.ascontiguousarray(
        np.asarray(inp["L1_attn_A"]).reshape(16, 128, 64).transpose(1, 0, 2))
    w["Bfm"] = np.ascontiguousarray(
        np.asarray(inp["L1_attn_B"]).reshape(16, 128, 64).transpose(1, 0, 2))
    w["wstd"] = np.ascontiguousarray(
        np.asarray(inp["L0_ffn_w_std"], dtype=np.float32).reshape(1, 16))
    w["wrec"] = np.ascontiguousarray(
        np.asarray(inp["L0_ffn_w_rec"], dtype=np.float32).reshape(1, 16))
    w["gidxT"] = np.ascontiguousarray(np.asarray(inp["L1_ffn_gate_idx"]).T)
    w["uidxT"] = np.ascontiguousarray(np.asarray(inp["L1_ffn_up_idx"]).T)
    w["didxT"] = np.ascontiguousarray(np.asarray(inp["L1_ffn_down_idx"]).T)
    w["lut"] = np.ascontiguousarray(
        np.asarray(inp["L1_ffn_lut"], dtype=np.float32).reshape(1, 16))
    return w


ENGS = ("sync", "tensor", "vector", "scalar", "gpsimd")

class Sem:
    def __init__(self, nc, name):
        self.h = nc.alloc_semaphore(name)
        self.v = 0
    def inc(self, n=1):
        self.v += n
        return (self.h, n)
    def at(self):
        return (self.h, self.v)

_BLK_N = [0]

class Prog2:
    def __init__(self):
        self.ops = {e: [] for e in ENGS}
    def add(self, eng, fn, waits=(), incs=()):
        self.ops[eng].append((fn, tuple(waits), tuple(incs)))
    def emit(self, nc, name=None):
        if name is not None:
            _BLK_N[0] += 1
            name = f"{name}_{_BLK_N[0]}"
        with nc.Block(name) as block:
            for e in ENGS:
                oplist = self.ops[e]
                if not oplist:
                    continue
                def body(eng, oplist=oplist):
                    for fn, waits, incs in oplist:
                        for (sem, val) in waits:
                            if val > 0:
                                eng.wait_ge(sem, val)
                        ins = fn(eng) if fn is not None else None
                        for (sem, delta) in incs:
                            ins = ins.then_inc(sem, delta)
                getattr(block, e)(body)

class SbufPlan:
    def __init__(self, nc, base=18432, top=229248):
        self.nc = nc
        self.cur = base
        self.top = top
        self.n = 0
    def take(self, name, shape, dtype):
        szb = {F32: 4, BF16: 2, I32: 4}[dtype]
        per_part = int(np.prod(shape[1:])) * szb
        off = (self.cur + 31) & ~31
        assert off + per_part <= self.top, (
            f"SBUF overflow at {name}: {off}+{per_part} > {self.top}")
        self.cur = off + per_part
        self.n += 1
        return self.nc.alloc_sbuf_tensor_at(f"{name}_{self.n}", list(shape), dtype,
                                            offset=off)
    def mark(self):
        return self.cur
    def reset(self, mark):
        self.cur = mark

def dma(ap_out, ap_in, **kw):
    def fn(eng):
        return eng.dma_start(ap_out, ap_in, **kw)
    return fn

def mm(out, lhsT, rhs, start, stop, **kw):
    def fn(eng):
        return eng.matmul(out, lhsT, rhs, start=start, stop=stop,
                          skip_group_check=True, **kw)
    return fn

class KB:
    def __init__(self, nc):
        self.nc = nc
        self.plan = SbufPlan(nc)
        self.sems = {}
        self.pb = [nc.alloc_psum_tensor(f"pb{i}", [128, 512], F32) for i in range(8)]
        self.d = {}
        self.c = {}

    def sem(self, name):
        if name not in self.sems:
            self.sems[name] = Sem(self.nc, name)
        return self.sems[name]

    def dram_in(self, name, shape, dtype=F32):
        if name not in self.d:
            self.d[name] = self.nc.dram_tensor(name, list(shape), dtype,
                                               kind="ExternalInput").ap()
        return self.d[name]

    def dram_out(self, name, shape, dtype=F32):
        self.d[name] = self.nc.dram_tensor(name, list(shape), dtype,
                                           kind="ExternalOutput").ap()
        return self.d[name]

    def dram_tmp(self, name, shape, dtype=F32):
        if name not in self.d:
            self.d[name] = self.nc.dram_tensor(name, list(shape), dtype,
                                               kind="Internal").ap()
        return self.d[name]

    def stage_prep(self, want=("l0a", "l0f", "l1a", "l1f")):
        nc, pl, c = self.nc, self.plan, self.c
        p = Prog2()
        ld = self.sem("ld")
        v = self.sem("vv")
        s_ = self.sem("ss")
        m_ = self.sem("mm")
        for nm, dt in (("cCOS", BF16), ("cSIN", BF16), ("cR128T", BF16),
                       ("cID128", BF16), ("cMASKT", BF16), ("cONES128", BF16),
                       ("cMEANVEC", BF16), ("cONESROW", F32),
                       ("cONESROWB", BF16)):
            shape = list(CONSTS[nm].shape)
            c[nm] = pl.take(nm, shape, dt)
            dr = self.dram_in(nm, shape, dt)
            p.add("sync", dma(c[nm][:], dr), incs=[ld.inc(16)])
        c["r_row"] = pl.take("r_row", [1, T], BF16)
        c["negmu_row"] = pl.take("negmu_row", [1, T], BF16)
        c["rb"] = pl.take("rb", [128, T], F32)
        for i, val in enumerate((1e-5, 2.0)):
            t = pl.take(f"cstf{i}", [128, 1], F32)
            nc.const_aps.register(F32, val, t[:])
            p.add("gpsimd", lambda e, t=t, val=val: e.memset(t[:], val))
        if "l0f" in want:
            c["CSQ"] = pl.take("csq", [128, 16], F32)
            c["CSK"] = pl.take("csk", [128, 16], F32)
            c["BM"] = pl.take("bm", [128, 16], F32)
        if "l1a" in want:
            c["ADIAG"] = pl.take("adiag", [128, 16], F32)
            c["AB"] = pl.take("ab", [128, 16], F32)
        if "l1f" in want:
            c["LUTB"] = pl.take("lutb", [128, 16], F32)
        self.c["prepM"] = pl.mark()
        if "l0f" in want:
            for nm in ("cSTDMASK", "cTAILMASK"):
                c[nm] = pl.take(nm, [1, 128], F32)
                dr = self.dram_in(nm, [1, 128], F32)
                p.add("sync", dma(c[nm][:], dr), incs=[ld.inc(16)])
            wstd = self.dram_in("wstd", [1, 16], F32)
            wrec = self.dram_in("wrec", [1, 16], F32)
            srow = pl.take("srow", [1, 48], F32)
            p.add("sync", dma(srow[0:1, 0:16], wstd), incs=[ld.inc(16)])
            p.add("sync", dma(srow[0:1, 16:32], wrec), incs=[ld.inc(16)])
            p.add("sync", dma(srow[0:1, 32:48], wrec), incs=[ld.inc(16)])
            p.add("vector", lambda e: e.tensor_scalar_max(
                srow[0:1, 0:32], srow[0:1, 0:32], 1e-8),
                waits=[ld.at()], incs=[v.inc()])
            p.add("scalar", lambda e: e.activation(
                srow[0:1, 0:32], srow[0:1, 0:32], AF.Sqrt),
                waits=[v.at()], incs=[s_.inc()])
            p.add("vector", lambda e: e.tensor_single_scalar(
                srow[0:1, 32:48], srow[0:1, 32:48], 0.1, ALU.is_gt),
                waits=[ld.at()], incs=[v.inc()])
            p.add("tensor", mm(self.pb[0][:, 0:16], c["cSTDMASK"][:],
                               srow[0:1, 0:16], True, False),
                  waits=[s_.at(), v.at()])
            p.add("tensor", mm(self.pb[0][:, 0:16], c["cTAILMASK"][:],
                               srow[0:1, 16:32], False, True), incs=[m_.inc()])
            p.add("tensor", mm(self.pb[1][:, 0:16], c["cTAILMASK"][:],
                               srow[0:1, 32:48], True, True), incs=[m_.inc()])
            p.add("scalar", lambda e: e.copy(c["CSK"][:], self.pb[0][:, 0:16]),
                  waits=[m_.at()], incs=[s_.inc()])
            p.add("scalar", lambda e: e.copy(c["BM"][:], self.pb[1][:, 0:16]),
                  incs=[s_.inc()])
            p.add("vector", lambda e: e.tensor_scalar_mul(
                c["CSQ"][:], self.pb[0][:, 0:16], 1.0 / math.sqrt(128.0)),
                waits=[m_.at()], incs=[v.inc()])
        if "l1a" in want:
            ldr = self.dram_in("LD", [128, 16], F32)
            afm = self.dram_in("Afm", [128, 16, 64], F32)
            bfm = self.dram_in("Bfm", [128, 16, 64], F32)
            tmpA = pl.take("tmpA", [128, 16, 64], F32)
            tmpB = pl.take("tmpB", [128, 16, 64], F32)
            sp1 = pl.take("sp1", [128, 16], F32)
            sp2 = pl.take("sp2", [128, 16], F32)
            p.add("sync", dma(c["ADIAG"][:], ldr), incs=[ld.inc(16)])
            p.add("sync", dma(tmpA[:], afm), incs=[ld.inc(16)])
            p.add("sync", dma(tmpB[:], bfm), incs=[ld.inc(16)])
            p.add("scalar", lambda e: e.activation(sp1[:], c["ADIAG"][:], AF.Abs),
                  waits=[ld.at()])
            p.add("scalar", lambda e: e.activation(sp1[:], sp1[:], AF.Exp,
                                                   scale=-1.0))
            p.add("scalar", lambda e: e.activation(sp1[:], sp1[:], AF.Ln, bias=1.0))
            p.add("scalar", lambda e: e.activation(sp2[:], c["ADIAG"][:], AF.Relu),
                  incs=[s_.inc()])
            p.add("vector", lambda e: e.tensor_add(sp1[:], sp1[:], sp2[:]),
                  waits=[s_.at()], incs=[v.inc()])
            p.add("scalar", lambda e: e.activation(c["ADIAG"][:], sp1[:], AF.Exp,
                                                   scale=-1.0),
                  waits=[v.at()], incs=[s_.inc()])
            p.add("vector", lambda e: e.tensor_mul(tmpA[:], tmpA[:], tmpB[:]),
                  waits=[ld.at()])
            p.add("vector", lambda e: e.tensor_reduce(
                c["AB"][:], tmpA[:], AX.X, ALU.add), incs=[v.inc()])
        if "l1f" in want:
            lut = self.dram_in("lut", [1, 16], F32)
            lrow = pl.take("lrow", [1, 16], F32)
            p.add("sync", dma(lrow[:], lut), incs=[ld.inc(16)])
            p.add("tensor", mm(self.pb[2][:, 0:16], c["cONESROW"][:], lrow[:],
                               True, True), waits=[ld.at()], incs=[m_.inc()])
            p.add("scalar", lambda e: e.copy(c["LUTB"][:], self.pb[2][:, 0:16]),
                  waits=[m_.at()], incs=[s_.inc()])
        p.add("sync", None, waits=[ld.at()])
        p.emit(nc, "prep")
        if "prepM" in self.c:
            pl.reset(self.c.pop("prepM"))

    def stats_pass(self, p, xdr, xbf_full=None, pfx="st"):
        nc, pl, c = self.nc, self.plan, self.c
        mark = pl.mark()
        ld2 = [self.sem(pfx + "_ld0"), self.sem(pfx + "_ld1")]
        cast = self.sem(pfx + "_cast")
        sqd = self.sem(pfx + "_sq")
        pemm = self.sem(pfx + "_pe")
        fin = self.sem(pfx + "_fin")
        xf = pl.take(pfx + "xf", [128, 2, SLAB], F32)
        xb2 = None if xbf_full is not None else pl.take(pfx + "xb", [128, 2, SLAB], BF16)
        sq = pl.take(pfx + "sq", [128, 2, SLAB], BF16)
        rowbuf = pl.take(pfx + "row", [1, 2 * SLAB], F32)
        pbS, pbB = self.pb[6], self.pb[7]
        xdrv = xdr.rearrange("(c p) t -> p c t", p=128)
        cast0, sq0, pe0 = cast.v, sqd.v, pemm.v
        for s in range(NSLAB):
            for ch in range(NCHUNK):
                i = s * NCHUNK + ch
                b = i % 2
                p.add("sync", dma(xf[:, b], xdrv[:, ch, s * SLAB:(s + 1) * SLAB]),
                      waits=[(cast.h, cast0 + i - 1)], incs=[ld2[b].inc(16)])
                dst = (xbf_full[:, ch, s * SLAB:(s + 1) * SLAB]
                       if xbf_full is not None else xb2[:, b])
                p.add("vector",
                      lambda e, dst=dst, b=b: e.tensor_copy(dst, xf[:, b]),
                      waits=[ld2[b].at(), (pemm.h, pe0 + i - 1)], incs=[cast.inc()])
                p.add("scalar",
                      lambda e, dst=dst, b=b: e.activation(sq[:, b], dst, AF.Square),
                      waits=[cast.at(), (pemm.h, pe0 + i - 1)], incs=[sqd.inc()])
                p.add("tensor", mm(pbS[0:1, :], c["cMEANVEC"][:], dst,
                                   ch == 0, ch == 15), waits=[cast.at()])
                p.add("tensor", mm(pbS[32:33, :], c["cMEANVEC"][:], sq[:, b],
                                   ch == 0, ch == 15),
                      waits=[sqd.at()], incs=[pemm.inc()])
            sl = slice(s * SLAB, (s + 1) * SLAB)
            p.add("scalar", lambda e: e.copy(rowbuf[0:1, 0:SLAB], pbS[0:1, :]),
                  waits=[pemm.at()])
            p.add("scalar", lambda e: e.activation(rowbuf[0:1, SLAB:],
                                                   rowbuf[0:1, 0:SLAB], AF.Square),
                  incs=[fin.inc()])
            p.add("vector", lambda e: e.tensor_sub(rowbuf[0:1, SLAB:],
                                                   pbS[32:33, :], rowbuf[0:1, SLAB:]),
                  waits=[fin.at()], incs=[fin.inc()])
            p.add("vector", lambda e, sl=sl: e.tensor_scalar_mul(
                c["negmu_row"][0:1, sl], rowbuf[0:1, 0:SLAB], -1.0),
                incs=[fin.inc()])
            p.add("scalar", lambda e: e.activation(rowbuf[0:1, SLAB:],
                                                   rowbuf[0:1, SLAB:], AF.Ln,
                                                   bias=1e-5), waits=[fin.at()])
            p.add("scalar", lambda e, sl=sl: e.activation(
                c["r_row"][0:1, sl], rowbuf[0:1, SLAB:], AF.Exp, scale=-0.5),
                incs=[fin.inc()])
            p.add("scalar", lambda e, sl=sl: e.activation(
                c["invr_row"][0:1, sl], rowbuf[0:1, SLAB:], AF.Exp, scale=0.5),
                incs=[fin.inc()])
            p.add("tensor", mm(pbB[:, :], c["cONESROWB"][:], c["r_row"][0:1, sl],
                               True, True), waits=[fin.at()], incs=[pemm.inc()])
            p.add("scalar", lambda e, sl=sl: e.copy(c["rb"][:, sl], pbB[:, :]),
                  waits=[pemm.at()], incs=[fin.inc()])
            p.add("tensor", None, waits=[fin.at()])
        pl.reset(mark)

    def alloc_globals(self):
        self.c["xbf"] = self.plan.take("xbf", [128, NCHUNK, T], BF16)
        self.c["invr_row"] = self.plan.take("invr_row", [1, T], BF16)

    def finalize_ln(self, p, pbS, r_dst, negmu_dst, rowbuf, pe, fin,
                    invr_dst=None, rb_dst=None, pbB=None):
        c = self.c
        p.add("scalar", lambda e: e.copy(rowbuf[0:1, 0:SLAB], pbS[0:1, :]),
              waits=[pe.at()])
        p.add("scalar", lambda e: e.activation(rowbuf[0:1, SLAB:],
                                               rowbuf[0:1, 0:SLAB], AF.Square),
              incs=[fin.inc()])
        p.add("vector", lambda e: e.tensor_sub(rowbuf[0:1, SLAB:], pbS[32:33, :],
                                               rowbuf[0:1, SLAB:]),
              waits=[fin.at()], incs=[fin.inc()])
        p.add("vector", lambda e: e.tensor_scalar_mul(negmu_dst, rowbuf[0:1, 0:SLAB],
                                                      -1.0), incs=[fin.inc()])
        p.add("scalar", lambda e: e.activation(rowbuf[0:1, SLAB:], rowbuf[0:1, SLAB:],
                                               AF.Ln, bias=1e-5), waits=[fin.at()])
        p.add("scalar", lambda e: e.activation(r_dst, rowbuf[0:1, SLAB:], AF.Exp,
                                               scale=-0.5), incs=[fin.inc()])
        if invr_dst is not None:
            p.add("scalar", lambda e: e.activation(invr_dst, rowbuf[0:1, SLAB:],
                                                   AF.Exp, scale=0.5),
                  incs=[fin.inc()])
        if rb_dst is not None:
            p.add("tensor", mm(pbB[:, :], c["cONESROWB"][:], r_dst, True, True),
                  waits=[fin.at()], incs=[pe.inc()])
            p.add("scalar", lambda e: e.copy(rb_dst, pbB[:, :]),
                  waits=[pe.at()], incs=[fin.inc()])

    def load_w_tiles(self, p, src, col0, ncols, wf, wbf, b, ld, wcast, pe_done,
                     pe_need, kchunks=NCHUNK):
        v = src.rearrange("(kc kp) n -> kp kc n", kp=128)
        p.add("sync", dma(wf[:, 0:kchunks], v[:, :, col0:col0 + ncols]),
              waits=[wcast.at()], incs=[ld.inc(16)])
        p.add("vector", lambda e: e.tensor_copy(wbf[:, b, 0:kchunks], wf[:, 0:kchunks]),
              waits=[ld.at(), (pe_done.h, pe_need)], incs=[wcast.inc()])

    def stage_l0a(self, xdr, ydr, last=False):
        nc, pl, c = self.nc, self.plan, self.c
        qsp = self.dram_tmp("sp_a", [D, T], BF16)
        ysp = self.dram_tmp("sp_y", [D, T], BF16)
        wq = self.dram_in("Wq", [D, D])
        wk = self.dram_in("Wk", [D, 512])
        wv = self.dram_in("Wv", [D, 512])
        wo = self.dram_in("Wo", [D, D])
        xbf = c["xbf"]
        ao = c["xbf"]
        frame = pl.mark()
        kp = pl.take("kp", [128, 4, T], BF16)
        vtm = pl.take("vtm", [128, 16, 520], BF16)
        M1 = pl.mark()

        p = Prog2()
        w1row = pl.take("w1row", [1, 24 * 128], BF16)
        ld = self.sem("a_ld"); wcast = self.sem("a_wc"); pe1 = self.sem("a_pe1")
        w1ok = self.sem("a_w1"); gmm = self.sem("a_gmm"); ev = self.sem("a_ev")
        rot = self.sem("a_rot"); rp = self.sem("a_rp"); sp = self.sem("a_sp")
        vt = self.sem("a_vt"); vc = self.sem("a_vc")
        wf = pl.take("wfA", [128, NCHUNK, 128], F32)
        wbf = pl.take("wbA", [128, 2, NCHUNK, 128], BF16)
        qn = pl.take("qnA", [128, 2, SLAB], BF16)
        qo = pl.take("qoA", [128, 2, SLAB], BF16)
        rt = pl.take("rtA", [128, SLAB], BF16)
        vtm_h = vtm.rearrange("p k (h c) -> p k h c", c=65)
        p.add("gpsimd", lambda e: e.memset(vtm_h[:, :, :, 64:65], 1.0),
              incs=[vc.inc()])
        specs = [("q", wq, i, i) for i in range(16)] + \
                [("k", wk, j, 16 + j) for j in range(4)] + \
                [("v", wv, j, 20 + j) for j in range(4)]
        for kind, src, cc, idx in specs:
            b = idx % 2
            self.load_w_tiles(p, src, cc * 128, 128, wf, wbf, b, ld, wcast,
                              gmm, gmm.v - 4)
            for k in range(NCHUNK):
                p.add("tensor", mm(self.pb[5][0:1, 0:128], c["cONES128"][:],
                                   wbf[:, b, k, :], k == 0, k == 15),
                      waits=[wcast.at()] if k == 0 else [],
                      incs=[pe1.inc()] if k == 15 else [])
            p.add("scalar", lambda e, idx=idx: e.copy(
                w1row[0:1, idx * 128:(idx + 1) * 128], self.pb[5][0:1, 0:128]),
                waits=[pe1.at()], incs=[w1ok.inc()])
            for s in range(NSLAB):
                sl = slice(s * SLAB, (s + 1) * SLAB)
                bq = (idx * 4 + s) % 2
                pbp = self.pb[bq]
                for k in range(NCHUNK):
                    p.add("tensor", mm(pbp[:, :], wbf[:, b, k, :],
                                       xbf[:, k, sl], k == 0, False),
                          waits=([(ev.h, ev.v - 1)] if k == 0 else []))
                p.add("tensor", mm(pbp[:, :], w1row[0:1, idx * 128:(idx + 1) * 128],
                                   c["negmu_row"][0:1, sl], False, True),
                      waits=[w1ok.at()], incs=[gmm.inc()])
                if kind in ("q", "k"):
                    p.add("vector", lambda e, pbp=pbp, bq=bq, sl=sl: e.tensor_mul(
                        qn[:, bq], pbp[:, :], c["rb"][:, sl]),
                        waits=[gmm.at()], incs=[ev.inc()])
                    pbr = self.pb[2 + bq]
                    p.add("tensor", mm(pbr[:, :], c["cR128T"][:], qn[:, bq],
                                       True, True),
                          waits=[ev.at(), (rp.h, rp.v - 1)], incs=[rot.inc()])
                    p.add("vector", lambda e, bq=bq, sl=sl: e.tensor_mul(
                        qo[:, bq], qn[:, bq], c["cCOS"][:, sl]),
                        waits=[(sp.h, 16 * (sp.v // 16 - 1))])
                    p.add("vector", lambda e, pbr=pbr, sl=sl: e.tensor_mul(
                        rt[:, :], pbr[:, :], c["cSIN"][:, sl]),
                        waits=[rot.at()])
                    if kind == "q":
                        p.add("vector", lambda e, bq=bq: e.tensor_add(
                            qo[:, bq], qo[:, bq], rt[:, :]), incs=[rp.inc()])
                        p.add("gpsimd", lambda e, cc=cc, sl=sl, bq=bq: e.dma_start(
                            qsp[cc * 128:(cc + 1) * 128, sl], qo[:, bq]),
                            waits=[rp.at()], incs=[sp.inc(16)])
                    else:
                        p.add("vector", lambda e, cc=cc, sl=sl, bq=bq: e.tensor_add(
                            kp[:, cc, sl], qo[:, bq], rt[:, :]), incs=[rp.inc()])
                else:
                    p.add("vector", lambda e, pbp=pbp, bq=bq, sl=sl: e.tensor_mul(
                        qn[:, bq], pbp[:, :], c["rb"][:, sl]),
                        waits=[gmm.at()], incs=[ev.inc()])
                    pbt = self.pb[2 + bq]
                    for j in range(4):
                        p.add("tensor", mm(pbt[:, j * 128:(j + 1) * 128],
                                           qn[:, bq, j * 128:(j + 1) * 128],
                                           c["cID128"][:], True, True),
                              waits=([ev.at(), (vc.h, vc.v - 8), rp.at()]
                                     if j == 0 else []),
                              incs=[vt.inc()] if j == 3 else [])
                    for j in range(4):
                        kt = 4 * s + j
                        p.add("scalar", lambda e, pbt=pbt, j=j, kt=kt, cc=cc: e.copy(
                            vtm[:, kt, 130 * cc:130 * cc + 64],
                            pbt[:, j * 128:j * 128 + 64]),
                            waits=[vt.at()] if j == 0 else [])
                        p.add("scalar", lambda e, pbt=pbt, j=j, kt=kt, cc=cc: e.copy(
                            vtm[:, kt, 130 * cc + 65:130 * cc + 129],
                            pbt[:, j * 128 + 64:j * 128 + 128]),
                            incs=[vc.inc()])
        p.add("gpsimd", None, waits=[sp.at()])
        if getattr(self, "dbg_l0a", False):
            dq = self.sem("dbg_q")
            oq = self.dram_out("o_qsp", [D, T], BF16)
            p.add("gpsimd", dma(oq, qsp[:, :]), waits=[rp.at(), vc.at()],
                  incs=[dq.inc(16)])
            okp = self.dram_out("o_kp", [128, 4, T], BF16)
            p.add("gpsimd", dma(okp, kp[:]), incs=[dq.inc(16)])
            ovt = self.dram_out("o_vtm", [128, 16, 520], BF16)
            p.add("gpsimd", dma(ovt, vtm[:]), incs=[dq.inc(16)])
            orr = self.dram_out("o_rr", [1, T], BF16)
            p.add("gpsimd", dma(orr, c["r_row"][:]), incs=[dq.inc(16)])
            onm = self.dram_out("o_nm", [1, T], BF16)
            p.add("gpsimd", dma(onm, c["negmu_row"][:]), incs=[dq.inc(16)])
            orb = self.dram_out("o_rb", [128, T], F32)
            p.add("gpsimd", dma(orb, c["rb"][:]), incs=[dq.inc(16)])
            oxb = self.dram_out("o_xb", [128, 16, T], BF16)
            p.add("gpsimd", dma(oxb, c["xbf"][:]), incs=[dq.inc(16)])
            p.add("gpsimd", None, waits=[dq.at()])
        p.emit(nc, "l0a_A")
        pl.reset(M1)

        p = Prog2()
        qld = self.sem("b_qld")
        stm = [self.sem(f"b_st{i}") for i in range(4)]
        exs = [self.sem(f"b_ex{i}") for i in range(4)]
        pvs = [self.sem(f"b_pv{i}") for i in range(2)]
        aoe = [self.sem(f"b_ao{i}") for i in range(2)]
        dn = self.sem("b_dn"); rc = self.sem("b_rc")
        bc = [self.sem(f"b_bc{i}") for i in range(2)]
        qsl = pl.take("qslB", [128, NCHUNK, SLAB], BF16)
        pt = pl.take("ptB", [128, 2, 2, SLAB], BF16)
        den = pl.take("denB", [1, 2, SLAB], F32)
        tnn = pl.take("tnB", [1, 2, SLAB], F32)
        rcb = pl.take("rcbB", [128, 2, SLAB], F32)
        qv = qsp.rearrange("(cc p) t -> p cc t", p=128)
        for s in range(NSLAB):
            sl = slice(s * SLAB, (s + 1) * SLAB)
            nkc = 4 * (s + 1)
            p.add("sync", dma(qsl[:], qv[:, :, sl]),
                  waits=[x.at() for x in stm], incs=[qld.inc(16)])
            for cc in range(16):
                j = cc // 4
                for h2, (rl, rh) in enumerate(((0, 64), (64, 128))):
                    kh = j + 4 * h2
                    pba = self.pb[4 + h2]
                    for kc in range(nkc):
                        bank = 2 * h2 + kc % 2
                        pbst = self.pb[bank]
                        diag = kc >= 4 * s
                        w = [qld.at(), exs[bank].at()]
                        p.add("tensor", mm(pbst[:, :],
                                           kp[rl:rh, j, kc * 128:(kc + 1) * 128],
                                           qsl[rl:rh, cc, :], True, not diag),
                              waits=w,
                              incs=[] if diag else [stm[bank].inc()])
                        if diag:
                            p.add("tensor", mm(pbst[:, :], c["cID128"][:],
                                               c["cMASKT"][:, kc - 4 * s, :],
                                               False, True),
                                  incs=[stm[bank].inc()])
                        p.add("scalar",
                              lambda e, pbst=pbst, h2=h2, kc=kc: e.activation(
                                  pt[:, h2, kc % 2, :], pbst[:, :], AF.Exp,
                                  scale=0.125),
                              waits=[stm[bank].at(), (pvs[h2].h, pvs[h2].v - 1)],
                              incs=[exs[bank].inc()])
                        pvw = [exs[bank].at()]
                        if kc == 0:
                            pvw.append(aoe[h2].at())
                        p.add("tensor", mm(pba[0:65, :],
                                           vtm[:, kc, 65 * kh:65 * kh + 65],
                                           pt[:, h2, kc % 2, :], kc == 0,
                                           kc == nkc - 1),
                              waits=pvw, incs=[pvs[h2].inc()])
                    bmm = self.sem(f"b_bm{h2}")
                    p.add("scalar", lambda e, pba=pba, h2=h2: e.activation(
                        den[0:1, h2, :], pba[64:65, :], AF.Ln),
                        waits=[pvs[h2].at(), bmm.at()])
                    p.add("scalar", lambda e, h2=h2: e.activation(
                        den[0:1, h2, :], den[0:1, h2, :], AF.Exp, scale=-1.0),
                        incs=[dn.inc()])
                    p.add("vector", lambda e, pba=pba, h2=h2: e.tensor_mul(
                        tnn[0:1, h2, :], pba[64:65, :], den[0:1, h2, :]),
                        waits=[dn.at()])
                    p.add("vector", lambda e, h2=h2: e.tensor_scalar(
                        tnn[0:1, h2, :], tnn[0:1, h2, :], -1.0, 2.0, ALU.mult,
                        ALU.add))
                    p.add("vector", lambda e, h2=h2: e.tensor_mul(
                        den[0:1, h2, :], den[0:1, h2, :], tnn[0:1, h2, :]),
                        incs=[rc.inc()])
                    pbb = self.pb[6 + h2]
                    p.add("tensor", mm(pbb[:, :], c["cONESROW"][:], den[0:1, h2, :],
                                       True, True), waits=[rc.at(), bc[h2].at()],
                          incs=[bmm.inc()])
                    p.add("scalar", lambda e, pbb=pbb, h2=h2: e.copy(
                        rcb[:, h2, :], pbb[:, :]),
                        waits=[bmm.at(), aoe[h2].at()], incs=[bc[h2].inc()])
                    p.add("vector",
                          lambda e, pba=pba, h2=h2, cc=cc, sl=sl: e.tensor_mul(
                              ao[64 * h2:64 * (h2 + 1), cc, sl], pba[0:64, :],
                              rcb[0:64, h2, :]),
                          waits=[bc[h2].at()], incs=[aoe[h2].inc()])
        if getattr(self, "dbg_l0a", False):
            dq = self.sem("dbg_q")
            oao = self.dram_out("o_ao", [128, 16, T], BF16)
            p.add("gpsimd", dma(oao, ao[:]), waits=[aoe[0].at(), aoe[1].at()],
                  incs=[dq.inc(16)])
            p.add("gpsimd", None, waits=[dq.at()])
        p.emit(nc, "l0a_B")
        pl.reset(M1)

        self.proj_spill_phase(wo, ao, ysp, name="l0a_C")
        pl.reset(M1)
        self.residual_phase(ysp, xdr, ydr, last=last)
        pl.reset(frame)

    def stage_l0f(self, xdr, ydr, last=False):
        nc, pl, c = self.nc, self.plan, self.c
        qsp = self.dram_tmp("sp_a", [D, T], BF16)
        ksp = self.dram_tmp("sp_b", [D, T], BF16)
        vsp = self.dram_tmp("sp_c", [D, T], BF16)
        ysp = self.dram_tmp("sp_y", [D, T], BF16)
        waqk = self.dram_in("WaQK", [D, 4096])
        wav = self.dram_in("WaV", [D, D])
        wp = self.dram_in("Wp", [D, D])
        xbf = c["xbf"]
        ao = c["xbf"]
        frame = pl.mark()
        M1 = pl.mark()

        p = Prog2()
        w1row = pl.take("w1rF", [1, 48 * 128], BF16)
        ld = self.sem("a_ld"); wcast = self.sem("a_wc"); pe1 = self.sem("a_pe1")
        w1ok = self.sem("a_w1"); gmm = self.sem("a_gmm"); ev = self.sem("a_ev")
        bl = self.sem("f_bl"); sp = self.sem("a_sp")
        vt = self.sem("a_vt"); vc = self.sem("a_vc"); vs = self.sem("f_vs")
        wf = pl.take("wfF", [128, NCHUNK, 128], F32)
        wbf = pl.take("wbF", [128, 2, NCHUNK, 128], BF16)
        qtmp = pl.take("qtF", [128, NSLAB, SLAB], BF16)
        ktmp = pl.take("ktF", [128, 2, SLAB], BF16)
        dtmp = pl.take("dtF", [128, SLAB], BF16)
        qf = pl.take("qfF", [128, 2, SLAB], BF16)
        kf = pl.take("kfF", [128, 2, SLAB], BF16)
        vn = pl.take("vnF", [128, 2, SLAB], BF16)
        vsta = pl.take("vstF", [128, 2, 4, 128], BF16)
        vspv = vsp.rearrange("(kc p) d -> p kc d", p=128)
        for h in range(16):
            for qk in range(2):
                idx = 2 * h + qk
                b = idx % 2
                self.load_w_tiles(p, waqk, idx * 128, 128, wf, wbf, b, ld, wcast,
                                  gmm, gmm.v - 4)
                for k in range(NCHUNK):
                    p.add("tensor", mm(self.pb[5][0:1, 0:128], c["cONES128"][:],
                                       wbf[:, b, k, :], k == 0, k == 15),
                          waits=[wcast.at()] if k == 0 else [],
                          incs=[pe1.inc()] if k == 15 else [])
                p.add("scalar", lambda e, idx=idx: e.copy(
                    w1row[0:1, idx * 128:(idx + 1) * 128], self.pb[5][0:1, 0:128]),
                    waits=[pe1.at()], incs=[w1ok.inc()])
                for s in range(NSLAB):
                    sl = slice(s * SLAB, (s + 1) * SLAB)
                    bq = (idx * 4 + s) % 2
                    pbp = self.pb[bq]
                    for k in range(NCHUNK):
                        p.add("tensor", mm(pbp[:, :], wbf[:, b, k, :],
                                           xbf[:, k, sl], k == 0, False),
                              waits=([(ev.h, ev.v - 1)] if k == 0 else []))
                    p.add("tensor", mm(pbp[:, :],
                                       w1row[0:1, idx * 128:(idx + 1) * 128],
                                       c["negmu_row"][0:1, sl], False, True),
                          waits=[w1ok.at()], incs=[gmm.inc()])
                    if qk == 0:
                        p.add("vector",
                              lambda e, pbp=pbp, s=s, h=h, sl=sl: e.scalar_tensor_tensor(
                                  qtmp[:, s, :], pbp[:, :], c["CSK"][:, h:h + 1],
                                  c["rb"][:, sl], ALU.mult, ALU.mult),
                              waits=[gmm.at(), bl.at()], incs=[ev.inc()])
                    else:
                        bk = s % 2
                        p.add("vector",
                              lambda e, pbp=pbp, bk=bk, h=h, sl=sl: e.scalar_tensor_tensor(
                                  ktmp[:, bk, :], pbp[:, :], c["CSK"][:, h:h + 1],
                                  c["rb"][:, sl], ALU.mult, ALU.mult),
                              waits=[gmm.at(), (sp.h, 16 * (sp.v // 16 - 2))],
                              incs=[ev.inc()])
                        p.add("vector", lambda e, bk=bk, s=s: e.tensor_sub(
                            dtmp[:, :], ktmp[:, bk, :], qtmp[:, s, :]))
                        p.add("vector", lambda e, h=h: e.tensor_scalar(
                            dtmp[:, :], dtmp[:, :], c["BM"][:, h:h + 1], None,
                            ALU.mult))
                        p.add("vector", lambda e, bk=bk, s=s: e.tensor_add(
                            qf[:, bk, :], qtmp[:, s, :], dtmp[:, :]))
                        p.add("vector", lambda e, bk=bk: e.tensor_sub(
                            kf[:, bk, :], ktmp[:, bk, :], dtmp[:, :]),
                            incs=[bl.inc()])
                        p.add("gpsimd", lambda e, h=h, sl=sl, bk=bk: e.dma_start(
                            qsp[h * 128:(h + 1) * 128, sl], qf[:, bk, :]),
                            waits=[bl.at()], incs=[sp.inc(16)])
                        p.add("gpsimd", lambda e, h=h, sl=sl, bk=bk: e.dma_start(
                            ksp[h * 128:(h + 1) * 128, sl], kf[:, bk, :]),
                            incs=[sp.inc(16)])
        for cc in range(16):
            idx = 32 + cc
            b = idx % 2
            self.load_w_tiles(p, wav, cc * 128, 128, wf, wbf, b, ld, wcast,
                              gmm, gmm.v - 4)
            for k in range(NCHUNK):
                p.add("tensor", mm(self.pb[5][0:1, 0:128], c["cONES128"][:],
                                   wbf[:, b, k, :], k == 0, k == 15),
                      waits=[wcast.at()] if k == 0 else [],
                      incs=[pe1.inc()] if k == 15 else [])
            p.add("scalar", lambda e, idx=idx: e.copy(
                w1row[0:1, idx * 128:(idx + 1) * 128], self.pb[5][0:1, 0:128]),
                waits=[pe1.at()], incs=[w1ok.inc()])
            for s in range(NSLAB):
                sl = slice(s * SLAB, (s + 1) * SLAB)
                bq = (idx * 4 + s) % 2
                pbp = self.pb[bq]
                for k in range(NCHUNK):
                    p.add("tensor", mm(pbp[:, :], wbf[:, b, k, :],
                                       xbf[:, k, sl], k == 0, False),
                          waits=([(ev.h, ev.v - 1)] if k == 0 else []))
                p.add("tensor", mm(pbp[:, :], w1row[0:1, idx * 128:(idx + 1) * 128],
                                   c["negmu_row"][0:1, sl], False, True),
                      waits=[w1ok.at()], incs=[gmm.inc()])
                p.add("vector", lambda e, pbp=pbp, bq=bq, sl=sl: e.tensor_mul(
                    vn[:, bq], pbp[:, :], c["rb"][:, sl]),
                    waits=[gmm.at()], incs=[ev.inc()])
                pbt = self.pb[2 + bq]
                for j in range(4):
                    p.add("tensor", mm(pbt[:, j * 128:(j + 1) * 128],
                                       vn[:, bq, j * 128:(j + 1) * 128],
                                       c["cID128"][:], True, True),
                          waits=([ev.at(), (vc.h, vc.v - 4)] if j == 0 else []),
                          incs=[vt.inc()] if j == 3 else [])
                for j in range(4):
                    p.add("scalar", lambda e, pbt=pbt, j=j, bq=bq: e.copy(
                        vsta[:, bq, j, :], pbt[:, j * 128:(j + 1) * 128]),
                        waits=([vt.at(), (vs.h, 16 * (vs.v // 16 - 1))]
                               if j == 0 else []),
                        incs=[vc.inc()])
                p.add("gpsimd", lambda e, s=s, cc=cc, bq=bq: e.dma_start(
                    vspv[:, 4 * s:4 * s + 4, cc * 128:(cc + 1) * 128],
                    vsta[:, bq, :, :]), waits=[vc.at()], incs=[vs.inc(16)])
        p.add("gpsimd", None, waits=[sp.at(), vs.at()])
        p.emit(nc, "l0f_A")
        pl.reset(M1)

        p = Prog2()
        qld = self.sem("g_qld"); kld = self.sem("g_kld")
        vld2 = [self.sem("g_vld0"), self.sem("g_vld1")]
        stm = [self.sem(f"g_st{i}") for i in range(2)]
        exs = [self.sem(f"g_ex{i}") for i in range(2)]
        pvs = self.sem("g_pv"); ape = self.sem("g_ape")
        dnm = self.sem("g_dnm"); dne = self.sem("g_dne")
        rc = self.sem("g_rc"); bmm = self.sem("g_bmm"); bc = self.sem("g_bc")
        aoe = self.sem("g_aoe")
        qsl = pl.take("qslG", [128, NCHUNK, SLAB], BF16)
        kfsl = pl.take("kslG", [128, NCHUNK, SLAB], BF16)
        vsl = pl.take("vslG", [128, 2, 4, D], BF16)
        aopart = pl.take("aopG", [128, 8, SLAB], F32)
        denpart = pl.take("denG", [1, 8 * SLAB], F32)
        pt = pl.take("ptG", [128, 2, SLAB], BF16)
        dnr = pl.take("dnrG", [1, SLAB], F32)
        tnn = pl.take("tnG", [1, SLAB], F32)
        rcb = pl.take("rcbG", [128, SLAB], F32)
        qv = qsp.rearrange("(cc p) t -> p cc t", p=128)
        kv_ = ksp.rearrange("(cc p) t -> p cc t", p=128)
        vv_ = vsp.rearrange("(kc p) d -> p kc d", p=128)
        sc128 = 1.0 / math.sqrt(128.0)
        vsl_last = [0, 0]
        nblk = 0
        for s in range(NSLAB):
            sl = slice(s * SLAB, (s + 1) * SLAB)
            p.add("sync", dma(qsl[:], qv[:, :, sl]),
                  waits=[x.at() for x in stm], incs=[qld.inc(16)])
            for hg in range(2):
                for kk in range(s + 1):
                    bv = nblk % 2
                    nblk += 1
                    p.add("sync", dma(kfsl[:], kv_[:, :, kk * SLAB:(kk + 1) * SLAB]),
                          waits=[x.at() for x in stm], incs=[kld.inc(16)])
                    p.add("sync", dma(vsl[:, bv], vv_[:, 4 * kk:4 * kk + 4, :]),
                          waits=[(pvs.h, vsl_last[bv])], incs=[vld2[bv].inc(16)])
                    vsl_last[bv] = pvs.v + 32
                    for hh in range(8):
                        h = 8 * hg + hh
                        pba = self.pb[4 + hh % 2]
                        dsl = slice(hh * SLAB, (hh + 1) * SLAB)
                        for kcl in range(4):
                            bank = (hh * 4 + kcl) % 2
                            pbst = self.pb[bank]
                            diag = kk == s
                            p.add("tensor", mm(pbst[:, :],
                                               kfsl[:, h, kcl * 128:(kcl + 1) * 128],
                                               qsl[:, h, :], True, not diag),
                                  waits=[qld.at(), kld.at(), exs[bank].at()],
                                  incs=[] if diag else [stm[bank].inc()])
                            if diag:
                                p.add("tensor", mm(pbst[:, :], c["cID128"][:],
                                                   c["cMASKT"][:, kcl, :],
                                                   False, True),
                                      incs=[stm[bank].inc()])
                            p.add("scalar",
                                  lambda e, pbst=pbst, bank=bank: e.activation(
                                      pt[:, bank, :], pbst[:, :], AF.Exp,
                                      scale=sc128),
                                  waits=[stm[bank].at(), (pvs.h, pvs.v - 1)],
                                  incs=[exs[bank].inc()])
                            p.add("tensor", mm(self.pb[6][0:1, :], c["cONES128"][:],
                                               pt[:, bank, :], kcl == 0, kcl == 3),
                                  waits=[exs[bank].at()] +
                                        ([ape.at()] if kcl == 0 else []))
                            p.add("tensor", mm(pba[:, :],
                                               vsl[:, bv, kcl, h * 128:(h + 1) * 128],
                                               pt[:, bank, :], kcl == 0, kcl == 3),
                                  waits=[vld2[bv].at()], incs=[pvs.inc()])
                        if kk == 0:
                            p.add("scalar", lambda e, pba=pba, hh=hh: e.copy(
                                aopart[:, hh, :], pba[:, :]),
                                waits=[pvs.at(), aoe.at()], incs=[ape.inc()])
                            p.add("scalar", lambda e, dsl=dsl: e.copy(
                                denpart[0:1, dsl], self.pb[6][0:1, :]),
                                incs=[ape.inc()])
                        else:
                            p.add("vector", lambda e, pba=pba, hh=hh: e.tensor_add(
                                aopart[:, hh, :], aopart[:, hh, :], pba[:, :]),
                                waits=[pvs.at()], incs=[ape.inc()])
                            p.add("vector", lambda e, dsl=dsl: e.tensor_add(
                                denpart[0:1, dsl], denpart[0:1, dsl],
                                self.pb[6][0:1, :]),
                                waits=[pvs.at()], incs=[ape.inc()])
                for hh in range(8):
                    h = 8 * hg + hh
                    dsl = slice(hh * SLAB, (hh + 1) * SLAB)
                    p.add("scalar", lambda e, dsl=dsl: e.activation(
                        dnr[0:1, :], denpart[0:1, dsl], AF.Ln),
                        waits=[ape.at(), bmm.at()])
                    p.add("scalar", lambda e: e.activation(
                        dnr[0:1, :], dnr[0:1, :], AF.Exp, scale=-1.0),
                        incs=[dne.inc()])
                    p.add("vector", lambda e, dsl=dsl: e.tensor_mul(
                        tnn[0:1, :], denpart[0:1, dsl], dnr[0:1, :]),
                        waits=[dne.at()])
                    p.add("vector", lambda e: e.tensor_scalar(
                        tnn[0:1, :], tnn[0:1, :], -1.0, 2.0, ALU.mult, ALU.add))
                    p.add("vector", lambda e: e.tensor_mul(
                        dnr[0:1, :], dnr[0:1, :], tnn[0:1, :]), incs=[rc.inc()])
                    p.add("tensor", mm(self.pb[7][:, :], c["cONESROW"][:],
                                       dnr[0:1, :], True, True),
                          waits=[rc.at(), bc.at()], incs=[bmm.inc()])
                    p.add("scalar", lambda e: e.copy(rcb[:, :], self.pb[7][:, :]),
                          waits=[bmm.at(), aoe.at()], incs=[bc.inc()])
                    p.add("vector", lambda e, h=h, hh=hh, sl=sl: e.tensor_mul(
                        ao[:, h, sl], aopart[:, hh, :], rcb[:, :]),
                        waits=[bc.at()], incs=[aoe.inc()])
        p.emit(nc, "l0f_B")
        pl.reset(M1)

        self.proj_spill_phase(wp, ao, ysp)
        pl.reset(M1)
        self.residual_phase(ysp, xdr, ydr, last=last)
        pl.reset(frame)

    def proj_spill_phase(self, wdr, ao, ysp, name="projC"):
        nc, pl, c = self.nc, self.plan, self.c
        p = Prog2()
        M = pl.mark()
        ld = self.sem("c_ld"); wcast = self.sem("c_wc"); gmm = self.sem("c_gmm")
        yev = self.sem("c_yev"); csp = self.sem("c_sp")
        wf = pl.take("wfC", [128, NCHUNK, 128], F32)
        wbf = pl.take("wbC", [128, 2, NCHUNK, 128], BF16)
        yb = pl.take("ybC", [128, 2, SLAB], BF16)
        for c2 in range(16):
            b = c2 % 2
            self.load_w_tiles(p, wdr, c2 * 128, 128, wf, wbf, b, ld, wcast,
                              gmm, gmm.v - 4)
            for s in range(NSLAB):
                sl = slice(s * SLAB, (s + 1) * SLAB)
                bq = (c2 * 4 + s) % 2
                pbp = self.pb[bq]
                for k in range(NCHUNK):
                    p.add("tensor", mm(pbp[:, :], wbf[:, b, k, :], ao[:, k, sl],
                                       k == 0, k == 15),
                          waits=([wcast.at(), (yev.h, yev.v - 1)] if k == 0 else []),
                          incs=[gmm.inc()] if k == 15 else [])
                p.add("scalar", lambda e, pbp=pbp, bq=bq: e.copy(yb[:, bq], pbp[:, :]),
                      waits=[gmm.at(), (csp.h, csp.v - 16)], incs=[yev.inc()])
                p.add("gpsimd", lambda e, c2=c2, sl=sl, bq=bq: e.dma_start(
                    ysp[c2 * 128:(c2 + 1) * 128, sl], yb[:, bq]),
                    waits=[yev.at()], incs=[csp.inc(16)])
        p.add("gpsimd", None, waits=[csp.at()])
        p.emit(nc, name)
        pl.reset(M)

    def stage_l1a(self, xdr, ydr, last=False):
        nc, pl, c = self.nc, self.plan, self.c
        ysp = self.dram_tmp("sp_y", [D, T], BF16)
        cwt = self.dram_in("CwT", [D, D])
        dwt = self.dram_in("DwT", [D, D])
        cb = self.dram_in("Cb", [1, D])
        db = self.dram_in("Db", [1, D])
        xbf = c["xbf"]
        frame = pl.mark()
        p = Prog2()
        ld = self.sem("a_ld"); wcast = self.sem("a_wc"); w1ok = self.sem("a_w1")
        pe1 = self.sem("a_pe1"); gmm = self.sem("a_gmm"); ev = self.sem("a_ev")
        xpd = self.sem("h_xp"); sc_ = self.sem("h_sc"); sp = self.sem("a_sp")
        sg = self.sem("h_sg")
        w1row = pl.take("w1rH", [1, 32 * 128], BF16)
        cbrow = pl.take("cbH", [1, D], BF16)
        dbrow = pl.take("dbH", [1, D], BF16)
        cbf = pl.take("cbfH", [1, D], F32)
        wf = pl.take("wfH", [128, NCHUNK, 128], F32)
        wc_ = pl.take("wcH", [128, 2, NCHUNK, 128], BF16)
        wd_ = pl.take("wdH", [128, 2, NCHUNK, 128], BF16)
        uf = pl.take("ufH", [128, SLAB], F32)
        gs = pl.take("gsH", [128, SLAB], F32)
        xp = pl.take("xpH", [128, T], F32)
        ysc = pl.take("yscH", [128, T], F32)
        ybo = pl.take("yboH", [128, 2, T], BF16)
        cbl = self.sem("h_cbl")
        p.add("sync", dma(cbf[:], cb), incs=[cbl.inc(16)])
        p.add("vector", lambda e: e.tensor_copy(cbrow[:], cbf[:]),
              waits=[cbl.at()], incs=[ev.inc()])
        p.add("sync", dma(cbf[:], db), waits=[ev.at()], incs=[cbl.inc(16)])
        p.add("vector", lambda e: e.tensor_copy(dbrow[:], cbf[:]),
              waits=[cbl.at()], incs=[ev.inc()])
        bias_rdy = (ev.h, ev.v)
        for cc in range(NCHUNK):
            for wi, (src, dst) in enumerate(((cwt, wc_), (dwt, wd_))):
                idx = 2 * cc + wi
                b = cc % 2
                v = src.rearrange("(kc kp) n -> kp kc n", kp=128)
                p.add("sync", dma(wf[:], v[:, :, cc * 128:(cc + 1) * 128]),
                      waits=[wcast.at()], incs=[ld.inc(16)])
                p.add("vector", lambda e, dst=dst, b=b: e.tensor_copy(
                    dst[:, b], wf[:]),
                    waits=[ld.at(), (gmm.h, gmm.v - 8)], incs=[wcast.inc()])
                for k in range(NCHUNK):
                    p.add("tensor", mm(self.pb[5][0:1, 0:128], c["cONES128"][:],
                                       dst[:, b, k, :], k == 0, k == 15),
                          waits=[wcast.at()] if k == 0 else [],
                          incs=[pe1.inc()] if k == 15 else [])
                p.add("scalar", lambda e, idx=idx: e.copy(
                    w1row[0:1, idx * 128:(idx + 1) * 128], self.pb[5][0:1, 0:128]),
                    waits=[pe1.at()], incs=[w1ok.inc()])
            b = cc % 2
            b2 = cc % 2
            for s in range(NSLAB):
                sl = slice(s * SLAB, (s + 1) * SLAB)
                pbu = self.pb[s % 2]
                pbg = self.pb[2 + s % 2]
                for k in range(NCHUNK):
                    p.add("tensor", mm(pbu[:, :], wc_[:, b, k, :], xbf[:, k, sl],
                                       k == 0, False),
                          waits=([(ev.h, ev.v - 3)] if k == 0 else []))
                p.add("tensor", mm(pbu[:, :], w1row[0:1, 2 * cc * 128:
                                                    (2 * cc + 1) * 128],
                                   c["negmu_row"][0:1, sl], False, False),
                      waits=[w1ok.at()])
                p.add("tensor", mm(pbu[:, :], cbrow[0:1, cc * 128:(cc + 1) * 128],
                                   c["invr_row"][0:1, sl], False, True),
                      waits=[bias_rdy], incs=[gmm.inc()])
                for k in range(NCHUNK):
                    p.add("tensor", mm(pbg[:, :], wd_[:, b2, k, :], xbf[:, k, sl],
                                       k == 0, False),
                          waits=([(sg.h, sg.v - 1)] if k == 0 else []))
                p.add("tensor", mm(pbg[:, :], w1row[0:1, (2 * cc + 1) * 128:
                                                    (2 * cc + 2) * 128],
                                   c["negmu_row"][0:1, sl], False, False),
                      waits=[w1ok.at()])
                p.add("tensor", mm(pbg[:, :], dbrow[0:1, cc * 128:(cc + 1) * 128],
                                   c["invr_row"][0:1, sl], False, True),
                      waits=[bias_rdy], incs=[gmm.inc()])
                p.add("vector", lambda e, pbu=pbu, sl=sl: e.tensor_mul(
                    uf[:, :], pbu[:, :], c["rb"][:, sl]),
                    waits=[gmm.at(), xpd.at()], incs=[ev.inc()])
                p.add("vector", lambda e, pbg=pbg, sl=sl: e.tensor_mul(
                    gs[:, :], pbg[:, :], c["rb"][:, sl]),
                    waits=[sg.at()], incs=[ev.inc()])
                p.add("scalar", lambda e: e.activation(gs[:, :], gs[:, :],
                                                       AF.Sigmoid),
                      waits=[ev.at()], incs=[sg.inc()])
                p.add("vector", lambda e, sl=sl: e.tensor_mul(
                    xp[:, sl], uf[:, :], gs[:, :]),
                    waits=[sg.at()], incs=[xpd.inc()])
            p.add("vector", lambda e, cc=cc: e.tensor_tensor_scan(
                ysc[:, :], c["ADIAG"][:, cc:cc + 1].broadcast_to([128, T]),
                xp[:, :], 0.0, ALU.mult, ALU.add),
                waits=[xpd.at(), (sc_.h, sc_.v - 1)], incs=[sc_.inc()])
            bo = cc % 2
            p.add("vector", lambda e, cc=cc, bo=bo: e.tensor_scalar(
                ybo[:, bo, :], ysc[:, :], c["AB"][:, cc:cc + 1], None, ALU.mult),
                waits=[(sp.h, 16 * (sp.v // 16 - 1))], incs=[sc_.inc()])
            p.add("gpsimd", lambda e, cc=cc, bo=bo: e.dma_start(
                ysp[cc * 128:(cc + 1) * 128, :], ybo[:, bo, :]),
                waits=[sc_.at()], incs=[sp.inc(16)])
        p.add("gpsimd", None, waits=[sp.at()])
        p.emit(nc, "l1a_A")
        pl.reset(frame)
        self.residual_phase(ysp, xdr, ydr, last=last)
        pl.reset(frame)

    def stage_ghost(self):
        nc, pl, c = self.nc, self.plan, self.c
        gi = self.dram_in("gidxT", [D, INTER], I32)
        ui = self.dram_in("uidxT", [D, INTER], I32)
        di = self.dram_in("didxT", [INTER, D], I32)
        gw = self.dram_tmp("gw", [D, INTER], BF16)
        uw = self.dram_tmp("uw", [D, INTER], BF16)
        dw = self.dram_tmp("dw", [INTER, D], BF16)
        w1g = self.dram_tmp("w1g", [1, INTER], F32)
        w1u = self.dram_tmp("w1u", [1, INTER], F32)
        frame = pl.mark()
        p = Prog2()
        cst = self.sem("gh_cs"); acc = self.sem("gh_ac")
        pe = self.sem("gh_pe"); ws = self.sem("gh_ws"); sp = self.sem("gh_sp")
        ldg = self.sem("gh_ld0")
        idxt = pl.take("idxt", [128, INTER], I32)
        idxf = pl.take("idxf", [128, 2, INTER], BF16)
        accb = pl.take("accb", [128, 2, INTER], BF16)
        tb = pl.take("tb", [128, 2, INTER], BF16)
        w1acc = pl.take("w1acc", [1, INTER], F32)
        ti = 0
        for src_, dst, w1p, nr, ncols in ((gi, gw, w1g, 16, INTER),
                                          (ui, uw, w1u, 16, INTER),
                                          (di, dw, None, ICHUNK, D)):
            for rc in range(nr):
                b = ti % 2
                eng = "vector"
                p.add("sync", dma(idxt[:, 0:ncols],
                                  src_[rc * 128:(rc + 1) * 128, :]),
                      waits=[cst.at()], incs=[ldg.inc(16)])
                p.add(eng, lambda e, b=b, ncols=ncols: e.tensor_copy(
                    idxf[:, b, 0:ncols], idxt[:, 0:ncols]),
                    waits=[ldg.at(), (acc.h, acc.v - 1)], incs=[cst.inc()])
                for j in range(16):
                    if j == 0:
                        p.add(eng, lambda e, b=b, ncols=ncols, j=j: e.tensor_scalar(
                            accb[:, b, 0:ncols], idxf[:, b, 0:ncols], float(j),
                            c["LUTB"][:, j:j + 1], ALU.is_equal, ALU.mult),
                            waits=[(sp.h, 16 * (sp.v // 16 - 1)),
                                   (pe.h, pe.v - 11)])
                    else:
                        p.add(eng, lambda e, b=b, ncols=ncols, j=j: e.tensor_scalar(
                            tb[:, b, 0:ncols], idxf[:, b, 0:ncols], float(j),
                            c["LUTB"][:, j:j + 1], ALU.is_equal, ALU.mult))
                        p.add(eng, lambda e, b=b, ncols=ncols: e.tensor_tensor(
                            accb[:, b, 0:ncols], accb[:, b, 0:ncols],
                            tb[:, b, 0:ncols], ALU.add),
                            incs=[acc.inc()] if j == 15 else [])
                p.add("gpsimd", lambda e, dst=dst, rc=rc, b=b, ncols=ncols:
                      e.dma_start(dst[rc * 128:(rc + 1) * 128, :],
                                  accb[:, b, 0:ncols]),
                      waits=[acc.at()], incs=[sp.inc(16)])
                if w1p is not None:
                    for bi in range(11):
                        c0, c1 = bi * 512, min((bi + 1) * 512, ncols)
                        p.add("tensor", mm(self.pb[0][0:1, 0:c1 - c0],
                                           c["cONES128"][:],
                                           accb[:, b, c0:c1], True, True),
                              waits=[acc.at(), ws.at()], incs=[pe.inc()])
                        if rc == 0:
                            p.add("scalar", lambda e, c0=c0, c1=c1:
                                  e.copy(w1acc[0:1, c0:c1],
                                         self.pb[0][0:1, 0:c1 - c0]),
                                  waits=[pe.at()], incs=[ws.inc()])
                        else:
                            p.add("vector", lambda e, c0=c0, c1=c1:
                                  e.tensor_add(w1acc[0:1, c0:c1],
                                               w1acc[0:1, c0:c1],
                                               self.pb[0][0:1, 0:c1 - c0]),
                                  waits=[pe.at()], incs=[ws.inc()])
                if w1p is not None and rc == nr - 1:
                    p.add("gpsimd", dma(w1p, w1acc[0:1, :]), waits=[ws.at()],
                          incs=[sp.inc(16)])
                    p.add("scalar", None, waits=[(sp.h, sp.v)])
                ti += 1
        p.add("gpsimd", None, waits=[sp.at()])
        p.emit(nc, "ghost")
        pl.reset(frame)

    def stage_l1f(self, xdr, ydr, last=True, dbg=False):
        nc, pl, c = self.nc, self.plan, self.c
        gw = self.d["gw"]
        uw = self.d["uw"]
        dw = self.d["dw"]
        w1gd = self.d["w1g"]
        w1ud = self.d["w1u"]
        xbf = c["xbf"]
        frame = pl.mark()
        w1g = pl.take("w1gL", [1, INTER], BF16)
        w1u = pl.take("w1uL", [1, INTER], BF16)
        Mw = pl.mark()
        w1f = pl.take("w1fL", [1, INTER], F32)
        pld = Prog2()
        ld0 = self.sem("l_ld0"); cv0 = self.sem("l_cv0")
        pld.add("sync", dma(w1f[:], w1gd), incs=[ld0.inc(16)])
        pld.add("vector", lambda e: e.tensor_copy(w1g[:], w1f[:]),
                waits=[ld0.at()], incs=[cv0.inc()])
        pld.add("sync", dma(w1f[:], w1ud), waits=[cv0.at()], incs=[ld0.inc(16)])
        pld.add("vector", lambda e: e.tensor_copy(w1u[:], w1f[:]),
                waits=[ld0.at()], incs=[cv0.inc()])
        pld.emit(nc, "l1f_w1")
        pl.reset(Mw)
        M1 = pl.mark()
        gv = gw.rearrange("(kc kp) n -> kp kc n", kp=128)
        uv = uw.rearrange("(kc kp) n -> kp kc n", kp=128)
        dv_ = dw.rearrange("(kc kp) n -> kp kc n", kp=128)
        xv = xdr.rearrange("(cc p) t -> p cc t", p=128)
        ov = ydr.rearrange("(cc p) t -> p cc t", p=128)
        for s in range(NSLAB):
            sl = slice(s * SLAB, (s + 1) * SLAB)
            p = Prog2()
            ld = self.sem("l_ld"); gmm = self.sem("l_gmm"); ev = self.sem("l_ev")
            sil = self.sem("l_sil"); hm = self.sem("l_hm")
            ybsl = pl.take("ybfL", [128, NCHUNK, SLAB], BF16)
            hmid = pl.take("hmidL", [128, ICHUNK, SLAB], BF16)
            Mh = pl.mark()
            wgt = pl.take("wgtL", [128, 2, NCHUNK, 128], BF16)
            wut = pl.take("wutL", [128, 2, NCHUNK, 128], BF16)
            gf = pl.take("gfL", [128, SLAB], F32)
            sgb = pl.take("sgbL", [128, SLAB], BF16)
            ubf = pl.take("ubfL", [128, SLAB], BF16)
            ld2 = [self.sem("l_lda"), self.sem("l_ldb")]
            for ic in range(ICHUNK):
                b = ic % 2
                p.add("sync", dma(wgt[:, b], gv[:, :, ic * 128:(ic + 1) * 128]),
                      waits=[(gmm.h, gmm.v - 2)], incs=[ld2[b].inc(16)])
                p.add("sync", dma(wut[:, b], uv[:, :, ic * 128:(ic + 1) * 128]),
                      incs=[ld2[b].inc(16)])
                pbg = self.pb[b]
                pbu = self.pb[2 + b]
                for k in range(NCHUNK):
                    p.add("tensor", mm(pbg[:, :], wgt[:, b, k, :], xbf[:, k, sl],
                                       k == 0, False),
                          waits=([ld2[b].at(), (ev.h, ev.v - 2)] if k == 0 else []))
                p.add("tensor", mm(pbg[:, :], w1g[0:1, ic * 128:(ic + 1) * 128],
                                   c["negmu_row"][0:1, sl], False, True),
                      incs=[gmm.inc()])
                for k in range(NCHUNK):
                    p.add("tensor", mm(pbu[:, :], wut[:, b, k, :], xbf[:, k, sl],
                                       k == 0, False),
                          waits=([ld2[b].at(), (sil.h, sil.v - 1)]
                                 if k == 0 else []))
                p.add("tensor", mm(pbu[:, :], w1u[0:1, ic * 128:(ic + 1) * 128],
                                   c["negmu_row"][0:1, sl], False, True),
                      incs=[gmm.inc()])
                p.add("vector", lambda e, pbg=pbg, sl=sl: e.tensor_mul(
                    gf[:, :], pbg[:, :], c["rb"][:, sl]),
                    waits=[gmm.at(), sil.at()], incs=[ev.inc()])
                p.add("scalar", lambda e: e.activation(sgb[:, :], gf[:, :],
                                                       AF.Silu),
                      waits=[ev.at(), hm.at()], incs=[sil.inc()])
                p.add("vector", lambda e, pbu=pbu, sl=sl: e.tensor_mul(
                    ubf[:, :], pbu[:, :], c["rb"][:, sl]),
                    waits=[(hm.h, hm.v - 1)], incs=[ev.inc()])
                p.add("vector", lambda e, ic=ic: e.tensor_mul(
                    hmid[:, ic, :], sgb[:, :], ubf[:, :]),
                    waits=[sil.at()], incs=[hm.inc()])
                if dbg:
                    hdbg = self.dram_tmp("hdbg", [INTER, T], BF16)
                    p.add("gpsimd", lambda e, ic=ic, sl=sl: e.dma_start(
                        hdbg[ic * 128:(ic + 1) * 128, sl], hmid[:, ic, :]),
                        waits=[hm.at()],
                        incs=[self.sem("l_dbg").inc(16)])
            pl.reset(Mh)
            KH = 22
            wdt = pl.take("wdtL", [128, 2, KH, 128], BF16)
            dgm = self.sem("l_dgm"); dev = self.sem("l_dev")
            dev0 = dev.v
            half_last = getattr(self, "_l1f_half_last", [0, 0])
            for oc in range(NCHUNK):
                pbp = self.pb[4 + oc % 2]
                for half in range(2):
                    k0 = half * KH
                    kn = min(KH, ICHUNK - k0)
                    dld = self.sem(f"l_dld{half}")
                    dw_waits = [(dgm.h, half_last[half])]
                    if oc == 0 and half == 0:
                        dw_waits += [gmm.at(), hm.at()]
                    p.add("sync", dma(wdt[:, half, 0:kn],
                                      dv_[:, k0:k0 + kn, oc * 128:(oc + 1) * 128]),
                          waits=dw_waits, incs=[dld.inc(16)])
                    for kq in range(kn):
                        k = k0 + kq
                        p.add("tensor", mm(pbp[:, :], wdt[:, half, kq, :],
                                           hmid[:, k, :],
                                           k == 0, k == ICHUNK - 1),
                              waits=([dld.at(), hm.at(), (dev.h, dev.v - 1)]
                                     if k == 0 else ([dld.at()] if kq == 0 else [])),
                              incs=[dgm.inc()] if kq == kn - 1 else [])
                    half_last[half] = dgm.v
                self._l1f_half_last = half_last
                p.add("scalar", lambda e, pbp=pbp, oc=oc: e.copy(
                    ybsl[:, oc, :], pbp[:, :]),
                    waits=[dgm.at()], incs=[dev.inc()])
                if dbg:
                    ydbg = self.dram_tmp("ydbg", [D, T], BF16)
                    p.add("gpsimd", lambda e, oc=oc, sl=sl: e.dma_start(
                        ydbg[oc * 128:(oc + 1) * 128, sl], ybsl[:, oc, :]),
                        waits=[dev.at()],
                        incs=[self.sem("l_dbg").inc(16)])
            yready = [(dev.h, dev0 + k + 1) for k in range(NCHUNK)]
            self.residual_slab(p, ybsl, s, xv, ov, last=last, yready=yready)
            p.emit(nc, f"l1f_s{s}")
            pl.reset(M1)
        pl.reset(frame)

    def residual_slab(self, p, ybsl, s, xv, ov, last, yready=None):
        nc, pl, c = self.nc, self.plan, self.c
        M = pl.mark()
        sq = self.sem("d_sq"); pe = self.sem("d_pe"); fin = self.sem("d_fin")
        xld = self.sem("d_xld"); st = self.sem("d_st"); xa = self.sem("d_xa")
        xc = self.sem("d_xc"); nxt = self.sem("d_nxt")
        ysq = pl.take("ysqS", [128, 1, SLAB], BF16)
        yrow = pl.take("yrowS", [1, 2 * SLAB], F32)
        rowbuf = yrow
        aysb = pl.take("ayS", [128, SLAB], F32)
        bysb = pl.take("byS", [128, SLAB], F32)
        xf = pl.take("xfS", [128, 1, SLAB], F32)
        xo = pl.take("xoS", [128, 1, SLAB], F32)
        xsq = pl.take("xsqS", [128, 1, SLAB], BF16)
        pbS, pbN, pbB = self.pb[6], self.pb[7], self.pb[6]
        sl = slice(s * SLAB, (s + 1) * SLAB)
        for k in range(NCHUNK):
            b = 0
            yw = [yready[k]] if yready else []
            p.add("scalar", lambda e, k=k, b=b: e.activation(
                ysq[:, b], ybsl[:, k, :], AF.Square),
                waits=[pe.at()] + yw, incs=[sq.inc()])
            p.add("tensor", mm(pbS[0:1, :], c["cMEANVEC"][:], ybsl[:, k, :],
                               k == 0, k == 15),
                  waits=([fin.at()] if k == 0 else []) + yw)
            p.add("tensor", mm(pbS[32:33, :], c["cMEANVEC"][:], ysq[:, b],
                               k == 0, k == 15), waits=[sq.at()],
                  incs=[pe.inc()])
        p.add("scalar", lambda e: e.copy(yrow[0:1, 0:SLAB], pbS[0:1, :]),
              waits=[pe.at()])
        p.add("scalar", lambda e: e.activation(yrow[0:1, SLAB:],
                                               yrow[0:1, 0:SLAB], AF.Square),
              incs=[fin.inc()])
        p.add("vector", lambda e: e.tensor_sub(yrow[0:1, SLAB:], pbS[32:33, :],
                                               yrow[0:1, SLAB:]),
              waits=[fin.at()], incs=[fin.inc()])
        p.add("scalar", lambda e: e.activation(yrow[0:1, SLAB:], yrow[0:1, SLAB:],
                                               AF.Ln, bias=1e-5),
              waits=[fin.at()])
        p.add("scalar", lambda e: e.activation(yrow[0:1, SLAB:], yrow[0:1, SLAB:],
                                               AF.Exp, scale=-0.5),
              incs=[fin.inc()])
        p.add("vector", lambda e: e.tensor_mul(yrow[0:1, 0:SLAB],
                                               yrow[0:1, 0:SLAB],
                                               yrow[0:1, SLAB:]),
              waits=[fin.at()], incs=[fin.inc()])
        p.add("tensor", mm(pbB[:, :], c["cONESROW"][:], yrow[0:1, SLAB:],
                           True, True), waits=[fin.at()], incs=[pe.inc()])
        p.add("scalar", lambda e: e.copy(aysb[:, :], pbB[:, :]),
              waits=[pe.at(), xa.at()], incs=[fin.inc()])
        p.add("tensor", mm(pbB[:, :], c["cONESROW"][:], yrow[0:1, 0:SLAB],
                           True, True), waits=[fin.at()], incs=[pe.inc()])
        p.add("scalar", lambda e: e.copy(bysb[:, :], pbB[:, :]),
              waits=[pe.at()], incs=[fin.inc()])
        for k in range(NCHUNK):
            b = 0
            p.add("sync", dma(xf[:, b], xv[:, k, sl]),
                  waits=[xa.at()], incs=[xld.inc(16)])
            p.add("vector", lambda e, k=k, b=b: e.tensor_mul(
                xo[:, b], ybsl[:, k, :], aysb[:, :]),
                waits=[fin.at(), xld.at(), st.at()])
            p.add("vector", lambda e, b=b: e.tensor_sub(xo[:, b], xo[:, b],
                                                        bysb[:, :]))
            p.add("vector", lambda e, b=b: e.tensor_add(xo[:, b], xo[:, b],
                                                        xf[:, b]),
                  incs=[xa.inc()])
            p.add("gpsimd", lambda e, k=k, sl=sl, b=b: e.dma_start(
                ov[:, k, sl], xo[:, b]), waits=[xa.at()], incs=[st.inc(16)])
            if not last:
                p.add("vector", lambda e, k=k, sl=sl, b=b: e.tensor_copy(
                    c["xbf"][:, k, sl], xo[:, b]), incs=[xc.inc()])
                p.add("scalar", lambda e, k=k, sl=sl, b=b: e.activation(
                    xsq[:, b], c["xbf"][:, k, sl], AF.Square),
                    waits=[xc.at(), nxt.at()], incs=[sq.inc()])
                p.add("tensor", mm(pbN[0:1, :], c["cMEANVEC"][:],
                                   c["xbf"][:, k, sl], k == 0, k == 15),
                      waits=[xc.at()])
                p.add("tensor", mm(pbN[32:33, :], c["cMEANVEC"][:], xsq[:, b],
                                   k == 0, k == 15), waits=[sq.at()],
                      incs=[nxt.inc()])
        if not last:
            self.finalize_ln(p, pbN, c["r_row"][0:1, sl],
                             c["negmu_row"][0:1, sl], rowbuf, nxt, fin,
                             invr_dst=c["invr_row"][0:1, sl],
                             rb_dst=c["rb"][:, sl], pbB=self.pb[6])
            p.add("tensor", None, waits=[fin.at()])
        p.add("gpsimd", None, waits=[st.at()])
        pl.reset(M)

    def residual_phase(self, ysp, xdr, ydr, last=False):
        nc, pl, c = self.nc, self.plan, self.c
        p = Prog2()
        M = pl.mark()
        ld = self.sem("d_ld"); sq = self.sem("d_sq"); pe = self.sem("d_pe")
        fin = self.sem("d_fin"); ev = self.sem("d_ev"); st = self.sem("d_st")
        xld2 = [self.sem("d_xld0"), self.sem("d_xld1")]; nxt = self.sem("d_nxt")
        ybsl = pl.take("ybD", [128, NCHUNK, SLAB], BF16)
        ysq = pl.take("ysqD", [128, 2, SLAB], BF16)
        rowbuf = pl.take("rowD", [1, 2 * SLAB], F32)
        yrow = pl.take("yrowD", [1, 2 * SLAB], F32)
        aysb = pl.take("ayD", [128, SLAB], F32)
        bysb = pl.take("byD", [128, SLAB], F32)
        xf = pl.take("xfD", [128, 2, SLAB], F32)
        xo = pl.take("xoD", [128, 2, SLAB], F32)
        xsq = pl.take("xsqD", [128, 2, SLAB], BF16)
        yv = ysp.rearrange("(cc p) t -> p cc t", p=128)
        xv = xdr.rearrange("(cc p) t -> p cc t", p=128)
        ov = ydr.rearrange("(cc p) t -> p cc t", p=128)
        pbS, pbN, pbB = self.pb[2], self.pb[3], self.pb[6]
        xa = self.sem("d_xa")
        xc = self.sem("d_xc")
        for s in range(NSLAB):
            sl = slice(s * SLAB, (s + 1) * SLAB)
            p.add("sync", dma(ybsl[:], yv[:, :, sl]),
                  waits=[sq.at(), pe.at(), xa.at()], incs=[ld.inc(16)])
            for k in range(NCHUNK):
                b = k % 2
                p.add("scalar", lambda e, k=k, b=b: e.activation(
                    ysq[:, b], ybsl[:, k, :], AF.Square),
                    waits=[ld.at(), (pe.h, pe.v - 1)], incs=[sq.inc()])
                p.add("tensor", mm(pbS[0:1, :], c["cMEANVEC"][:], ybsl[:, k, :],
                                   k == 0, k == 15),
                      waits=[ld.at(), fin.at()] if k == 0 else [])
                p.add("tensor", mm(pbS[32:33, :], c["cMEANVEC"][:], ysq[:, b],
                                   k == 0, k == 15), waits=[sq.at()],
                      incs=[pe.inc()])
            p.add("scalar", lambda e: e.copy(yrow[0:1, 0:SLAB], pbS[0:1, :]),
                  waits=[pe.at()])
            p.add("scalar", lambda e: e.activation(yrow[0:1, SLAB:],
                                                   yrow[0:1, 0:SLAB], AF.Square),
                  incs=[fin.inc()])
            p.add("vector", lambda e: e.tensor_sub(yrow[0:1, SLAB:], pbS[32:33, :],
                                                   yrow[0:1, SLAB:]),
                  waits=[fin.at()], incs=[fin.inc()])
            p.add("scalar", lambda e: e.activation(yrow[0:1, SLAB:], yrow[0:1, SLAB:],
                                                   AF.Ln, bias=1e-5),
                  waits=[fin.at()])
            p.add("scalar", lambda e: e.activation(yrow[0:1, SLAB:], yrow[0:1, SLAB:],
                                                   AF.Exp, scale=-0.5),
                  incs=[fin.inc()])
            p.add("vector", lambda e: e.tensor_mul(yrow[0:1, 0:SLAB],
                                                   yrow[0:1, 0:SLAB],
                                                   yrow[0:1, SLAB:]),
                  waits=[fin.at()], incs=[fin.inc()])
            p.add("tensor", mm(pbB[:, :], c["cONESROW"][:], yrow[0:1, SLAB:],
                               True, True), waits=[fin.at()], incs=[pe.inc()])
            p.add("scalar", lambda e: e.copy(aysb[:, :], pbB[:, :]),
                  waits=[pe.at(), xa.at()], incs=[fin.inc()])
            p.add("tensor", mm(pbB[:, :], c["cONESROW"][:], yrow[0:1, 0:SLAB],
                               True, True), waits=[fin.at()], incs=[pe.inc()])
            p.add("scalar", lambda e: e.copy(bysb[:, :], pbB[:, :]),
                  waits=[pe.at()], incs=[fin.inc()])
            for k in range(NCHUNK):
                b = k % 2
                p.add("sync", dma(xf[:, b], xv[:, k, sl]),
                      waits=[(xa.h, xa.v - 1)], incs=[xld2[b].inc(16)])
                p.add("vector", lambda e, k=k, b=b: e.tensor_mul(
                    xo[:, b], ybsl[:, k, :], aysb[:, :]),
                    waits=[fin.at(), xld2[b].at(), (st.h, 16 * (st.v // 16 - 1))])
                p.add("vector", lambda e, b=b: e.tensor_sub(xo[:, b], xo[:, b],
                                                            bysb[:, :]))
                p.add("vector", lambda e, b=b: e.tensor_add(xo[:, b], xo[:, b],
                                                            xf[:, b]),
                      incs=[xa.inc()])
                p.add("gpsimd", lambda e, k=k, sl=sl, b=b: e.dma_start(
                    ov[:, k, sl], xo[:, b]), waits=[xa.at()], incs=[st.inc(16)])
                if not last:
                    p.add("vector", lambda e, k=k, sl=sl, b=b: e.tensor_copy(
                        c["xbf"][:, k, sl], xo[:, b]), incs=[xc.inc()])
                    p.add("scalar", lambda e, k=k, sl=sl, b=b: e.activation(
                        xsq[:, b], c["xbf"][:, k, sl], AF.Square),
                        waits=[xc.at(), (nxt.h, nxt.v - 1)], incs=[sq.inc()])
                    p.add("tensor", mm(pbN[0:1, :], c["cMEANVEC"][:],
                                       c["xbf"][:, k, sl], k == 0, k == 15),
                          waits=[xc.at()])
                    p.add("tensor", mm(pbN[32:33, :], c["cMEANVEC"][:], xsq[:, b],
                                       k == 0, k == 15), waits=[sq.at()],
                          incs=[nxt.inc()])
            if not last:
                sl2 = slice(s * SLAB, (s + 1) * SLAB)
                self.finalize_ln(p, pbN, c["r_row"][0:1, sl2],
                                 c["negmu_row"][0:1, sl2], rowbuf, nxt, fin,
                                 invr_dst=c["invr_row"][0:1, sl2],
                                 rb_dst=c["rb"][:, sl2], pbB=self.pb[7])
                p.add("tensor", None, waits=[fin.at()])
        p.add("gpsimd", None, waits=[st.at()])
        p.emit(nc, "resid")
        pl.reset(M)


N_CORES = 8

def _build_launch(kind):
    nc = bass.Bass("TRN2", target_bir_lowering=False)
    kb = KB(nc)
    want = {"l0a": (), "l0f": ("l0f",), "l1a": ("l1a",), "l1f": ("l1f",)}[kind]
    kb.stage_prep(want=want)
    kb.alloc_globals()
    x0 = kb.dram_in("x0", [D, T]); y0 = kb.dram_out("y0", [D, T])
    x1 = kb.dram_in("x1", [D, T]); y1 = kb.dram_out("y1", [D, T])
    if kind == "l1f":
        kb.stage_ghost()
    for seq, (xdr, ydr) in enumerate(((x0, y0), (x1, y1))):
        p = Prog2()
        kb.stats_pass(p, xdr, xbf_full=kb.c["xbf"], pfx="st")
        p.emit(nc, f"stats{seq}")
        if kind == "l0a":
            kb.stage_l0a(xdr, ydr, last=True)
        elif kind == "l0f":
            kb.stage_l0f(xdr, ydr, last=True)
        elif kind == "l1a":
            kb.stage_l1a(xdr, ydr, last=True)
        else:
            kb.stage_l1f(xdr, ydr, last=True)
    nc.finalize()
    return nc

_LAUNCH_INPUTS = {
    "l0a": ("Wq", "Wk", "Wv", "Wo"),
    "l0f": ("WaQK", "WaV", "Wp", "wstd", "wrec"),
    "l1a": ("CwT", "DwT", "Cb", "Db", "LD", "Afm", "Bfm"),
    "l1f": ("gidxT", "uidxT", "didxT", "lut"),
}
_BASE_CONSTS = ("cCOS", "cSIN", "cR128T", "cID128", "cMASKT", "cONES128",
                "cMEANVEC", "cONESROW", "cONESROWB")
_LAUNCH_CONSTS = {
    "l0a": _BASE_CONSTS,
    "l0f": _BASE_CONSTS + ("cSTDMASK", "cTAILMASK"),
    "l1a": _BASE_CONSTS,
    "l1f": _BASE_CONSTS,
}

def _build_fused():
    nc = bass.Bass("TRN2", target_bir_lowering=False)
    kb = KB(nc)
    kb.stage_prep(want=("l0f", "l1a", "l1f"))
    kb.alloc_globals()
    x0 = kb.dram_in("x0", [D, T]); y0 = kb.dram_out("y0", [D, T])
    x1 = kb.dram_in("x1", [D, T]); y1 = kb.dram_out("y1", [D, T])
    kb.stage_ghost()
    for seq, (xin, yout) in enumerate(((x0, y0), (x1, y1))):
        p = Prog2()
        kb.stats_pass(p, xin, xbf_full=kb.c["xbf"], pfx="st")
        p.emit(nc, f"stats{seq}")
        xa = kb.dram_tmp(f"xa{seq}", [D, T])
        xb = kb.dram_tmp(f"xb{seq}", [D, T])
        xc = kb.dram_tmp(f"xc{seq}", [D, T])
        kb.stage_l0a(xin, xa, last=False)
        kb.stage_l0f(xa, xb, last=False)
        kb.stage_l1a(xb, xc, last=False)
        kb.stage_l1f(xc, yout, last=True)
    nc.finalize()
    return nc

_FUSED = True

def kernel(**inputs):
    w = layout_weights(inputs)
    x = np.asarray(inputs["L0_attn_x"])
    xfm = np.ascontiguousarray(x.transpose(0, 2, 1))
    cur0 = [np.ascontiguousarray(xfm[2 * cid]) for cid in range(N_CORES)]
    cur1 = [np.ascontiguousarray(xfm[2 * cid + 1]) for cid in range(N_CORES)]
    if _FUSED:
        nc = _build_fused()
        in_maps = []
        for cid in range(N_CORES):
            im = {"x0": cur0[cid], "x1": cur1[cid]}
            for kind in ("l0a", "l0f", "l1a", "l1f"):
                for nm in _LAUNCH_INPUTS[kind]:
                    im[nm] = w[nm]
            for nm in CONSTS:
                im[nm] = CONSTS[nm]
            in_maps.append(im)
        res = run_bass_kernel_spmd(nc, in_maps, core_ids=list(range(N_CORES)))
        out = np.empty((16, T, D), dtype=np.float32)
        for cid in range(N_CORES):
            out[2 * cid] = res.results[cid]["y0"].T
            out[2 * cid + 1] = res.results[cid]["y1"].T
        return out
    for kind in ("l0a", "l0f", "l1a", "l1f"):
        nc = _build_launch(kind)
        in_maps = []
        for cid in range(N_CORES):
            im = {"x0": cur0[cid], "x1": cur1[cid]}
            for nm in _LAUNCH_INPUTS[kind]:
                im[nm] = w[nm]
            for nm in _LAUNCH_CONSTS[kind]:
                im[nm] = CONSTS[nm]
            in_maps.append(im)
        res = run_bass_kernel_spmd(nc, in_maps, core_ids=list(range(N_CORES)))
        cur0 = [res.results[cid]["y0"] for cid in range(N_CORES)]
        cur1 = [res.results[cid]["y1"] for cid in range(N_CORES)]
    out = np.empty((16, T, D), dtype=np.float32)
    for cid in range(N_CORES):
        out[2 * cid] = cur0[cid].T
        out[2 * cid + 1] = cur1[cid].T
    return out
